# version 05
# speedup vs baseline: 1.4948x; 1.0901x over previous
import numpy as np
import ml_dtypes
from contextlib import ExitStack
from types import SimpleNamespace
import concourse.bass as bass
import concourse.mybir as mybir
from concourse.bass_utils import run_bass_kernel_spmd

F32 = mybir.dt.float32
BF16 = mybir.dt.bfloat16
I8 = mybir.dt.int8
AF = mybir.ActivationFunctionType
ALU = mybir.AluOpType
BF16NP = ml_dtypes.bfloat16

NCORES = 8
D = 2048
TOK = 16384
TPC = TOK // NCORES
NFC = D // 128
CLAMP = 21.0


class Ctr:
    def __init__(self, handle):
        self.h = handle
        self.n = 0

    def inc(self, instr, k=1):
        self.n += k
        instr.then_inc(self.h, k)
        return instr

    def wait(self, eng, n=None):
        eng.wait_ge(self.h, self.n if n is None else n)



def build_ln(with_out_f32=False):
    nc = bass.Bass("TRN2", target_bir_lowering=False, debug=False)
    x = nc.dram_tensor("x", [D, TPC], F32, kind="ExternalInput").ap()
    xn = nc.dram_tensor("xn", [D, TPC], BF16, kind="ExternalOutput").ap()
    TT = 512
    NT = TPC // TT
    with ExitStack() as st:
        def sbuf(n, sh, dt=F32):
            return st.enter_context(nc.sbuf_tensor(n, sh, dt))
        xin = [sbuf(f"xin{p}", [128, NFC * TT]) for p in range(2)]
        sq = sbuf("sq", [128, NFC * TT])
        ones1 = sbuf("ones1", [128, 1])
        ones128 = sbuf("ones128", [1, 128])
        epsc = sbuf("epsc", [1, 1])
        stat = sbuf("stat", [1, 6 * TT])
        xnb = sbuf("xnb", [128, NFC * TT], BF16)
        tmp = sbuf("tmpx", [128, TT])
        ps = st.enter_context(nc.psum_tensor("ps", [1, TT], F32))
        pq = st.enter_context(nc.psum_tensor("pq", [1, TT], F32))
        pmu = st.enter_context(nc.psum_tensor("pmu", [128, TT], F32))
        prs = st.enter_context(nc.psum_tensor("prs", [128, TT], F32))
        ld = Ctr(st.enter_context(nc.semaphore("ld")))
        sto = Ctr(st.enter_context(nc.semaphore("sto")))
        sA = Ctr(st.enter_context(nc.semaphore("sA")))
        sB = Ctr(st.enter_context(nc.semaphore("sB")))
        sC = Ctr(st.enter_context(nc.semaphore("sC")))
        sD = Ctr(st.enter_context(nc.semaphore("sD")))
        block = st.enter_context(nc.Block())

        @block.vector
        def _(v):
            v.memset(ones1[:, :], 1.0)
            v.memset(ones128[:, :], 1.0)
            sA.inc(v.memset(epsc[:, :], 1e-5))

        @block.sync
        def _(sy):
            for ti in range(NT):
                if ti >= 2:
                    sy.wait_ge(sD.h, 2 * (ti - 1))
                for j in range(NFC):
                    ins = sy.dma_start(out=xin[ti % 2][:, j * TT:(j + 1) * TT],
                                       in_=x[j * 128:(j + 1) * 128, ti * TT:(ti + 1) * TT])
                    ld.inc(ins, 16)

        for ti in range(NT):
            buf = xin[ti % 2]

            @block.scalar
            def _(s, ti=ti, buf=buf):
                s.wait_ge(ld.h, (ti + 1) * NFC * 16)
                if ti >= 1:
                    s.wait_ge(sB.h, ti)
                for j in range(NFC):
                    ins = s.activation(out=sq[:, j * TT:(j + 1) * TT],
                                       in_=buf[:, j * TT:(j + 1) * TT], func=AF.Square)
                    if j == NFC - 1:
                        sA.inc(ins)

            @block.tensor
            def _(t, ti=ti, buf=buf):
                t.wait_ge(sA.h, 1 + ti + 1)
                if ti >= 1:
                    t.wait_ge(sC.h, 3 * (ti - 1) + 1)
                for j in range(NFC):
                    t.matmul(out=ps[:, :], lhsT=ones1[:, :], rhs=buf[:, j * TT:(j + 1) * TT],
                             start=(j == 0), stop=(j == NFC - 1))
                for j in range(NFC):
                    ins = t.matmul(out=pq[:, :], lhsT=ones1[:, :], rhs=sq[:, j * TT:(j + 1) * TT],
                                   start=(j == 0), stop=(j == NFC - 1))
                sB.inc(ins)

            @block.vector
            def _(v, ti=ti):
                v.wait_ge(sB.h, ti + 1)
                if ti >= 1:
                    v.wait_ge(sD.h, 2 * (ti - 1) + 1)
                v.tensor_scalar_mul(out=stat[:, 0 * TT:1 * TT], in0=ps[:, :], scalar1=1.0 / D)
                v.tensor_scalar_mul(out=stat[:, 1 * TT:2 * TT], in0=pq[:, :], scalar1=1.0 / D)
                v.tensor_mul(out=stat[:, 2 * TT:3 * TT], in0=stat[:, 0 * TT:1 * TT],
                             in1=stat[:, 0 * TT:1 * TT])
                sC.inc(v.tensor_sub(out=stat[:, 3 * TT:4 * TT], in0=stat[:, 1 * TT:2 * TT],
                                    in1=stat[:, 2 * TT:3 * TT]))

            @block.scalar
            def _(s, ti=ti):
                s.wait_ge(sC.h, 3 * ti + 1)
                sC.inc(s.activation(out=stat[:, 4 * TT:5 * TT], in_=stat[:, 3 * TT:4 * TT],
                                    func=AF.Sqrt, bias=epsc[0:1, 0:1]))

            @block.vector
            def _(v, ti=ti):
                v.wait_ge(sC.h, 3 * ti + 2)
                sC.inc(v.reciprocal(out=stat[:, 5 * TT:6 * TT], in_=stat[:, 4 * TT:5 * TT]))

            @block.tensor
            def _(t, ti=ti):
                t.wait_ge(sC.h, 3 * ti + 3)
                if ti >= 1:
                    t.wait_ge(sD.h, 2 * ti)
                t.matmul(out=pmu[:, :], lhsT=ones128[:, :], rhs=stat[:, 0 * TT:1 * TT],
                         start=True, stop=True)
                ins = t.matmul(out=prs[:, :], lhsT=ones128[:, :], rhs=stat[:, 5 * TT:6 * TT],
                               start=True, stop=True)
                sD.inc(ins)

            @block.vector
            def _(v, ti=ti, buf=buf):
                v.wait_ge(sD.h, 2 * ti + 1)
                if ti >= 1:
                    v.wait_ge(sto.h, NFC * 16 * ti)
                for j in range(NFC):
                    v.tensor_sub(out=tmp[:, :], in0=buf[:, j * TT:(j + 1) * TT], in1=pmu[:, :])
                    ins = v.tensor_mul(out=xnb[:, j * TT:(j + 1) * TT], in0=tmp[:, :], in1=prs[:, :])
                    if j == NFC - 1:
                        sD.inc(ins)

            @block.gpsimd
            def _(g, ti=ti):
                g.wait_ge(sD.h, 2 * ti + 2)
                for j in range(NFC):
                    ins = g.dma_start(out=xn[j * 128:(j + 1) * 128, ti * TT:(ti + 1) * TT],
                                      in_=xnb[:, j * TT:(j + 1) * TT])
                    sto.inc(ins, 16)

        @block.gpsimd
        def _(g):
            sto.wait(g)
    return nc


def run_ln(x_fm_slices):
    nc = build_ln()
    in_maps = [{"x": s} for s in x_fm_slices]
    res = run_bass_kernel_spmd(nc, in_maps, core_ids=list(range(NCORES)))
    return [r["xn"] for r in res.results]



def l0attn_consts():
    GT = 512
    col = np.arange(GT)
    c = {}
    for m in (8, 16, 32, 64, 128):
        c[f"rm{m}"] = np.broadcast_to((col % m != 0).astype(np.float32), (128, GT)).copy()
    c["evn8"] = np.broadcast_to(((col // 8) % 2 == 0).astype(BF16NP), (128, GT)).copy()
    c["odd8"] = np.broadcast_to(((col // 8) % 2 == 1).astype(BF16NP), (128, GT)).copy()
    s = np.arange(128)[:, None]; t = np.arange(128)[None, :]
    c["maskAb"] = np.where(s <= t, -1, 0).astype(np.int32)
    c["maskMb"] = np.where(s < t, -1, 0).astype(np.int32)
    c["identb"] = np.eye(128, dtype=BF16NP)
    c["identf"] = np.eye(128, dtype=np.float32)
    return c


def build_l0attn(ngroups=None, stop_after=None):
    nc = bass.Bass("TRN2", target_bir_lowering=False, debug=False)
    GT = 512
    NG = TOK // GT
    if ngroups is not None:
        NG = ngroups
    NCH = GT // 128
    xn = nc.dram_tensor("xn", [D, TOK], BF16, kind="ExternalInput").ap()
    wq = nc.dram_tensor("wq", [D, 256], F32, kind="ExternalInput").ap()
    wk = nc.dram_tensor("wk", [D, 256], F32, kind="ExternalInput").ap()
    wv = nc.dram_tensor("wv", [D, 256], F32, kind="ExternalInput").ap()
    cw = nc.dram_tensor("cw", [128, 24], F32, kind="ExternalInput").ap()
    fw1 = nc.dram_tensor("fw1", [D, 128], F32, kind="ExternalInput").ap()
    fw2 = nc.dram_tensor("fw2", [128, 256], F32, kind="ExternalInput").ap()
    gw1 = nc.dram_tensor("gw1", [D, 128], F32, kind="ExternalInput").ap()
    gw2 = nc.dram_tensor("gw2", [128, 256], F32, kind="ExternalInput").ap()
    gb2 = nc.dram_tensor("gb2", [1, 256], F32, kind="ExternalInput").ap()
    bw = nc.dram_tensor("bw", [D, 2], F32, kind="ExternalInput").ap()
    alog = nc.dram_tensor("alog", [1, 2], F32, kind="ExternalInput").ap()
    dtb = nc.dram_tensor("dtb", [128, 2], F32, kind="ExternalInput").ap()
    CST = l0attn_consts()
    I32 = mybir.dt.int32
    cin = {}
    for k, v in CST.items():
        dt = BF16 if v.dtype == BF16NP else (I32 if v.dtype == np.int32 else F32)
        cin[k] = nc.dram_tensor(k, list(v.shape), dt, kind="ExternalInput").ap()
    ofeat = nc.dram_tensor("ofeat", [TOK, 256], BF16, kind="ExternalOutput").ap()

    with ExitStack() as st:
        def sbuf(n, sh, dt=F32):
            return st.enter_context(nc.sbuf_tensor(n, sh, dt))
        wqs = sbuf("wqs", [128, NFC * 256], BF16)
        wks = sbuf("wks", [128, NFC * 256], BF16)
        wvs = sbuf("wvs", [128, NFC * 256], BF16)
        fw1s = sbuf("fw1s", [128, NFC * 128], BF16)
        gw1s = sbuf("gw1s", [128, NFC * 128], BF16)
        fw2s = sbuf("fw2s", [128, 256], BF16)
        gw2s = sbuf("gw2s", [128, 256], BF16)
        gb2s = sbuf("gb2s", [1, 256], F32)
        bws = sbuf("bws", [128, NFC * 2], BF16)
        cws = sbuf("cws", [128, 24], F32)
        alogs = sbuf("alogs", [1, 2], F32)
        negA = sbuf("negA", [128, 2], F32)
        dts = sbuf("dts", [128, 2], F32)
        onesc = sbuf("onesc", [1, 128], F32)
        ones1b = sbuf("ones1b", [1, 128], BF16)
        onecol = sbuf("onecol", [128, 1], F32)
        onebias = sbuf("onebias", [128, 1], F32)
        epsl2 = sbuf("epsl2", [1, 1], F32)
        cs = {}
        for k, v in CST.items():
            dt = BF16 if v.dtype == BF16NP else (mybir.dt.int32 if v.dtype == np.int32 else F32)
            cs[k] = sbuf("c_" + k, list(v.shape), dt)
        mskd = sbuf("mskd", [128, 128], mybir.dt.int32)
        xng = [sbuf(f"xng{p}", [128, NFC * GT], BF16) for p in range(2)]
        raw = {nm: sbuf("raw" + nm, [128, GT + 3], BF16)
               for nm in ("q0", "q1", "k0", "k1", "v0", "v1")}
        qh = [sbuf(f"qh{h}", [128, GT], BF16) for h in range(2)]
        kh = [sbuf(f"kh{h}", [128, GT], BF16) for h in range(2)]
        vh = [sbuf(f"vh{h}", [128, GT], BF16) for h in range(2)]
        innf = sbuf("innf", [128, GT], BF16)
        inng = sbuf("inng", [128, GT], BF16)
        betac = sbuf("betac", [2, GT], F32)
        nsc = sbuf("nsc", [1, GT], F32)
        nsc2 = sbuf("nsc2", [1, GT], F32)
        beta_f = sbuf("beta_f", [128, NCH * 2], F32)
        beta_fn = sbuf("beta_fn", [128, NCH * 2], F32)
        beta_b = sbuf("beta_b", [128, NCH * 2], BF16)
        gate_tm = sbuf("gate_tm", [128, NCH * 256], BF16)
        Gf = sbuf("Gf", [128, GT], F32)
        RDf = sbuf("RDf", [128, GT], F32)
        SDf = sbuf("SDf", [128, GT], F32)
        ERf = sbuf("ERf", [128, GT], F32)
        gkf = sbuf("gkf", [128, GT], F32)
        sp1 = sbuf("sp1", [128, GT], F32)
        spb = sbuf("spb", [128, GT], BF16)
        mapn = ["EQ", "ED", "rowb", "colb", "r8", "c8", "r16", "c16", "r32", "c32", "r64", "c64"]
        maps = {k: sbuf("m_" + k, [128, GT], BF16) for k in mapn}
        C1 = sbuf("C1", [128, GT], BF16); C2 = sbuf("C2", [128, GT], BF16)
        C3 = sbuf("C3", [128, GT], BF16); C4 = sbuf("C4", [128, GT], BF16)
        C5 = sbuf("C5", [128, GT], BF16); C6 = sbuf("C6", [128, GT], BF16)
        RBe = sbuf("RBe", [128, 2 * GT], BF16); RBo = sbuf("RBo", [128, 2 * GT], BF16)
        R8 = sbuf("R8", [128, 2 * GT], BF16); R16 = sbuf("R16", [128, 2 * GT], BF16)
        R32 = sbuf("R32", [128, 2 * GT], BF16); R64 = sbuf("R64", [128, 2 * GT], BF16)
        Qs = sbuf("Qs", [128, GT], BF16); Ws = sbuf("Ws", [128, GT], BF16)
        KD = sbuf("KD", [128, GT], BF16)
        eGend = sbuf("eGend", [128, NCH], F32)
        Aqt = sbuf("Aqt", [128, 128], BF16)
        Ntn = sbuf("Ntn", [128, 128], BF16)
        powT = [sbuf(f"powT{i}", [128, 128], BF16) for i in range(6)]
        pown = [sbuf(f"pown{i}", [128, 128], BF16) for i in range(6)]
        ybuf = [sbuf(f"ybuf{i}", [128, 256], BF16) for i in range(2)]
        ydel = sbuf("ydel", [128, 256], BF16)
        rhsb = sbuf("rhsb", [128, 256], BF16)
        S_f = [sbuf(f"S_f{h}", [128, 128], F32) for h in range(2)]
        S_b = [sbuf(f"S_b{h}", [128, 128], BF16) for h in range(2)]
        UWc = sbuf("UWc", [128, 128], BF16)
        KDt = sbuf("KDt", [128, 128], BF16)
        useq = sbuf("useq", [128, 128], BF16)
        ssq = sbuf("ssq", [128, 4], F32)
        obuf = sbuf("obuf", [128, NCH * 256], BF16)
        osq = sbuf("osq", [128, 128], F32)
        pA = st.enter_context(nc.psum_tensor("pA", [128, GT], F32))
        pB = st.enter_context(nc.psum_tensor("pB", [128, GT], F32))
        pC = st.enter_context(nc.psum_tensor("pC", [2, GT], F32))
        pScore = st.enter_context(nc.psum_tensor("pScore", [128, 256], F32))
        pStage = st.enter_context(nc.psum_tensor("pStage", [16, 32], F32))
        pT = st.enter_context(nc.psum_tensor("pT", [128, 128], BF16))
        pSq = st.enter_context(nc.psum_tensor("pSq", [128, 128], F32))
        pY = st.enter_context(nc.psum_tensor("pY", [128, 256], F32))
        ph = Ctr(st.enter_context(nc.semaphore("ph")))
        sto = Ctr(st.enter_context(nc.semaphore("sto")))
        block = st.enter_context(nc.Block())

        def T(fn):
            @block.tensor
            def _(t):
                ph.wait(t)
                fn(t)
        def V(fn):
            @block.vector
            def _(v):
                ph.wait(v)
                fn(v)
        def SC(fn):
            @block.scalar
            def _(s):
                ph.wait(s)
                fn(s)
        def G3(fn):
            @block.gpsimd
            def _(g):
                ph.wait(g)
                fn(g)

        @block.gpsimd
        def _(g):
            for w_dram, w_sb, nch in ((wq, wqs, 256), (wk, wks, 256), (wv, wvs, 256),
                                      (fw1, fw1s, 128), (gw1, gw1s, 128), (bw, bws, 2)):
                src = w_dram.rearrange("(j p) n -> p j n", p=128)
                dst = w_sb[:, :].rearrange("p (j n) -> p j n", n=nch)
                ph.inc(g.dma_start(out=dst, in_=src), 16)
            for w_dram, w_sb in ((fw2, fw2s), (gw2, gw2s), (cw, cws)):
                ph.inc(g.dma_start(out=w_sb[:, :], in_=w_dram), 16)

        @block.sync
        def _(sy):
            ph.inc(sy.dma_start(out=gb2s[:, :], in_=gb2), 16)
            ph.inc(sy.dma_start(out=alogs[:, :], in_=alog), 16)
            ph.inc(sy.dma_start(out=dts[:, :], in_=dtb), 16)
            for k in CST:
                ph.inc(sy.dma_start(out=cs[k][:, :], in_=cin[k]), 16)

        def setup1(v):
            v.memset(onesc[:, :], 1.0)
            v.memset(ones1b[:, :], 1.0)
            v.memset(onecol[:, :], 1.0)
            v.memset(onebias[:, :], 1.0)
            v.memset(epsl2[:, :], 1e-6)
            for h in range(2):
                v.memset(S_f[h][:, :], 0.0)
                v.memset(S_b[h][:, :], 0.0)
            ph.inc(v.engine_nop())
        V(setup1)
        SC(lambda s: ph.inc(s.activation(out=alogs[:, :], in_=alogs[:, :], func=AF.Exp)))
        V(lambda v: ph.inc(v.tensor_scalar_mul(out=alogs[:, :], in0=alogs[:, :], scalar1=-1.0)))
        T(lambda t: ph.inc(t.matmul(out=pSq[:, 0:2], lhsT=onesc[:, :], rhs=alogs[:, :],
                                    start=True, stop=True)))
        V(lambda v: ph.inc(v.tensor_copy(out=negA[:, :], in_=pSq[:, 0:2])))

        for g in range(NG):
            bstart = (g % max(NG // 4, 8) == 0)
            xbuf = xng[g % 2]

            def loadxn(sy, g=g, xbuf=xbuf):
                for j in range(NFC):
                    ins = sy.dma_start(out=xbuf[:, j * GT:(j + 1) * GT],
                                       in_=xn[j * 128:(j + 1) * 128, g * GT:(g + 1) * GT])
                    ph.inc(ins, 16)
            @block.sync
            def _(sy):
                ph.wait(sy)
                loadxn(sy)

            for pi, wsb in enumerate((wqs, wks, wvs)):
                for ht in range(2):
                    nm = ("q0q1", "k0k1", "v0v1")[pi][ht * 2:ht * 2 + 2]
                    rb = raw[nm]
                    def mmproj(t, wsb=wsb, ht=ht, xbuf=xbuf):
                        for j in range(NFC):
                            ins = t.matmul(out=pA[:, :],
                                           lhsT=wsb[:, j * 256 + ht * 128:j * 256 + ht * 128 + 128],
                                           rhs=xbuf[:, j * GT:(j + 1) * GT],
                                           start=(j == 0), stop=(j == NFC - 1))
                        ph.inc(ins)
                    T(mmproj)
                    SC(lambda s, rb=rb: ph.inc(s.copy(out=rb[:, 3:3 + GT], in_=pA[:, :])))
                    def convf(v, rb=rb, pi=pi, ht=ht, bstart=bstart):
                        if bstart:
                            v.memset(rb[:, 0:3], 0.0)
                        ci = (pi * 2 + ht) * 4
                        v.tensor_scalar(out=spb[:, :], in0=rb[:, 0:GT],
                                        scalar1=cws[:, ci:ci + 1], scalar2=None, op0=ALU.mult)
                        for i in range(1, 4):
                            v.scalar_tensor_tensor(out=spb[:, :], in0=rb[:, i:i + GT],
                                                   scalar=cws[:, ci + i:ci + i + 1],
                                                   in1=spb[:, :], op0=ALU.mult, op1=ALU.add)
                        ph.inc(v.tensor_copy(out=rb[:, 0:3], in_=rb[:, GT:GT + 3]))
                    V(convf)
                    dst = (qh, kh, vh)[pi][ht]
                    SC(lambda s, dst=dst: ph.inc(s.activation(out=dst[:, :], in_=spb[:, :],
                                                              func=AF.Silu)))

            for pi in range(2):
                for ht in range(2):
                    src = (qh, kh)[pi][ht]
                    SC(lambda s, src=src: ph.inc(s.activation(out=sp1[:, :], in_=src[:, :],
                                                              func=AF.Square)))
                    T(lambda t: ph.inc(t.matmul(out=pC[0:1, :], lhsT=onecol[:, 0:1],
                                                rhs=sp1[:, :], start=True, stop=True)))
                    V(lambda v: ph.inc(v.tensor_scalar_add(out=nsc[0:1, :], in0=pC[0:1, :],
                                                           scalar1=epsl2[0:1, 0:1])))
                    SC(lambda s: ph.inc(s.activation(out=nsc2[0:1, :], in_=nsc[0:1, :],
                                                     func=AF.Sqrt)))
                    V(lambda v: ph.inc(v.reciprocal(out=nsc[0:1, :], in_=nsc2[0:1, :])))
                    T(lambda t: ph.inc(t.matmul(out=pB[:, :], lhsT=onesc[:, :],
                                                rhs=nsc[0:1, :], start=True, stop=True)))
                    scl = (128.0 ** -0.5) if pi == 0 else 1.0
                    SC(lambda s, scl=scl: ph.inc(s.mul(out=spb[:, :], in_=pB[:, :], mul=scl)))
                    V(lambda v, src=src: ph.inc(v.tensor_mul(out=src[:, :], in0=src[:, :],
                                                             in1=spb[:, :])))

            for wsb, dst in ((fw1s, innf), (gw1s, inng)):
                def mminner(t, wsb=wsb, xbuf=xbuf):
                    for j in range(NFC):
                        ins = t.matmul(out=pA[:, :], lhsT=wsb[:, j * 128:(j + 1) * 128],
                                       rhs=xbuf[:, j * GT:(j + 1) * GT],
                                       start=(j == 0), stop=(j == NFC - 1))
                    ph.inc(ins)
                T(mminner)
                SC(lambda s, dst=dst: ph.inc(s.copy(out=dst[:, :], in_=pA[:, :])))

            def mmbeta(t, xbuf=xbuf):
                for j in range(NFC):
                    ins = t.matmul(out=pC[:, :], lhsT=bws[:, j * 2:(j + 1) * 2],
                                   rhs=xbuf[:, j * GT:(j + 1) * GT],
                                   start=(j == 0), stop=(j == NFC - 1))
                ph.inc(ins)
            T(mmbeta)
            SC(lambda s: ph.inc(s.activation(out=betac[:, :], in_=pC[:, :], func=AF.Sigmoid)))
            for ch in range(NCH):
                T(lambda t, ch=ch: ph.inc(t.transpose(out=pSq[:, 0:2],
                                                      in_=betac[:, ch * 128:(ch + 1) * 128],
                                                      identity=cs["identf"][0:2, 0:2])))
                def betaev(v, ch=ch):
                    v.tensor_copy(out=beta_f[:, ch * 2:(ch + 1) * 2], in_=pSq[:, 0:2])
                    v.tensor_scalar_mul(out=beta_fn[:, ch * 2:(ch + 1) * 2], in0=pSq[:, 0:2],
                                        scalar1=-1.0)
                    ph.inc(v.tensor_copy(out=beta_b[:, ch * 2:(ch + 1) * 2], in_=pSq[:, 0:2]))
                V(betaev)

            for ch in range(NCH):
                def mmgate(t, ch=ch):
                    t.matmul(out=pB[:, 0:256], lhsT=inng[:, ch * 128:(ch + 1) * 128],
                             rhs=gw2s[:, :], start=True, stop=False)
                    ph.inc(t.matmul(out=pB[:, 0:256], lhsT=onesc[:, :],
                                    rhs=gb2s[:, :], start=False, stop=True,
                                    skip_group_check=True))
                T(mmgate)
                SC(lambda s, ch=ch: ph.inc(s.activation(out=gate_tm[:, ch * 256:(ch + 1) * 256],
                                                        in_=pB[:, 0:256], func=AF.Sigmoid)))

            for ht in range(2):
                T(lambda t, ht=ht: ph.inc(t.matmul(out=pA[:, :],
                                                   lhsT=fw2s[:, ht * 128:(ht + 1) * 128],
                                                   rhs=innf[:, :], start=True, stop=True)))
                V(lambda v, ht=ht: ph.inc(v.tensor_scalar_add(out=sp1[:, :], in0=pA[:, :],
                                                              scalar1=dts[:, ht:ht + 1])))
                SC(lambda s: ph.inc(s.activation(out=gkf[:, :], in_=sp1[:, :], func=AF.Exp)))
                SC(lambda s: ph.inc(s.activation(out=sp1[:, :], in_=gkf[:, :], func=AF.Ln,
                                                 bias=onebias[:, 0:1])))
                V(lambda v, ht=ht: ph.inc(v.tensor_scalar(out=gkf[:, :], in0=sp1[:, :],
                                                          scalar1=negA[:, ht:ht + 1],
                                                          scalar2=-CLAMP,
                                                          op0=ALU.mult, op1=ALU.max)))

                def scan_map(rm, dst):
                    V(lambda v, rm=rm, dst=dst: ph.inc(
                        v.tensor_tensor_scan(out=dst[:, :], data0=cs[rm][:, :],
                                             data1=gkf[:, :], initial=0.0,
                                             op0=ALU.mult, op1=ALU.add)))
                def expmap(src, dst, scale=1.0):
                    SC(lambda s, src=src, dst=dst, scale=scale: ph.inc(
                        s.activation(out=maps[dst][:, :], in_=src[:, :], func=AF.Exp,
                                     scale=scale)))
                def suffix_diff(src, dst, last, step):
                    def f(v, src=src, dst=dst, last=last, step=step):
                        nb = GT // step
                        gath = src[:, last::step].unsqueeze(2).broadcast_to([128, nb, step])
                        s3 = src[:, :].rearrange("p (a b) -> p a b", b=step)
                        d3 = dst[:, :].rearrange("p (a b) -> p a b", b=step)
                        ph.inc(v.tensor_sub(out=d3, in0=gath, in1=s3))
                    V(f)

                scan_map("rm128", Gf)
                expmap(Gf, "EQ")
                suffix_diff(Gf, SDf, 127, 128)
                expmap(SDf, "ED")
                SC(lambda s: ph.inc(s.activation(out=eGend[:, 0:NCH], in_=Gf[:, 127::128],
                                                 func=AF.Exp)))
                scan_map("rm8", RDf)
                expmap(RDf, "r8")
                suffix_diff(RDf, SDf, 7, 8)
                expmap(SDf, "c8")
                def erow(v):
                    gath = RDf[:, 3::8].unsqueeze(2).broadcast_to([128, GT // 8, 8])
                    s3 = RDf[:, :].rearrange("p (a b) -> p a b", b=8)
                    d3 = ERf[:, :].rearrange("p (a b) -> p a b", b=8)
                    ph.inc(v.tensor_sub(out=d3, in0=s3, in1=gath))
                V(erow)
                expmap(ERf, "rowb")
                expmap(ERf, "colb", scale=-1.0)
                for m, rn, cn in ((16, "r16", "c16"), (32, "r32", "c32"), (64, "r64", "c64")):
                    scan_map(f"rm{m}", RDf)
                    expmap(RDf, rn)
                    suffix_diff(RDf, SDf, m - 1, m)
                    expmap(SDf, cn)

                def optiles(v, ht=ht):
                    q = qh[ht]; k = kh[ht]
                    v.tensor_mul(out=Qs[:, :], in0=q[:, :], in1=maps["EQ"][:, :])
                    v.tensor_mul(out=Ws[:, :], in0=k[:, :], in1=maps["EQ"][:, :])
                    v.tensor_mul(out=KD[:, :], in0=k[:, :], in1=maps["ED"][:, :])
                    v.tensor_mul(out=C1[:, :], in0=k[:, :], in1=maps["colb"][:, :])
                    v.tensor_mul(out=C2[:, :], in0=C1[:, :], in1=cs["odd8"][:, :])
                    v.tensor_mul(out=C1[:, :], in0=C1[:, :], in1=cs["evn8"][:, :])
                    v.tensor_mul(out=RBe[:, 0:GT], in0=q[:, :], in1=maps["rowb"][:, :])
                    v.tensor_mul(out=RBe[:, GT:2 * GT], in0=k[:, :], in1=maps["rowb"][:, :])
                    v.tensor_mul(out=RBo[:, 0:GT], in0=RBe[:, 0:GT], in1=cs["odd8"][:, :])
                    v.tensor_mul(out=RBo[:, GT:2 * GT], in0=RBe[:, GT:2 * GT], in1=cs["odd8"][:, :])
                    v.tensor_mul(out=RBe[:, 0:GT], in0=RBe[:, 0:GT], in1=cs["evn8"][:, :])
                    v.tensor_mul(out=RBe[:, GT:2 * GT], in0=RBe[:, GT:2 * GT], in1=cs["evn8"][:, :])
                    v.tensor_mul(out=C3[:, :], in0=k[:, :], in1=maps["c8"][:, :])
                    v.tensor_mul(out=C3[:, :], in0=C3[:, :], in1=cs["evn8"][:, :])
                    v.tensor_mul(out=R8[:, 0:GT], in0=q[:, :], in1=maps["r8"][:, :])
                    v.tensor_mul(out=R8[:, GT:2 * GT], in0=k[:, :], in1=maps["r8"][:, :])
                    v.tensor_mul(out=R8[:, 0:GT], in0=R8[:, 0:GT], in1=cs["odd8"][:, :])
                    v.tensor_mul(out=R8[:, GT:2 * GT], in0=R8[:, GT:2 * GT], in1=cs["odd8"][:, :])
                    for Cx, Rx, rn, cn in ((C4, R16, "r16", "c16"), (C5, R32, "r32", "c32"),
                                           (C6, R64, "r64", "c64")):
                        v.tensor_mul(out=Cx[:, :], in0=k[:, :], in1=maps[cn][:, :])
                        v.tensor_mul(out=Rx[:, 0:GT], in0=q[:, :], in1=maps[rn][:, :])
                        v.tensor_mul(out=Rx[:, GT:2 * GT], in0=k[:, :], in1=maps[rn][:, :])
                    ph.inc(v.engine_nop())
                V(optiles)

                for ch in range(NCH):
                    if stop_after == "maps":
                        break
                    c0 = ch * 128

                    def scores(t, ch=ch, c0=c0):
                        def rhs2(Rt, lo, hi):
                            r3 = Rt[:, :].rearrange("p (h t) -> p h t", h=2)
                            return r3[:, :, c0 + lo:c0 + hi]
                        ps3 = pScore[:, :].rearrange("p (h t) -> p h t", h=2)
                        t.matmul(out=ps3[:, :, :], lhsT=C1[:, c0:c0 + 128],
                                 rhs=rhs2(RBe, 0, 128), start=True, stop=False)
                        t.matmul(out=ps3[:, :, :], lhsT=C2[:, c0:c0 + 128],
                                 rhs=rhs2(RBo, 0, 128), start=False, stop=False)
                        t.matmul(out=ps3[:, :, :], lhsT=C3[:, c0:c0 + 128],
                                 rhs=rhs2(R8, 0, 128), start=False, stop=True)
                        for p in range(3):
                            t.matmul(out=ps3[32 * p:32 * p + 16, :, 32 * p + 16:32 * p + 32],
                                     lhsT=C4[:, c0 + 32 * p:c0 + 32 * p + 16],
                                     rhs=rhs2(R16, 32 * p + 16, 32 * p + 32),
                                     start=True, stop=True)
                        st3 = pStage[:, :].rearrange("p (h t) -> p h t", h=2)
                        t.matmul(out=st3[0:16, :, 0:16], lhsT=C4[:, c0 + 96:c0 + 112],
                                 rhs=rhs2(R16, 112, 128), start=True, stop=True)
                        for p in range(2):
                            t.matmul(out=ps3[64 * p:64 * p + 32, :, 64 * p + 32:64 * p + 64],
                                     lhsT=C5[:, c0 + 64 * p:c0 + 64 * p + 32],
                                     rhs=rhs2(R32, 64 * p + 32, 64 * p + 64),
                                     start=True, stop=True)
                        ph.inc(t.matmul(out=ps3[0:64, :, 64:128], lhsT=C6[:, c0:c0 + 64],
                                        rhs=rhs2(R64, 64, 128), start=True, stop=True))
                    T(scores)

                    def evac(v, ch=ch):
                        v.tensor_tensor(out=mskd[:, :],
                                        in0=pScore[:, 0:128].bitcast(mybir.dt.int32),
                                        in1=cs["maskAb"][:, :], op=ALU.bitwise_and)
                        v.engine_nop()
                        v.tensor_copy(out=Aqt[:, :], in_=mskd[:, :].bitcast(F32))
                        v.tensor_tensor(out=mskd[:, :],
                                        in0=pScore[:, 128:256].bitcast(mybir.dt.int32),
                                        in1=cs["maskMb"][:, :], op=ALU.bitwise_and)
                        v.engine_nop()
                        v.tensor_scalar_mul(out=Ntn[:, :], in0=mskd[:, :].bitcast(F32),
                                            scalar1=beta_fn[:, ch * 2 + ht:ch * 2 + ht + 1])
                        v.tensor_copy(out=Aqt[96:112, 112:128], in_=pStage[0:16, 0:16])
                        ph.inc(v.tensor_scalar_mul(
                            out=Ntn[96:112, 112:128], in0=pStage[0:16, 16:32],
                            scalar1=beta_fn[96:112, ch * 2 + ht:ch * 2 + ht + 1]))
                    V(evac)

                    if stop_after == "scores":
                        continue
                    T(lambda t: ph.inc(t.transpose(out=pT[:, :], in_=Ntn[:, :],
                                                   identity=cs["identb"][:, :])))
                    SC(lambda s: ph.inc(s.copy(out=pown[0][:, :], in_=pT[:, :])))
                    for j in range(1, 7):
                        lastT = Ntn if j == 1 else powT[j - 2]
                        lastN = pown[j - 1]
                        def sqr(t, lastT=lastT, lastN=lastN, j=j):
                            ins = t.matmul(out=pSq[:, :], lhsT=lastN[:, :], rhs=lastT[:, :],
                                           start=True, stop=True)
                            if j < 6:
                                ins = t.matmul(out=pY[:, 0:128], lhsT=lastT[:, :],
                                               rhs=lastN[:, :], start=True, stop=True)
                            ph.inc(ins)
                        T(sqr)
                        def sqev(s, j=j):
                            ins = s.copy(out=powT[j - 1][:, :], in_=pSq[:, :])
                            if j < 6:
                                ins = s.copy(out=pown[j][:, :], in_=pY[:, 0:128])
                            ph.inc(ins)
                        SC(sqev)
                    T(lambda t, c0=c0: ph.inc(t.transpose(out=pT[:, :], in_=vh[ht][:, c0:c0 + 128],
                                                          identity=cs["identb"][:, :])))
                    SC(lambda s: ph.inc(s.copy(out=rhsb[:, 0:128], in_=pT[:, :])))
                    T(lambda t, c0=c0: ph.inc(t.transpose(out=pT[:, :], in_=Ws[:, c0:c0 + 128],
                                                          identity=cs["identb"][:, :])))
                    SC(lambda s: ph.inc(s.copy(out=rhsb[:, 128:256], in_=pT[:, :])))
                    ycur = rhsb
                    for j in range(6, -1, -1):
                        p = Ntn if j == 0 else powT[j - 1]
                        ynext = ybuf[j % 2]
                        T(lambda t, p=p, ycur=ycur: ph.inc(t.matmul(out=pY[:, :], lhsT=p[:, :],
                                                                    rhs=ycur[:, :],
                                                                    start=True, stop=True)))
                        SC(lambda s: ph.inc(s.copy(out=ydel[:, :], in_=pY[:, :])))
                        V(lambda v, ynext=ynext, ycur=ycur: ph.inc(
                            v.tensor_add(out=ynext[:, :], in0=ycur[:, :], in1=ydel[:, :])))
                        ycur = ynext
                    U = ycur

                    if stop_after == "solve":
                        continue
                    T(lambda t, U=U: ph.inc(t.transpose(out=pT[:, :], in_=U[:, 128:256],
                                                        identity=cs["identb"][:, :])))
                    SC(lambda s: ph.inc(s.copy(out=UWc[:, :], in_=pT[:, :])))
                    T(lambda t, ht=ht: ph.inc(t.matmul(out=pY[:, 0:128], lhsT=UWc[:, :],
                                                       rhs=S_b[ht][:, :], start=True, stop=True)))
                    SC(lambda s: ph.inc(s.copy(out=ydel[:, 0:128], in_=pY[:, 0:128])))
                    def useqf(v, U=U, ch=ch, ht=ht):
                        v.tensor_sub(out=spb[:, 0:128], in0=U[:, 0:128], in1=ydel[:, 0:128])
                        ph.inc(v.tensor_scalar_mul(out=useq[:, :], in0=spb[:, 0:128],
                                                   scalar1=beta_f[:, ch * 2 + ht:ch * 2 + ht + 1]))
                    V(useqf)
                    def omm(t, c0=c0, ht=ht):
                        t.matmul(out=pSq[:, :], lhsT=Qs[:, c0:c0 + 128], rhs=S_b[ht][:, :],
                                 start=True, stop=False)
                        ph.inc(t.matmul(out=pSq[:, :], lhsT=Aqt[:, :], rhs=useq[:, :],
                                        start=False, stop=True))
                    T(omm)
                    T(lambda t, c0=c0: ph.inc(t.transpose(out=pT[:, :], in_=KD[:, c0:c0 + 128],
                                                          identity=cs["identb"][:, :])))
                    SC(lambda s: ph.inc(s.copy(out=KDt[:, :], in_=pT[:, :])))
                    T(lambda t: ph.inc(t.matmul(out=pY[:, 0:128], lhsT=KDt[:, :],
                                                rhs=useq[:, :], start=True, stop=True)))
                    def supd(v, ch=ch, ht=ht):
                        v.scalar_tensor_tensor(out=S_f[ht][:, :], in0=S_f[ht][:, :],
                                               scalar=eGend[:, ch:ch + 1], in1=pY[:, 0:128],
                                               op0=ALU.mult, op1=ALU.add)
                        ph.inc(v.tensor_copy(out=S_b[ht][:, :], in_=S_f[ht][:, :]))
                    V(supd)
                    SC(lambda s: ph.inc(s.activation(out=osq[:, :], in_=pSq[:, :],
                                                     func=AF.Square, accum_out=ssq[:, 0:1])))
                    V(lambda v: ph.inc(v.tensor_scalar(out=ssq[:, 1:2], in0=ssq[:, 0:1],
                                                       scalar1=1.0 / 128, scalar2=1e-5,
                                                       op0=ALU.mult, op1=ALU.add)))
                    SC(lambda s: ph.inc(s.activation(out=ssq[:, 2:3], in_=ssq[:, 1:2],
                                                     func=AF.Sqrt)))
                    V(lambda v: ph.inc(v.reciprocal(out=ssq[:, 3:4], in_=ssq[:, 2:3])))
                    def ofin(v, ch=ch, ht=ht):
                        v.tensor_scalar_mul(out=spb[:, 0:128], in0=pSq[:, :], scalar1=ssq[:, 3:4])
                        dst = obuf[:, ch * 256 + ht * 128:ch * 256 + ht * 128 + 128]
                        ph.inc(v.tensor_mul(out=dst, in0=spb[:, 0:128],
                                            in1=gate_tm[:, ch * 256 + ht * 128:
                                                        ch * 256 + ht * 128 + 128]))
                    V(ofin)

            @block.gpsimd
            def _(gq, g=g):
                ph.wait(gq)
                for ch in range(NCH):
                    ins = gq.dma_start(out=ofeat[g * GT + ch * 128:g * GT + (ch + 1) * 128, :],
                                       in_=obuf[:, ch * 256:(ch + 1) * 256])
                    ph.inc(ins, 16)

        @block.gpsimd
        def _(gq):
            ph.wait(gq)
    return nc


def l0attn_inmap(core, xn_fm_bf16, ins):
    c = core
    hs = slice(256 * c, 256 * c + 256)
    CST = l0attn_consts()
    cwq = ins["L0_attn_q_conv_w"][hs].reshape(2, 128, 4)
    cwk = ins["L0_attn_k_conv_w"][hs].reshape(2, 128, 4)
    cwv = ins["L0_attn_v_conv_w"][hs].reshape(2, 128, 4)
    cw = np.concatenate([cwq[0], cwq[1], cwk[0], cwk[1], cwv[0], cwv[1]],
                        axis=1)
    m = {
        "wq": np.ascontiguousarray(ins["L0_attn_q_w"][:, hs]),
        "wk": np.ascontiguousarray(ins["L0_attn_k_w"][:, hs]),
        "wv": np.ascontiguousarray(ins["L0_attn_v_w"][:, hs]),
        "cw": np.ascontiguousarray(cw),
        "fw1": ins["L0_attn_f_w1"],
        "fw2": np.ascontiguousarray(ins["L0_attn_f_w2"][:, hs]),
        "gw1": ins["L0_attn_g_w1"],
        "gw2": np.ascontiguousarray(ins["L0_attn_g_w2"][:, hs]),
        "gb2": np.ascontiguousarray(ins["L0_attn_g_b2"][hs][None, :]),
        "bw": np.ascontiguousarray(ins["L0_attn_b_w"][:, 2 * c:2 * c + 2]),
        "alog": np.ascontiguousarray(ins["L0_attn_A_log"][2 * c:2 * c + 2][None, :]),
        "dtb": np.ascontiguousarray(ins["L0_attn_dt_bias"].reshape(16, 128)
                                    [2 * c:2 * c + 2].T),
    }
    for k, v in CST.items():
        m[k] = v
    return m


def build_mid(debug=False):
    nc = bass.Bass("TRN2", target_bir_lowering=False, debug=False)
    TT = 512
    NT = TPC // TT
    E, I = 8, 1408
    NI = I // 128
    ofm = nc.dram_tensor("ofm", [D, TPC], BF16, kind="ExternalInput").ap()
    xns = nc.dram_tensor("xns", [D, TPC], BF16, kind="ExternalInput").ap()
    xs = nc.dram_tensor("xs", [D, TPC], F32, kind="ExternalInput").ap()
    onw = nc.dram_tensor("onw", [128, 1], F32, kind="ExternalInput").ap()
    ow = nc.dram_tensor("ow", [D, D], F32, kind="ExternalInput").ap()
    rout = nc.dram_tensor("rout", [D, 8], F32, kind="ExternalInput").ap()
    wug = nc.dram_tensor("wug", [E, D, 2 * I], F32, kind="ExternalInput").ap()
    wdn = nc.dram_tensor("wdn", [E, I, D], F32, kind="ExternalInput").ap()
    swu = nc.dram_tensor("swu", [D, 2 * 5632], F32, kind="ExternalInput").ap()
    swd = nc.dram_tensor("swd", [5632, D], F32, kind="ExternalInput").ap()
    identf_in = nc.dram_tensor("identf", [128, 128], F32, kind="ExternalInput").ap()
    esel_in = nc.dram_tensor("esel", [8, 8 * 128], BF16, kind="ExternalInput").ap()
    x1o = nc.dram_tensor("x1o", [D, TPC], F32, kind="ExternalOutput").ap()
    xn1o = nc.dram_tensor("xn1o", [D, TPC], BF16, kind="ExternalOutput").ap()
    attnd = nc.dram_tensor("attnd", [D, TPC], F32, kind="Internal").ap()
    if debug:
        dbg_w8 = nc.dram_tensor("dbg_w8", [8, TPC], F32, kind="ExternalOutput").ap()
        dbg_y = nc.dram_tensor("dbg_y", [128, TPC], F32, kind="ExternalOutput").ap()
        dbg_lg = nc.dram_tensor("dbg_lg", [8, TPC], F32, kind="ExternalOutput").ap()
        dbg_h = nc.dram_tensor("dbg_h", [128, TPC], F32, kind="ExternalOutput").ap()

    with ExitStack() as st:
        def sbuf(n, sh, dt=F32):
            return st.enter_context(nc.sbuf_tensor(n, sh, dt))
        ph = Ctr(st.enter_context(nc.semaphore("ph")))
        block = st.enter_context(nc.Block())

        def T(fn):
            @block.tensor
            def _(t):
                ph.wait(t)
                fn(t)
        def V(fn):
            @block.vector
            def _(v):
                ph.wait(v)
                fn(v)
        def SC(fn):
            @block.scalar
            def _(s):
                ph.wait(s)
                fn(s)
        def SY(fn):
            @block.sync
            def _(sy):
                ph.wait(sy)
                fn(sy)
        def GQ(fn):
            @block.gpsimd
            def _(g):
                ph.wait(g)
                fn(g)

        y_sb = sbuf("y_sb", [128, NFC * TPC], BF16)
        onws = sbuf("onws", [128, 1], F32)
        idf = sbuf("idf", [128, 128], F32)
        onesc = sbuf("onesc", [1, 128], F32)
        onecol = sbuf("onecol", [128, 1], F32)
        epsc = sbuf("epsc", [1, 1], F32)
        gtile = sbuf("gtile", [128, TT], BF16)
        util = sbuf("util", [128, TT], BF16)
        wbt = sbuf("wbt", [128, TT], BF16)
        f32t = sbuf("f32t", [128, TT], F32)
        pG = st.enter_context(nc.psum_tensor("pG", [128, TT], F32))
        pU = st.enter_context(nc.psum_tensor("pU", [128, TT], F32))
        pL = st.enter_context(nc.psum_tensor("pL", [8, TT], F32))
        pW = st.enter_context(nc.psum_tensor("pW", [128, TT], F32))
        pS = st.enter_context(nc.psum_tensor("pS", [1, TT], F32))
        pQ = st.enter_context(nc.psum_tensor("pQ", [1, TT], F32))
        pM = st.enter_context(nc.psum_tensor("pM", [128, TT], F32))
        pR = st.enter_context(nc.psum_tensor("pR", [128, TT], F32))

        def setup(v):
            v.memset(onesc[:, :], 1.0)
            v.memset(onecol[:, :], 1.0)
            v.memset(epsc[:, :], 1e-5)
            for j in range(NFC):
                v.memset(y_sb[:, j * TPC:(j + 1) * TPC], 0.0)
            ph.inc(v.engine_nop())
        V(setup)

        with nc.sbuf_tensor("ofms", [128, NFC * TT], BF16) as ofms, \
             nc.sbuf_tensor("ows", [128, NFC * D], BF16) as ows:
            def ldA(sy):
                ph.inc(sy.dma_start(out=onws[:, :], in_=onw), 16)
                ph.inc(sy.dma_start(out=idf[:, :], in_=identf_in), 16)
            SY(ldA)
            def ldA2(g):
                src = ow.rearrange("(j p) n -> p j n", p=128)
                dst = ows[:, :].rearrange("p (j n) -> p j n", n=D)
                ph.inc(g.dma_start(out=dst, in_=src), 16)
            GQ(ldA2)
            for tt in range(NT):
                def ldof(sy, tt=tt):
                    for j in range(NFC):
                        ph.inc(sy.dma_start(out=ofms[:, j * TT:(j + 1) * TT],
                                            in_=ofm[j * 128:(j + 1) * 128,
                                                    tt * TT:(tt + 1) * TT]), 16)
                SY(ldof)
                def sclA(v, tt=tt):
                    for j in range(NFC):
                        v.tensor_scalar_mul(out=ofms[:, j * TT:(j + 1) * TT],
                                            in0=ofms[:, j * TT:(j + 1) * TT],
                                            scalar1=onws[:, 0:1])
                    ph.inc(v.engine_nop())
                V(sclA)
                for jo in range(NFC):
                    def mmA(t, tt=tt, jo=jo):
                        for ji in range(NFC):
                            ins = t.matmul(out=pM[:, :],
                                           lhsT=ows[:, ji * D + jo * 128:ji * D + jo * 128 + 128],
                                           rhs=ofms[:, ji * TT:(ji + 1) * TT],
                                           start=(ji == 0), stop=(ji == NFC - 1))
                        ph.inc(ins)
                    T(mmA)
                    SC(lambda s: ph.inc(s.copy(out=f32t[:, :], in_=pM[:, :])))
                    def stA(g, tt=tt, jo=jo):
                        ph.inc(g.dma_start(out=attnd[jo * 128:(jo + 1) * 128,
                                                     tt * TT:(tt + 1) * TT],
                                           in_=f32t[:, :]), 16)
                    GQ(stA)

        with nc.sbuf_tensor("xnsb", [128, NFC * TPC], BF16) as xnsb, \
             nc.sbuf_tensor("hprime", [128, NI * TPC], BF16) as hprime, \
             nc.sbuf_tensor("routs", [128, NFC * 8], BF16) as routs, \
             nc.sbuf_tensor("wga", [128, NFC * 128], BF16) as wga, \
             nc.sbuf_tensor("wgb", [128, NFC * 128], BF16) as wgb, \
             nc.sbuf_tensor("wdb", [128, NI * 128], BF16) as wdb, \
             nc.sbuf_tensor("lgt", [128, 8], F32) as lgt, \
             nc.sbuf_tensor("msk8", [128, 8], F32) as msk8, \
             nc.sbuf_tensor("negbig", [128, 1], F32) as negbig, \
             nc.sbuf_tensor("lg2", [128, 8], F32) as lg2, \
             nc.sbuf_tensor("sm", [128, 6], F32) as sm, \
             nc.sbuf_tensor("lgcm", [8, TPC], F32) as lgcm, \
             nc.sbuf_tensor("w8cm", [8, TPC], BF16) as w8cm, \
             nc.sbuf_tensor("esel_sb", [8, 8 * 128], BF16) as esel:
            def ldxn(sy):
                for j in range(NFC):
                    ph.inc(sy.dma_start(out=xnsb[:, j * TPC:(j + 1) * TPC],
                                        in_=xns[j * 128:(j + 1) * 128, :]), 16)
            SY(ldxn)
            def ldrt(g):
                src = rout.rearrange("(j p) n -> p j n", p=128)
                dst = routs[:, :].rearrange("p (j n) -> p j n", n=8)
                ph.inc(g.dma_start(out=dst, in_=src), 16)
            GQ(ldrt)
            SY(lambda sy: ph.inc(sy.dma_start(out=esel[:, :], in_=esel_in), 16))
            V(lambda v: ph.inc(v.memset(negbig[:, :], -1e30)))
            for tt in range(NT):
                def mmlg(t, tt=tt):
                    for j in range(NFC):
                        ins = t.matmul(out=pL[:, :], lhsT=routs[:, j * 8:(j + 1) * 8],
                                       rhs=xnsb[:, j * TPC + tt * TT:j * TPC + (tt + 1) * TT],
                                       start=(j == 0), stop=(j == NFC - 1))
                    ph.inc(ins)
                T(mmlg)
                SC(lambda s, tt=tt: ph.inc(s.copy(out=lgcm[:, tt * TT:(tt + 1) * TT],
                                                  in_=pL[:, :])))
                for cq in range(TT // 128):
                    def trlg(t, tt=tt, cq=cq):
                        ph.inc(t.transpose(
                            out=pW[:, 0:8],
                            in_=lgcm[:, tt * TT + cq * 128:tt * TT + (cq + 1) * 128],
                            identity=idf[0:8, 0:8]))
                    T(trlg)
                    V(lambda v: ph.inc(v.tensor_copy(out=lgt[:, :], in_=pW[:, 0:8])))
                    V(lambda v: ph.inc(v.reduce_max(out=sm[:, 0:1], in_=lgt[:, :],
                                                    axis=mybir.AxisListType.X)))
                    V(lambda v: ph.inc(v.tensor_scalar(out=msk8[:, :], in0=lgt[:, :],
                                                       scalar1=sm[:, 0:1], scalar2=None,
                                                       op0=ALU.is_ge)))
                    V(lambda v: ph.inc(v.scalar_tensor_tensor(out=lg2[:, :], in0=msk8[:, :],
                                                              scalar=negbig[:, 0:1],
                                                              in1=lgt[:, :], op0=ALU.mult,
                                                              op1=ALU.add)))
                    V(lambda v: ph.inc(v.reduce_max(out=sm[:, 1:2], in_=lg2[:, :],
                                                    axis=mybir.AxisListType.X)))
                    V(lambda v: ph.inc(v.tensor_sub(out=sm[:, 2:3], in0=sm[:, 1:2],
                                                    in1=sm[:, 0:1])))
                    V(lambda v: ph.inc(v.tensor_scalar_mul(out=sm[:, 3:4], in0=sm[:, 0:1],
                                                           scalar1=-1.0)))
                    def top2b(s):
                        s.activation(out=lg2[:, :], in_=lgt[:, :], func=AF.Exp, bias=sm[:, 3:4])
                        ph.inc(s.activation(out=sm[:, 4:5], in_=sm[:, 2:3], func=AF.Exp))
                    SC(top2b)
                    V(lambda v: ph.inc(v.tensor_scalar_add(out=sm[:, 4:5], in0=sm[:, 4:5],
                                                           scalar1=1.0)))
                    V(lambda v: ph.inc(v.reciprocal(out=sm[:, 5:6], in_=sm[:, 4:5])))
                    V(lambda v: ph.inc(v.tensor_scalar(out=lg2[:, :], in0=lg2[:, :],
                                                       scalar1=sm[:, 5:6], scalar2=None,
                                                       op0=ALU.mult)))
                    V(lambda v: ph.inc(v.tensor_scalar(out=lgt[:, :], in0=lgt[:, :],
                                                       scalar1=sm[:, 1:2], scalar2=None,
                                                       op0=ALU.is_ge)))
                    V(lambda v: ph.inc(v.tensor_mul(out=lg2[:, :], in0=lg2[:, :],
                                                    in1=lgt[:, :])))
                    def trw(t):
                        ph.inc(t.transpose(out=pW[0:8, 0:128], in_=lg2[:, :],
                                           identity=idf[:, :]))
                    T(trw)
                    V(lambda v, tt=tt, cq=cq: ph.inc(
                        v.tensor_copy(out=w8cm[:, tt * TT + cq * 128:tt * TT + (cq + 1) * 128],
                                      in_=pW[0:8, 0:128])))

            if debug:
                def dmpw(g):
                    for tt in range(NT):
                        ph.inc(g.dma_start(out=dbg_w8[:, tt * TT:(tt + 1) * TT].bitcast(BF16)[:, 0:TT] if False else dbg_w8[:, tt * TT:(tt + 1) * TT],
                                           in_=w8cm[:, tt * TT:(tt + 1) * TT]), 16)
                    ph.inc(g.dma_start(out=dbg_lg[:, :], in_=lgcm[:, :]), 16)
                GQ(dmpw)
            passes = [("e", e) for e in range(E)] + [("s", g_) for g_ in range(4)]
            for kind, ei in passes:
                for icp in range(NI):
                    def ldw(g, kind=kind, ei=ei, icp=icp):
                        if kind == "e":
                            gsrc = wug[ei, :, icp * 128:(icp + 1) * 128]
                            usrc = wug[ei, :, I + icp * 128:I + (icp + 1) * 128]
                        else:
                            base = ei * NI + icp
                            gsrc = swu[:, base * 128:(base + 1) * 128]
                            usrc = swu[:, 5632 + base * 128:5632 + (base + 1) * 128]
                        ph.inc(g.dma_start(out=wga[:, :].rearrange("p (j n) -> p j n", n=128),
                                           in_=gsrc.rearrange("(j p) n -> p j n", p=128)), 16)
                        ph.inc(g.dma_start(out=wgb[:, :].rearrange("p (j n) -> p j n", n=128),
                                           in_=usrc.rearrange("(j p) n -> p j n", p=128)), 16)
                    GQ(ldw)
                    for tt in range(NT):
                        def mmup(t, tt=tt):
                            for j in range(NFC):
                                t.matmul(out=pG[:, :], lhsT=wga[:, j * 128:(j + 1) * 128],
                                         rhs=xnsb[:, j * TPC + tt * TT:j * TPC + (tt + 1) * TT],
                                         start=(j == 0), stop=(j == NFC - 1))
                            for j in range(NFC):
                                ins = t.matmul(out=pU[:, :], lhsT=wgb[:, j * 128:(j + 1) * 128],
                                               rhs=xnsb[:, j * TPC + tt * TT:
                                                        j * TPC + (tt + 1) * TT],
                                               start=(j == 0), stop=(j == NFC - 1))
                            ph.inc(ins)
                        T(mmup)
                        SC(lambda s: ph.inc(s.activation(out=gtile[:, :], in_=pG[:, :],
                                                         func=AF.Silu)))
                        SC(lambda s: ph.inc(s.copy(out=util[:, :], in_=pU[:, :])))
                        if kind == "e":
                            def mkwb(t, ei=ei, tt=tt):
                                ph.inc(t.matmul(out=pW[:, :],
                                                lhsT=esel[:, ei * 128:(ei + 1) * 128],
                                                rhs=w8cm[0:8, tt * TT:(tt + 1) * TT],
                                                start=True, stop=True))
                            T(mkwb)
                            SC(lambda s: ph.inc(s.copy(out=wbt[:, :], in_=pW[:, :])))
                            def hmul(v, icp=icp, tt=tt):
                                v.tensor_mul(out=gtile[:, :], in0=gtile[:, :], in1=util[:, :])
                                ph.inc(v.tensor_mul(
                                    out=hprime[:, icp * TPC + tt * TT:
                                               icp * TPC + (tt + 1) * TT],
                                    in0=gtile[:, :], in1=wbt[:, :]))
                            V(hmul)
                        else:
                            def hmul2(v, icp=icp, tt=tt):
                                ph.inc(v.tensor_mul(
                                    out=hprime[:, icp * TPC + tt * TT:
                                               icp * TPC + (tt + 1) * TT],
                                    in0=gtile[:, :], in1=util[:, :]))
                            V(hmul2)
                for jo in range(NFC):
                    def lddn(g, kind=kind, ei=ei, jo=jo):
                        if kind == "e":
                            src = wdn[ei, :, jo * 128:(jo + 1) * 128]
                        else:
                            src = swd[ei * I:(ei + 1) * I, jo * 128:(jo + 1) * 128]
                        ph.inc(g.dma_start(out=wdb[:, :].rearrange("p (j n) -> p j n", n=128),
                                           in_=src.rearrange("(j p) n -> p j n", p=128)), 16)
                    GQ(lddn)
                    for tt in range(NT):
                        def mmdn(t, tt=tt):
                            for ic in range(NI):
                                ins = t.matmul(out=pM[:, :], lhsT=wdb[:, ic * 128:(ic + 1) * 128],
                                               rhs=hprime[:, ic * TPC + tt * TT:
                                                          ic * TPC + (tt + 1) * TT],
                                               start=(ic == 0), stop=(ic == NI - 1))
                            ph.inc(ins)
                        T(mmdn)
                        SC(lambda s: ph.inc(s.copy(out=util[:, :], in_=pM[:, :])))
                        def yadd(v, jo=jo, tt=tt):
                            dst = y_sb[:, jo * TPC + tt * TT:jo * TPC + (tt + 1) * TT]
                            ph.inc(v.tensor_add(out=dst, in0=dst, in1=util[:, :]))
                        V(yadd)

            if debug:
                def dmph(g):
                    ph.inc(g.dma_start(out=dbg_h[:, :], in_=hprime[:, 0:TPC]), 16)
                GQ(dmph)
        if debug:
            def dmpy(g):
                ph.inc(g.dma_start(out=dbg_y[:, :], in_=y_sb[:, 0:TPC]), 16)
            GQ(dmpy)
        with nc.sbuf_tensor("x1b", [128, NFC * TT], F32) as x1b, \
             nc.sbuf_tensor("sqf", [128, TT], F32) as sqf, \
             nc.sbuf_tensor("statm", [1, 6 * TT], F32) as statm, \
             nc.sbuf_tensor("xn1b", [128, TT], BF16) as xn1b:
            for tt in range(NT):
                def ldx(sy, tt=tt):
                    for j in range(NFC):
                        ph.inc(sy.dma_start(out=x1b[:, j * TT:(j + 1) * TT],
                                            in_=xs[j * 128:(j + 1) * 128,
                                                   tt * TT:(tt + 1) * TT]), 16)
                SY(ldx)
                for j in range(NFC):
                    def ldat(sy, tt=tt, j=j):
                        ph.inc(sy.dma_start(out=f32t[:, :],
                                            in_=attnd[j * 128:(j + 1) * 128,
                                                      tt * TT:(tt + 1) * TT]), 16)
                    SY(ldat)
                    V(lambda v, j=j: ph.inc(v.tensor_add(out=x1b[:, j * TT:(j + 1) * TT],
                                                         in0=x1b[:, j * TT:(j + 1) * TT],
                                                         in1=f32t[:, :])))
                    SC(lambda s, tt=tt, j=j: ph.inc(
                        s.copy(out=f32t[:, :],
                               in_=y_sb[:, j * TPC + tt * TT:j * TPC + (tt + 1) * TT])))
                    V(lambda v, j=j: ph.inc(v.tensor_add(out=x1b[:, j * TT:(j + 1) * TT],
                                                         in0=x1b[:, j * TT:(j + 1) * TT],
                                                         in1=f32t[:, :])))
                def stx1(g, tt=tt):
                    for j in range(NFC):
                        ph.inc(g.dma_start(out=x1o[j * 128:(j + 1) * 128,
                                                   tt * TT:(tt + 1) * TT],
                                           in_=x1b[:, j * TT:(j + 1) * TT]), 16)
                GQ(stx1)
                def mmsum(t, tt=tt):
                    for j in range(NFC):
                        ins = t.matmul(out=pS[:, :], lhsT=onecol[:, 0:1],
                                       rhs=x1b[:, j * TT:(j + 1) * TT],
                                       start=(j == 0), stop=(j == NFC - 1))
                    ph.inc(ins)
                T(mmsum)
                for j in range(NFC):
                    SC(lambda s, j=j: ph.inc(s.activation(out=sqf[:, :],
                                                          in_=x1b[:, j * TT:(j + 1) * TT],
                                                          func=AF.Square)))
                    def mmsq(t, j=j):
                        ph.inc(t.matmul(out=pQ[:, :], lhsT=onecol[:, 0:1], rhs=sqf[:, :],
                                        start=(j == 0), stop=(j == NFC - 1)))
                    T(mmsq)
                def stats(v):
                    v.tensor_scalar_mul(out=statm[:, 0 * TT:1 * TT], in0=pS[:, :], scalar1=1.0 / D)
                    v.tensor_scalar_mul(out=statm[:, 1 * TT:2 * TT], in0=pQ[:, :], scalar1=1.0 / D)
                    v.tensor_mul(out=statm[:, 2 * TT:3 * TT], in0=statm[:, 0 * TT:1 * TT],
                                 in1=statm[:, 0 * TT:1 * TT])
                    ph.inc(v.tensor_sub(out=statm[:, 3 * TT:4 * TT], in0=statm[:, 1 * TT:2 * TT],
                                        in1=statm[:, 2 * TT:3 * TT]))
                V(stats)
                SC(lambda s: ph.inc(s.activation(out=statm[:, 4 * TT:5 * TT],
                                                 in_=statm[:, 3 * TT:4 * TT],
                                                 func=AF.Sqrt, bias=epsc[0:1, 0:1])))
                V(lambda v: ph.inc(v.reciprocal(out=statm[:, 5 * TT:6 * TT],
                                                in_=statm[:, 4 * TT:5 * TT])))
                def bcst(t):
                    t.matmul(out=pM[:, :], lhsT=onesc[:, :], rhs=statm[:, 0 * TT:1 * TT],
                             start=True, stop=True)
                    ph.inc(t.matmul(out=pR[:, :], lhsT=onesc[:, :],
                                    rhs=statm[:, 5 * TT:6 * TT], start=True, stop=True))
                T(bcst)
                for j in range(NFC):
                    def norm(v, j=j):
                        v.tensor_sub(out=sqf[:, :], in0=x1b[:, j * TT:(j + 1) * TT],
                                     in1=pM[:, :])
                        ph.inc(v.tensor_mul(out=xn1b[:, :], in0=sqf[:, :], in1=pR[:, :]))
                    V(norm)
                    def stxn(g, tt=tt, j=j):
                        ph.inc(g.dma_start(out=xn1o[j * 128:(j + 1) * 128,
                                                    tt * TT:(tt + 1) * TT],
                                           in_=xn1b[:, :]), 16)
                    GQ(stxn)

        @block.gpsimd
        def _(g):
            ph.wait(g)
    return nc


def mid_inmap(core, ofeat_full_bf16, xn_fm_bf16, x_fm_f32, ins):
    c = core
    sl = slice(TPC * c, TPC * (c + 1))
    CST = l0attn_consts()
    m_ = {}
    if ofeat_full_bf16 is not None:
        m_["ofm"] = np.ascontiguousarray(ofeat_full_bf16.T[:, sl])
        m_["xns"] = np.ascontiguousarray(xn_fm_bf16[:, sl])
        m_["xs"] = np.ascontiguousarray(x_fm_f32[:, sl])
    return {
        **m_,
        "onw": np.ascontiguousarray(ins["L0_attn_o_norm_w"][:, None]),
        "ow": ins["L0_attn_o_w"],
        "rout": ins["L0_ffn_router"],
        "wug": ins["L0_ffn_w_up_gate"],
        "wdn": ins["L0_ffn_w_down"],
        "swu": ins["L0_ffn_shared_w_up_gate"],
        "swd": ins["L0_ffn_shared_w_down"],
        "identf": CST["identf"],
        "esel": np.ascontiguousarray(
            np.concatenate([np.tile((np.arange(8) == e).astype(BF16NP)[:, None],
                                    (1, 128)) for e in range(8)], axis=1)),
    }


def l1attn_consts(core):
    S = 4096
    t = np.arange(S, dtype=np.float32)
    inv = 1.0 / 10000.0 ** (np.arange(0, 64, 2) / 64.0)
    fr = t[:, None] * inv[None, :]
    cos = np.cos(fr).astype(np.float32)
    sin = np.sin(fr).astype(np.float32)
    cos_t = np.ascontiguousarray(cos.reshape(32, 128, 32).transpose(1, 0, 2).reshape(128, 32 * 32))
    sin_t = np.ascontiguousarray(sin.reshape(32, 128, 32).transpose(1, 0, 2).reshape(128, 32 * 32))
    slopes = 2.0 ** (-(np.arange(1, 33) * 8.0 / 32))
    qaug = np.zeros((2, 4 * S), np.float32)
    for h in range(4):
        sl = slopes[4 * core + h]
        qaug[0, h * S:(h + 1) * S] = -sl * t
        qaug[1, h * S:(h + 1) * S] = sl
    kpos = np.stack([np.ones(S, np.float32), t]).astype(np.float32)
    sq = np.arange(128)[:, None]; tq = np.arange(128)[None, :]
    causal = (sq <= tq).astype(BF16NP)
    return {"cos_t": cos_t, "sin_t": sin_t, "qaug": qaug, "kpos": kpos, "causal": causal}


def build_l1attn(core):
    nc = bass.Bass("TRN2", target_bir_lowering=False, debug=False)
    S = 4096
    NB = 4
    NQT = S // 128
    xn1 = nc.dram_tensor("xn1", [D, TOK], BF16, kind="ExternalInput").ap()
    wq = nc.dram_tensor("wq", [D, 256], F32, kind="ExternalInput").ap()
    wk = nc.dram_tensor("wk", [D, 64], F32, kind="ExternalInput").ap()
    wv = nc.dram_tensor("wv", [D, 64], F32, kind="ExternalInput").ap()
    CST = l1attn_consts(core)
    cin = {}
    for k, v in CST.items():
        dt = BF16 if v.dtype == BF16NP else F32
        cin[k] = nc.dram_tensor(k, list(v.shape), dt, kind="ExternalInput").ap()
    ctxo = nc.dram_tensor("ctxo", [256, TOK], BF16, kind="ExternalOutput").ap()
    qtm_d = nc.dram_tensor("qtm_d", [S, 384], BF16, kind="Internal").ap()

    with ExitStack() as st:
        def sbuf(n, sh, dt=F32):
            return st.enter_context(nc.sbuf_tensor(n, sh, dt))
        ph = Ctr(st.enter_context(nc.semaphore("ph")))
        block = st.enter_context(nc.Block())
        st.enter_context(nc.allow_low_precision(
            reason="bf16 elementwise per problem tolerance (honest bf16 blessed)"))

        def T(fn):
            @block.tensor
            def _(t):
                ph.wait(t)
                fn(t)
        def V(fn):
            @block.vector
            def _(v):
                ph.wait(v)
                fn(v)
        def SC(fn):
            @block.scalar
            def _(s):
                ph.wait(s)
                fn(s)
        def SY(fn):
            @block.sync
            def _(sy):
                ph.wait(sy)
                fn(sy)
        def GQ(fn):
            @block.gpsimd
            def _(g):
                ph.wait(g)
                fn(g)

        wqs = sbuf("wqs", [128, NFC * 256], BF16)
        wkv = sbuf("wkv", [128, NFC * 128], BF16)
        cs = {}
        for k, v in CST.items():
            dt = BF16 if v.dtype == BF16NP else F32
            cs[k] = sbuf("c_" + k, list(v.shape), dt)
        xnt = sbuf("xnt", [128, NFC * 128], BF16)
        qtmb = sbuf("qtmb", [128, 256], BF16)
        ktmb = sbuf("ktmb", [128, 64], BF16)
        rt1 = sbuf("rt1", [128, 128], BF16)
        rt2 = sbuf("rt2", [128, 128], BF16)
        vaug = sbuf("vaug", [128, NQT * 65], BF16)
        q4cm = sbuf("q4cm", [64, 4 * S], BF16)
        qx2 = sbuf("qx2", [128, S], BF16)
        kcmx = sbuf("kcmx", [128, S], BF16)
        st1 = sbuf("st1", [128, 512], F32)
        ptile = sbuf("ptile", [128, 512], BF16)
        neg30 = sbuf("neg30", [128, 1], F32)
        ctxb = sbuf("ctxb", [65, 512], BF16)
        rnb = sbuf("rnb", [64, 512], BF16)
        rsum = sbuf("rsum", [1, 512], BF16)
        ones64 = sbuf("ones64", [1, 64], BF16)
        ctxf = sbuf("ctxf", [64, 512], BF16)
        pP = st.enter_context(nc.psum_tensor("pP", [128, 256], F32))
        pKV = st.enter_context(nc.psum_tensor("pKV", [128, 128], F32))
        pSc = st.enter_context(nc.psum_tensor("pSc", [128, 512], F32))
        pCtx = st.enter_context(nc.psum_tensor("pCtx", [65, 512], F32))
        pBc = st.enter_context(nc.psum_tensor("pBc", [64, 512], F32))

        def ld0(g):
            src = wq.rearrange("(j p) n -> p j n", p=128)
            ph.inc(g.dma_start(out=wqs[:, :].rearrange("p (j n) -> p j n", n=256), in_=src), 16)
            ph.inc(g.dma_start(out=wkv[:, :].rearrange("p (j n) -> p j n", n=128)[:, :, 0:64],
                               in_=wk.rearrange("(j p) n -> p j n", p=128)), 16)
            ph.inc(g.dma_start(out=wkv[:, :].rearrange("p (j n) -> p j n", n=128)[:, :, 64:128],
                               in_=wv.rearrange("(j p) n -> p j n", p=128)), 16)
        GQ(ld0)
        def ld1(sy):
            for k in CST:
                ph.inc(sy.dma_start(out=cs[k][:, :], in_=cin[k]), 16)
        SY(ld1)
        def st0(v):
            v.memset(neg30[:, :], -30.0)
            ph.inc(v.memset(ones64[:, :], 1.0))
        V(st0)

        for b in range(NB):
            for qt in range(NQT):
                tok0 = b * S + qt * 128
                def ldxn(sy, tok0=tok0):
                    for j in range(NFC):
                        ph.inc(sy.dma_start(out=xnt[:, j * 128:(j + 1) * 128],
                                            in_=xn1[j * 128:(j + 1) * 128, tok0:tok0 + 128]), 16)
                SY(ldxn)
                def mmq(t):
                    for j in range(NFC):
                        t.matmul(out=pP[:, :], lhsT=xnt[:, j * 128:(j + 1) * 128],
                                 rhs=wqs[:, j * 256:(j + 1) * 256],
                                 start=(j == 0), stop=(j == NFC - 1))
                    for j in range(NFC):
                        ins = t.matmul(out=pKV[:, :], lhsT=xnt[:, j * 128:(j + 1) * 128],
                                       rhs=wkv[:, j * 128:(j + 1) * 128],
                                       start=(j == 0), stop=(j == NFC - 1))
                    ph.inc(ins)
                T(mmq)
                def ropeq(v, qt=qt):
                    _lp = st.enter_context(nc.allow_low_precision(
                        reason="bf16 rope elementwise, blessed tolerance")) if False else None
                    cosb = cs["cos_t"][:, qt * 32:(qt + 1) * 32]
                    sinb = cs["sin_t"][:, qt * 32:(qt + 1) * 32]
                    q3 = pP[:, :].rearrange("p (h d) -> p h d", h=4)
                    o3 = qtmb[:, :].rearrange("p (h d) -> p h d", h=4)
                    c3 = cosb.unsqueeze(1).broadcast_to([128, 4, 32])
                    s3 = sinb.unsqueeze(1).broadcast_to([128, 4, 32])
                    v.tensor_mul(out=o3[:, :, 0::2], in0=q3[:, :, 1::2], in1=s3)
                    v.engine_nop()
                    v.tensor_mul(out=rt1[:, :].rearrange("p (h d) -> p h d", h=4)[:, :, 0:32],
                                 in0=q3[:, :, 0::2], in1=c3)
                    v.engine_nop()
                    v.tensor_sub(out=o3[:, :, 0::2],
                                 in0=rt1[:, :].rearrange("p (h d) -> p h d", h=4)[:, :, 0:32],
                                 in1=o3[:, :, 0::2])
                    v.tensor_mul(out=o3[:, :, 1::2], in0=q3[:, :, 0::2], in1=s3)
                    v.engine_nop()
                    v.tensor_mul(out=rt1[:, :].rearrange("p (h d) -> p h d", h=4)[:, :, 0:32],
                                 in0=q3[:, :, 1::2], in1=c3)
                    v.engine_nop()
                    ph.inc(v.tensor_add(out=o3[:, :, 1::2],
                                        in0=rt1[:, :].rearrange("p (h d) -> p h d", h=4)[:, :, 0:32],
                                        in1=o3[:, :, 1::2]))
                V(ropeq)
                def ropek(v, qt=qt):
                    cosb = cs["cos_t"][:, qt * 32:(qt + 1) * 32]
                    sinb = cs["sin_t"][:, qt * 32:(qt + 1) * 32]
                    k2 = pKV[:, 0:64]
                    o2 = ktmb[:, :]
                    v.tensor_mul(out=o2[:, 0::2], in0=k2[:, 1::2], in1=sinb)
                    v.engine_nop()
                    v.tensor_mul(out=rt2[:, 0:32], in0=k2[:, 0::2], in1=cosb)
                    v.engine_nop()
                    v.tensor_sub(out=o2[:, 0::2], in0=rt2[:, 0:32], in1=o2[:, 0::2])
                    v.tensor_mul(out=o2[:, 1::2], in0=k2[:, 0::2], in1=sinb)
                    v.engine_nop()
                    v.tensor_mul(out=rt2[:, 0:32], in0=k2[:, 1::2], in1=cosb)
                    v.engine_nop()
                    v.tensor_add(out=o2[:, 1::2], in0=rt2[:, 0:32], in1=o2[:, 1::2])
                    v.tensor_copy(out=vaug[:, qt * 65:qt * 65 + 64], in_=pKV[:, 64:128])
                    ph.inc(v.memset(vaug[:, qt * 65 + 64:qt * 65 + 65], 1.0))
                V(ropek)
                def stqk(g, qt=qt):
                    ph.inc(g.dma_start(out=qtm_d[qt * 128:(qt + 1) * 128, 0:256],
                                       in_=qtmb[:, :]), 16)
                    ph.inc(g.dma_start(out=qtm_d[qt * 128:(qt + 1) * 128, 256:320],
                                       in_=ktmb[:, :]), 16)
                GQ(stqk)
            for hp in range(2):
                def trq(sy, hp=hp):
                    ph.inc(sy.dma_start_transpose(out=qx2[:, :],
                                                  in_=qtm_d[:, hp * 128:(hp + 1) * 128]), 16)
                SY(trq)
                def spl(v, hp=hp):
                    v.tensor_copy(out=q4cm[:, (2 * hp) * S:(2 * hp + 1) * S], in_=qx2[0:64, :])
                    ph.inc(v.tensor_copy(out=q4cm[:, (2 * hp + 1) * S:(2 * hp + 2) * S],
                                         in_=qx2[64:128, :]))
                V(spl)
            def trk(sy):
                ph.inc(sy.dma_start_transpose(out=kcmx[:, :],
                                              in_=qtm_d[:, 256:384]), 16)
            SY(trk)
            for qt in range(NQT):
                for kt in range(qt + 1):
                    def mmsc(t, qt=qt, kt=kt):
                        q3 = q4cm[:, :].rearrange("p (h t) -> p h t", h=4)
                        ps3 = pSc[:, :].rearrange("p (h t) -> p h t", h=4)
                        t.matmul(out=ps3[:, :, :], lhsT=kcmx[0:64, kt * 128:(kt + 1) * 128],
                                 rhs=q3[:, :, qt * 128:(qt + 1) * 128], start=True, stop=False)
                        qa3 = cs["qaug"][:, :].rearrange("p (h t) -> p h t", h=4)
                        ph.inc(t.matmul(out=ps3[:, :, :],
                                        lhsT=cs["kpos"][:, kt * 128:(kt + 1) * 128],
                                        rhs=qa3[:, :, qt * 128:(qt + 1) * 128],
                                        start=False, stop=True, skip_group_check=True))
                    T(mmsc)
                    def capexp(s, qt=qt, kt=kt):
                        s.activation(out=st1[:, :], in_=pSc[:, :], func=AF.Tanh,
                                     scale=1.0 / 240.0)
                        ph.inc(s.activation(out=ptile[:, :], in_=st1[:, :], func=AF.Exp,
                                            scale=30.0, bias=neg30[:, 0:1]))
                    SC(capexp)
                    if kt == qt:
                        def dmask(v):
                            p3 = ptile[:, :].rearrange("p (h t) -> p h t", h=4)
                            m3 = cs["causal"][:, :].unsqueeze(1).broadcast_to([128, 4, 128])
                            ph.inc(v.tensor_tensor(out=p3, in0=p3, in1=m3, op=ALU.mult))
                        V(dmask)
                    def mmpv(t, qt=qt, kt=kt):
                        ph.inc(t.matmul(out=pCtx[:, :], lhsT=vaug[:, kt * 65:(kt + 1) * 65],
                                        rhs=ptile[:, :], start=(kt == 0), stop=(kt == qt),
                                        skip_group_check=True))
                    T(mmpv)
                SC(lambda s: ph.inc(s.copy(out=ctxb[:, :], in_=pCtx[:, :])))
                V(lambda v: ph.inc(v.tensor_copy(out=rsum[:, :], in_=ctxb[64:65, :])))
                V(lambda v: ph.inc(v.reciprocal(out=rsum[:, :], in_=rsum[:, :])))
                def bcn(t):
                    ph.inc(t.matmul(out=pBc[:, :], lhsT=ones64[:, :],
                                    rhs=rsum[:, :], start=True, stop=True))
                T(bcn)
                SC(lambda s: ph.inc(s.copy(out=rnb[:, :], in_=pBc[:, :])))
                def cnorm(v):
                    ph.inc(v.tensor_mul(out=ctxf[:, :], in0=ctxb[0:64, :], in1=rnb[:, :]))
                V(cnorm)
                def stctx(g, b=b, qt=qt):
                    src = ctxf[:, :].rearrange("p (h t) -> p h t", h=4)
                    dst = ctxo[:, b * S + qt * 128:b * S + (qt + 1) * 128] \
                        .rearrange("(h p) t -> p h t", p=64)
                    ph.inc(g.dma_start(out=dst, in_=src), 16)
                GQ(stctx)

        @block.gpsimd
        def _(g):
            ph.wait(g)
    return nc


def l1attn_inmap(core, xn1_fm_bf16, ins):
    c = core
    CST = l1attn_consts(c)
    m = {
        "wq": np.ascontiguousarray(ins["L1_attn_Wq"][:, 256 * c:256 * (c + 1)]),
        "wk": np.ascontiguousarray(ins["L1_attn_Wk"][:, 64 * c:64 * (c + 1)]),
        "wv": np.ascontiguousarray(ins["L1_attn_Wv"][:, 64 * c:64 * (c + 1)]),
    }
    m.update(CST)
    if xn1_fm_bf16 is not None:
        m["xn1"] = xn1_fm_bf16
    return m


def build_fin():
    nc = bass.Bass("TRN2", target_bir_lowering=False, debug=False)
    TT = 512
    NT = TPC // TT
    NIC = 8192 // 128
    ctx = nc.dram_tensor("ctx", [D, TPC], BF16, kind="ExternalInput").ap()
    xn1s = nc.dram_tensor("xn1s", [D, TPC], BF16, kind="ExternalInput").ap()
    x1s = nc.dram_tensor("x1s", [D, TPC], F32, kind="ExternalInput").ap()
    wo = nc.dram_tensor("wo", [D, D], F32, kind="ExternalInput").ap()
    bo = nc.dram_tensor("bo", [128, NFC], F32, kind="ExternalInput").ap()
    wg8 = nc.dram_tensor("wg8", [D, 8192], I8, kind="ExternalInput").ap()
    wu8 = nc.dram_tensor("wu8", [D, 8192], I8, kind="ExternalInput").ap()
    wd8 = nc.dram_tensor("wd8", [8192, D], I8, kind="ExternalInput").ap()
    sg = nc.dram_tensor("sg", [128, NIC], F32, kind="ExternalInput").ap()
    su = nc.dram_tensor("su", [128, NIC], F32, kind="ExternalInput").ap()
    sd = nc.dram_tensor("sd", [128, NFC], F32, kind="ExternalInput").ap()
    x2o = nc.dram_tensor("x2o", [D, TPC], F32, kind="ExternalOutput").ap()
    attnd2 = nc.dram_tensor("attnd2", [D, TPC], F32, kind="Internal").ap()

    with ExitStack() as st:
        def sbuf(n, sh, dt=F32):
            return st.enter_context(nc.sbuf_tensor(n, sh, dt))
        ph = Ctr(st.enter_context(nc.semaphore("ph")))
        block = st.enter_context(nc.Block())
        def T(fn):
            @block.tensor
            def _(t):
                ph.wait(t)
                fn(t)
        def V(fn):
            @block.vector
            def _(v):
                ph.wait(v)
                fn(v)
        def SC(fn):
            @block.scalar
            def _(s):
                ph.wait(s)
                fn(s)
        def SY(fn):
            @block.sync
            def _(sy):
                ph.wait(sy)
                fn(sy)
        def GQ(fn):
            @block.gpsimd
            def _(g):
                ph.wait(g)
                fn(g)

        sgs = sbuf("sgs", [128, NIC], F32)
        sus = sbuf("sus", [128, NIC], F32)
        sds = sbuf("sds", [128, NFC], F32)
        bos = sbuf("bos", [128, NFC], F32)
        f32t = sbuf("f32t", [128, TT], F32)
        f32u = sbuf("f32u", [128, TT], F32)
        gt = sbuf("gt", [128, TT], BF16)
        ut = sbuf("ut", [128, TT], BF16)
        pM = st.enter_context(nc.psum_tensor("pM", [128, TT], F32))
        pG = st.enter_context(nc.psum_tensor("pG", [128, TT], F32))
        pU = st.enter_context(nc.psum_tensor("pU", [128, TT], F32))

        def lds(sy):
            for src, dst in ((sg, sgs), (su, sus), (sd, sds), (bo, bos)):
                ph.inc(sy.dma_start(out=dst[:, :], in_=src), 16)
        SY(lds)

        with nc.sbuf_tensor("ctxs", [128, NFC * TT], BF16) as ctxs, \
             nc.sbuf_tensor("wos", [128, NFC * D], BF16) as wos:
            def ldwo(g):
                src = wo.rearrange("(j p) n -> p j n", p=128)
                ph.inc(g.dma_start(out=wos[:, :].rearrange("p (j n) -> p j n", n=D), in_=src), 16)
            GQ(ldwo)
            for tt in range(NT):
                def ldctx(sy, tt=tt):
                    for j in range(NFC):
                        ph.inc(sy.dma_start(out=ctxs[:, j * TT:(j + 1) * TT],
                                            in_=ctx[j * 128:(j + 1) * 128,
                                                    tt * TT:(tt + 1) * TT]), 16)
                SY(ldctx)
                for jo in range(NFC):
                    def mmA(t, tt=tt, jo=jo):
                        for ji in range(NFC):
                            ins = t.matmul(out=pM[:, :],
                                           lhsT=wos[:, ji * D + jo * 128:ji * D + jo * 128 + 128],
                                           rhs=ctxs[:, ji * TT:(ji + 1) * TT],
                                           start=(ji == 0), stop=(ji == NFC - 1))
                        ph.inc(ins)
                    T(mmA)
                    def evA(v, jo=jo):
                        ph.inc(v.tensor_scalar_add(out=f32t[:, :], in0=pM[:, :],
                                                   scalar1=bos[:, jo:jo + 1]))
                    V(evA)
                    def stA(g, tt=tt, jo=jo):
                        ph.inc(g.dma_start(out=attnd2[jo * 128:(jo + 1) * 128,
                                                      tt * TT:(tt + 1) * TT],
                                           in_=f32t[:, :]), 16)
                    GQ(stA)

        with nc.sbuf_tensor("xn1sb", [128, NFC * TPC], BF16) as xn1sb, \
             nc.sbuf_tensor("hst", [128, NIC * TT], BF16) as hst, \
             nc.sbuf_tensor("wgc", [128, NFC * 128], BF16) as wgc, \
             nc.sbuf_tensor("wuc", [128, NFC * 128], BF16) as wuc, \
             nc.sbuf_tensor("wdc", [128, NIC * 128], BF16) as wdc, \
             nc.sbuf_tensor("x2b", [128, TT], F32) as x2b:
            def ldxn(sy):
                for j in range(NFC):
                    ph.inc(sy.dma_start(out=xn1sb[:, j * TPC:(j + 1) * TPC],
                                        in_=xn1s[j * 128:(j + 1) * 128, :]), 16)
            SY(ldxn)
            for tt in range(NT):
                for ic in range(NIC):
                    def ldw(g, ic=ic):
                        ph.inc(g.dma_start(out=wgc[:, :].rearrange("p (j n) -> p j n", n=128),
                                           in_=wg8[:, ic * 128:(ic + 1) * 128]
                                           .rearrange("(j p) n -> p j n", p=128)), 16)
                        ph.inc(g.dma_start(out=wuc[:, :].rearrange("p (j n) -> p j n", n=128),
                                           in_=wu8[:, ic * 128:(ic + 1) * 128]
                                           .rearrange("(j p) n -> p j n", p=128)), 16)
                    GQ(ldw)
                    def mmgu(t, tt=tt):
                        for j in range(NFC):
                            t.matmul(out=pG[:, :], lhsT=wgc[:, j * 128:(j + 1) * 128],
                                     rhs=xn1sb[:, j * TPC + tt * TT:j * TPC + (tt + 1) * TT],
                                     start=(j == 0), stop=(j == NFC - 1))
                        for j in range(NFC):
                            ins = t.matmul(out=pU[:, :], lhsT=wuc[:, j * 128:(j + 1) * 128],
                                           rhs=xn1sb[:, j * TPC + tt * TT:j * TPC + (tt + 1) * TT],
                                           start=(j == 0), stop=(j == NFC - 1))
                        ph.inc(ins)
                    T(mmgu)
                    SC(lambda s, ic=ic: ph.inc(s.activation(out=gt[:, :], in_=pG[:, :],
                                                            func=AF.Silu,
                                                            scale=sgs[:, ic:ic + 1])))
                    V(lambda v, ic=ic: ph.inc(v.tensor_scalar_mul(out=ut[:, :], in0=pU[:, :],
                                                                  scalar1=sus[:, ic:ic + 1])))
                    V(lambda v, ic=ic: ph.inc(v.tensor_mul(
                        out=hst[:, ic * TT:(ic + 1) * TT], in0=gt[:, :], in1=ut[:, :])))
                for jo in range(NFC):
                    def lddn(g, jo=jo):
                        ph.inc(g.dma_start(out=wdc[:, :].rearrange("p (j n) -> p j n", n=128),
                                           in_=wd8[:, jo * 128:(jo + 1) * 128]
                                           .rearrange("(j p) n -> p j n", p=128)), 16)
                    GQ(lddn)
                    def mmdn(t):
                        for ic in range(NIC):
                            ins = t.matmul(out=pM[:, :], lhsT=wdc[:, ic * 128:(ic + 1) * 128],
                                           rhs=hst[:, ic * TT:(ic + 1) * TT],
                                           start=(ic == 0), stop=(ic == NIC - 1))
                        ph.inc(ins)
                    T(mmdn)
                    def ldx1(sy, tt=tt, jo=jo):
                        ph.inc(sy.dma_start(out=x2b[:, :],
                                            in_=x1s[jo * 128:(jo + 1) * 128,
                                                    tt * TT:(tt + 1) * TT]), 16)
                        ph.inc(sy.dma_start(out=f32u[:, :],
                                            in_=attnd2[jo * 128:(jo + 1) * 128,
                                                       tt * TT:(tt + 1) * TT]), 16)
                    SY(ldx1)
                    def fin(v, jo=jo):
                        v.tensor_scalar_mul(out=f32t[:, :], in0=pM[:, :],
                                            scalar1=sds[:, jo:jo + 1])
                        v.engine_nop()
                        v.tensor_add(out=x2b[:, :], in0=x2b[:, :], in1=f32u[:, :])
                        v.engine_nop()
                        ph.inc(v.tensor_add(out=x2b[:, :], in0=x2b[:, :], in1=f32t[:, :]))
                    V(fin)
                    def stx2(g, tt=tt, jo=jo):
                        ph.inc(g.dma_start(out=x2o[jo * 128:(jo + 1) * 128,
                                                   tt * TT:(tt + 1) * TT],
                                           in_=x2b[:, :]), 16)
                    GQ(stx2)

        @block.gpsimd
        def _(g):
            ph.wait(g)
    return nc


def fin_inmap(core, ctx_fm_bf16, xn1_fm_bf16, x1_fm_f32, ins):
    c = core
    sl = slice(TPC * c, TPC * (c + 1))
    m_ = {}
    if ctx_fm_bf16 is not None:
        m_["ctx"] = np.ascontiguousarray(ctx_fm_bf16[:, sl])
        m_["xn1s"] = np.ascontiguousarray(xn1_fm_bf16[:, sl])
        m_["x1s"] = np.ascontiguousarray(x1_fm_f32[:, sl])
    return {
        **m_,
        "wo": ins["L1_attn_Wo"],
        "bo": np.ascontiguousarray(ins["L1_attn_bo"].reshape(NFC, 128).T),
        "wg8": ins["L1_ffn_w_gate"],
        "wu8": ins["L1_ffn_w_up"],
        "wd8": ins["L1_ffn_w_down"],
        "sg": np.ascontiguousarray(ins["L1_ffn_s_gate"].reshape(64, 128).T),
        "su": np.ascontiguousarray(ins["L1_ffn_s_up"].reshape(64, 128).T),
        "sd": np.ascontiguousarray(ins["L1_ffn_s_down"].reshape(NFC, 128).T),
    }


I32F = mybir.dt.int32


def mkctx(nc, st, block, ph, pfx):
    C = SimpleNamespace(nc=nc, st=st, block=block, ph=ph, pfx=pfx)

    def sbuf(n, sh, dt=F32):
        return st.enter_context(nc.sbuf_tensor(pfx + n, sh, dt))

    def psum(n, sh, dt=F32):
        return st.enter_context(nc.psum_tensor(pfx + n, sh, dt))

    def T(fn):
        @block.tensor
        def _(t):
            ph.wait(t)
            fn(t)

    def V(fn):
        @block.vector
        def _(v):
            ph.wait(v)
            fn(v)

    def SC(fn):
        @block.scalar
        def _(s):
            ph.wait(s)
            fn(s)

    def SY(fn):
        @block.sync
        def _(sy):
            ph.wait(sy)
            fn(sy)

    def GQ(fn):
        @block.gpsimd
        def _(g):
            ph.wait(g)
            fn(g)

    C.sbuf, C.psum = sbuf, psum
    C.T, C.V, C.SC, C.SY, C.GQ = T, V, SC, SY, GQ
    return C


def emit_coll(C, kind, in_ap, out_ap):
    ph = C.ph
    op = ALU.add if kind == "ReduceScatter" else ALU.bypass

    @C.block.gpsimd
    def _(g):
        ph.wait(g)
        ins = g.collective_compute(
            kind, op,
            replica_groups=[list(range(NCORES))],
            ins=[in_ap.opt()], outs=[out_ap.opt()])
        ph.inc(ins, 1)


def emit_ln(C, x_in, xn_out, eps=1e-5):
    TT = 512
    NT = TPC // TT
    nc, ph = C.nc, C.ph
    with ExitStack() as st2:
        def sb(n, sh, dt=F32):
            return st2.enter_context(nc.sbuf_tensor(C.pfx + n, sh, dt))
        xb = sb("xb", [128, NFC * TT])
        sqf = sb("sqf", [128, TT])
        onecol = sb("onecol", [128, 1])
        onesc = sb("onesc", [1, 128])
        epsc = sb("epsc", [1, 1])
        statm = sb("statm", [1, 6 * TT])
        xnb = sb("xnb", [128, TT], BF16)
        pS = st2.enter_context(nc.psum_tensor(C.pfx + "pS", [1, TT], F32))
        pQ = st2.enter_context(nc.psum_tensor(C.pfx + "pQ", [1, TT], F32))
        pM = st2.enter_context(nc.psum_tensor(C.pfx + "pM", [128, TT], F32))
        pR = st2.enter_context(nc.psum_tensor(C.pfx + "pR", [128, TT], F32))

        def setup(v):
            v.memset(onecol[:, :], 1.0)
            v.memset(onesc[:, :], 1.0)
            ph.inc(v.memset(epsc[:, :], eps))
        C.V(setup)
        for tt in range(NT):
            def ldx(sy, tt=tt):
                for j in range(NFC):
                    ph.inc(sy.dma_start(out=xb[:, j * TT:(j + 1) * TT],
                                        in_=x_in[j * 128:(j + 1) * 128,
                                                 tt * TT:(tt + 1) * TT]), 16)
            C.SY(ldx)
            def mmsum(t):
                for j in range(NFC):
                    ins = t.matmul(out=pS[:, :], lhsT=onecol[:, 0:1],
                                   rhs=xb[:, j * TT:(j + 1) * TT],
                                   start=(j == 0), stop=(j == NFC - 1))
                ph.inc(ins)
            C.T(mmsum)
            for j in range(NFC):
                C.SC(lambda s, j=j: ph.inc(s.activation(out=sqf[:, :],
                                                        in_=xb[:, j * TT:(j + 1) * TT],
                                                        func=AF.Square)))
                C.T(lambda t, j=j: ph.inc(t.matmul(out=pQ[:, :], lhsT=onecol[:, 0:1],
                                                   rhs=sqf[:, :], start=(j == 0),
                                                   stop=(j == NFC - 1))))
            def stats(v):
                v.tensor_scalar_mul(out=statm[:, 0 * TT:1 * TT], in0=pS[:, :], scalar1=1.0 / D)
                v.tensor_scalar_mul(out=statm[:, 1 * TT:2 * TT], in0=pQ[:, :], scalar1=1.0 / D)
                v.tensor_mul(out=statm[:, 2 * TT:3 * TT], in0=statm[:, 0 * TT:1 * TT],
                             in1=statm[:, 0 * TT:1 * TT])
                ph.inc(v.tensor_sub(out=statm[:, 3 * TT:4 * TT], in0=statm[:, 1 * TT:2 * TT],
                                    in1=statm[:, 2 * TT:3 * TT]))
            C.V(stats)
            C.SC(lambda s: ph.inc(s.activation(out=statm[:, 4 * TT:5 * TT],
                                               in_=statm[:, 3 * TT:4 * TT],
                                               func=AF.Sqrt, bias=epsc[0:1, 0:1])))
            C.V(lambda v: ph.inc(v.reciprocal(out=statm[:, 5 * TT:6 * TT],
                                              in_=statm[:, 4 * TT:5 * TT])))
            def bcst(t):
                t.matmul(out=pM[:, :], lhsT=onesc[:, :], rhs=statm[:, 0 * TT:1 * TT],
                         start=True, stop=True)
                ph.inc(t.matmul(out=pR[:, :], lhsT=onesc[:, :],
                                rhs=statm[:, 5 * TT:6 * TT], start=True, stop=True))
            C.T(bcst)
            for j in range(NFC):
                def norm(v, j=j):
                    v.tensor_sub(out=sqf[:, :], in0=xb[:, j * TT:(j + 1) * TT], in1=pM[:, :])
                    ph.inc(v.tensor_mul(out=xnb[:, :], in0=sqf[:, :], in1=pR[:, :]))
                C.V(norm)
                def stxn(g, tt=tt, j=j):
                    ph.inc(g.dma_start(out=xn_out[j * 128:(j + 1) * 128,
                                                  tt * TT:(tt + 1) * TT],
                                       in_=xnb[:, :]), 16)
                C.GQ(stxn)


def build_fused(core_check=None, upto=99):
    nc = bass.Bass("TRN2", target_bir_lowering=False, debug=False)
    EI = lambda n, sh, dt=F32: nc.dram_tensor(n, sh, dt, kind="ExternalInput").ap()
    IN = lambda n, sh, dt=F32: nc.dram_tensor(n, sh, dt, kind="Internal").ap()
    SH = lambda n, sh, dt=F32: nc.dram_tensor(n, sh, dt, kind="Internal", addr_space="Shared").ap()
    x_sl = EI("x_sl", [D, TPC])
    xn_sl = IN("xn_sl", [D, TPC], BF16)
    xn_g = IN("xn_g", [NCORES * D, TPC], BF16)
    pattn = IN("pattn", [NCORES * D, TPC], BF16)
    attn_sl = IN("attn_sl", [D, TPC], BF16)
    x1_sl = IN("x1_sl", [D, TPC], F32)
    xn1_sl = IN("xn1_sl", [D, TPC], BF16)
    xn1_g = IN("xn1_g", [NCORES * D, TPC], BF16)
    pattn2 = IN("pattn2", [NCORES * D, TPC], BF16)
    attn2_sl = IN("attn2_sl", [D, TPC], BF16)
    x2_sl = nc.dram_tensor("x2_sl", [D, TPC], F32, kind="ExternalOutput").ap()

    with ExitStack() as st:
        ph = Ctr(st.enter_context(nc.semaphore("ph")))
        block = st.enter_context(nc.Block())
        st.enter_context(nc.allow_low_precision(
            reason="bf16 elementwise per problem statement (honest bf16 blessed)"))
        C = mkctx(nc, st, block, ph, "f_")

        emit_ln(C, x_sl, xn_sl)
        if upto >= 1:
            emit_coll(C, "AllGather", xn_sl, xn_g)
        if upto >= 2:
            emit_l0attn_f(C, nc, xn_g, pattn)
        if upto >= 3:
            emit_coll(C, "ReduceScatter", pattn, attn_sl)
        if upto >= 4:
            emit_mid_f(C, nc, attn_sl, xn_sl, x_sl, x1_sl, xn1_sl)
        if upto >= 5:
            emit_coll(C, "AllGather", xn1_sl, xn1_g)
        if upto >= 6:
            emit_l1attn_f(C, nc, xn1_g, pattn2)
        if upto >= 7:
            emit_coll(C, "ReduceScatter", pattn2, attn2_sl)
        if upto >= 8:
            emit_fin_f(C, nc, attn2_sl, xn1_sl, x1_sl, x2_sl)
        if upto < 8:
            fin_map = {0: xn_sl, 1: xn_g, 2: pattn, 3: attn_sl, 4: xn1_sl,
                       5: xn1_g, 6: pattn2, 7: attn2_sl}
            srcap = fin_map[upto]
            dbgo = nc.dram_tensor("dbgo", list(srcap.shape), srcap.dtype,
                                  kind="ExternalOutput").ap()

            @block.gpsimd
            def _(g):
                ph.wait(g)
                rows = srcap.shape[0]
                step = min(2048, rows)
                for r0 in range(0, rows, step):
                    ph.inc(g.dma_start(out=dbgo[r0:r0 + step, :],
                                       in_=srcap[r0:r0 + step, :]), 16)

        @block.gpsimd
        def _(g):
            ph.wait(g)
    return nc


def emit_l0attn_f(C, nc, xn_g, pattn):
    GT = 512
    NG = TOK // GT
    NCH = GT // 128
    wq = nc.dram_tensor("wq", [D, 256], F32, kind="ExternalInput").ap()
    wk = nc.dram_tensor("wk", [D, 256], F32, kind="ExternalInput").ap()
    wv = nc.dram_tensor("wv", [D, 256], F32, kind="ExternalInput").ap()
    cw = nc.dram_tensor("cw", [128, 24], F32, kind="ExternalInput").ap()
    fw1 = nc.dram_tensor("fw1", [D, 128], F32, kind="ExternalInput").ap()
    fw2 = nc.dram_tensor("fw2", [128, 256], F32, kind="ExternalInput").ap()
    gw1 = nc.dram_tensor("gw1", [D, 128], F32, kind="ExternalInput").ap()
    gw2 = nc.dram_tensor("gw2", [128, 256], F32, kind="ExternalInput").ap()
    gb2 = nc.dram_tensor("gb2", [1, 256], F32, kind="ExternalInput").ap()
    bw = nc.dram_tensor("bw", [D, 2], F32, kind="ExternalInput").ap()
    alog = nc.dram_tensor("alog", [1, 2], F32, kind="ExternalInput").ap()
    dtb = nc.dram_tensor("dtb", [128, 2], F32, kind="ExternalInput").ap()
    owp = nc.dram_tensor("owp", [256, D], F32, kind="ExternalInput").ap()
    onw = nc.dram_tensor("onw", [128, 1], F32, kind="ExternalInput").ap()
    CST = l0attn_consts()
    I32 = mybir.dt.int32
    cin = {}
    for k, v in CST.items():
        dt = BF16 if v.dtype == BF16NP else (I32 if v.dtype == np.int32 else F32)
        cin[k] = nc.dram_tensor(k, list(v.shape), dt, kind="ExternalInput").ap()

    with ExitStack() as st:
        def sbuf(n, sh, dt=F32):
            return st.enter_context(nc.sbuf_tensor("a_" + n, sh, dt))
        wqs = sbuf("wqs", [128, NFC * 256], BF16)
        wks = sbuf("wks", [128, NFC * 256], BF16)
        wvs = sbuf("wvs", [128, NFC * 256], BF16)
        fw1s = sbuf("fw1s", [128, NFC * 128], BF16)
        gw1s = sbuf("gw1s", [128, NFC * 128], BF16)
        fw2s = sbuf("fw2s", [128, 256], BF16)
        gw2s = sbuf("gw2s", [128, 256], BF16)
        gb2s = sbuf("gb2s", [1, 256], F32)
        bws = sbuf("bws", [128, NFC * 2], BF16)
        cws = sbuf("cws", [128, 24], F32)
        alogs = sbuf("alogs", [1, 2], F32)
        negA = sbuf("negA", [128, 2], F32)
        dts = sbuf("dts", [128, 2], F32)
        onesc = sbuf("onesc", [1, 128], F32)
        ones1b = sbuf("ones1b", [1, 128], BF16)
        onecol = sbuf("onecol", [128, 1], F32)
        onebias = sbuf("onebias", [128, 1], F32)
        epsl2 = sbuf("epsl2", [1, 1], F32)
        cs = {}
        for k, v in CST.items():
            dt = BF16 if v.dtype == BF16NP else (mybir.dt.int32 if v.dtype == np.int32 else F32)
            cs[k] = sbuf("c2_" + k, list(v.shape), dt)
        mskd = sbuf("mskd", [128, 128], mybir.dt.int32)
        xng = sbuf("xng0", [128, NFC * GT], BF16)
        raw = {nm: sbuf("raw" + nm, [128, GT + 3], BF16)
               for nm in ("q0", "q1", "k0", "k1", "v0", "v1")}
        qh = [sbuf(f"qh{h}", [128, GT], BF16) for h in range(2)]
        kh = [sbuf(f"kh{h}", [128, GT], BF16) for h in range(2)]
        vh = [sbuf(f"vh{h}", [128, GT], BF16) for h in range(2)]
        innf = sbuf("innf", [128, GT], BF16)
        inng = sbuf("inng", [128, GT], BF16)
        betac = sbuf("betac", [2, GT], F32)
        nsc = sbuf("nsc", [1, GT], F32)
        beta_f = sbuf("beta_f", [128, NCH * 2], F32)
        beta_fn = sbuf("beta_fn", [128, NCH * 2], F32)
        gate_tm = sbuf("gate_tm", [128, NCH * 256], BF16)
        Gf = sbuf("Gf", [128, GT], F32)
        RDf = sbuf("RDf", [128, GT], F32)
        SDf = sbuf("SDf", [128, GT], F32)
        ERf = sbuf("ERf", [128, GT], F32)
        gkf = sbuf("gkf", [128, GT], F32)
        sp1 = sbuf("sp1", [128, GT], F32)
        spb = sbuf("spb", [128, GT], F32)
        spc = sbuf("spc", [128, GT], BF16)
        spb2 = sbuf("spb2", [128, GT], BF16)
        mapn = ["EQ", "ED", "rowb", "colb", "r8", "c8", "r16", "c16", "r32", "c32", "r64", "c64"]
        maps = {k: sbuf("m_" + k, [128, GT], BF16) for k in mapn}
        C1 = sbuf("C1", [128, GT], BF16); C2 = sbuf("C2", [128, GT], BF16)
        C3 = sbuf("C3", [128, GT], BF16); C4 = sbuf("C4", [128, GT], BF16)
        C5 = sbuf("C5", [128, GT], BF16); C6 = sbuf("C6", [128, GT], BF16)
        RBe = sbuf("RBe", [128, 2 * GT], BF16); RBo = sbuf("RBo", [128, 2 * GT], BF16)
        R8 = sbuf("R8", [128, 2 * GT], BF16); R16 = sbuf("R16", [128, 2 * GT], BF16)
        R32 = sbuf("R32", [128, 2 * GT], BF16); R64 = sbuf("R64", [128, 2 * GT], BF16)
        Qs = sbuf("Qs", [128, GT], BF16); Ws = sbuf("Ws", [128, GT], BF16)
        KD = sbuf("KD", [128, GT], BF16)
        eGend = sbuf("eGend", [128, NCH], F32)
        Aqt4 = sbuf("Aqt4", [128, NCH * 128], BF16)
        Ntn4 = sbuf("Ntn4", [128, NCH * 128], BF16)
        powT4 = [sbuf(f"powT4_{i}", [128, NCH * 128], BF16) for i in range(6)]
        pown4 = [sbuf(f"pown4_{i}", [128, NCH * 128], BF16) for i in range(6)]
        ybuf4 = [sbuf(f"ybuf4_{i}", [128, NCH * 256], BF16) for i in range(2)]
        rhsb4 = sbuf("rhsb4", [128, NCH * 256], BF16)
        S_f = [sbuf(f"S_f{h}", [128, 128], F32) for h in range(2)]
        S_b = [sbuf(f"S_b{h}", [128, 128], BF16) for h in range(2)]
        UWc = sbuf("UWc", [128, 128], BF16)
        KDt = sbuf("KDt", [128, 128], BF16)
        xdel = sbuf("xdel", [128, 128], BF16)
        useq = sbuf("useq", [128, 128], BF16)
        ssq = sbuf("ssq", [128, 12], F32)
        obuf = sbuf("obuf", [128, NCH * 256], BF16)
        ows2 = sbuf("ows2", [128, 2 * D], BF16)
        ofcm = sbuf("ofcm", [128, 2 * GT], BF16)
        patb2 = sbuf("patb2", [128, 2 * GT], BF16)
        onws = sbuf("onws", [128, 1], F32)
        pA = st.enter_context(nc.psum_tensor("a_pA", [128, GT], F32))
        pC = st.enter_context(nc.psum_tensor("a_pC", [2, GT], F32))
        pScoreB = st.enter_context(nc.psum_tensor("a_pScoreB", [128, NCH * 256], F32))
        pStageB = st.enter_context(nc.psum_tensor("a_pStageB", [16, NCH * 32], F32))
        pT4 = st.enter_context(nc.psum_tensor("a_pT4", [128, NCH * 128], BF16))
        pSolve = st.enter_context(nc.psum_tensor("a_pSolve", [128, NCH * 256], F32))
        ph = C.ph
        block = C.block

        def T(fn):
            @block.tensor
            def _(t):
                ph.wait(t)
                fn(t)
        def V(fn):
            @block.vector
            def _(v):
                ph.wait(v)
                fn(v)
        def SC(fn):
            @block.scalar
            def _(s):
                ph.wait(s)
                fn(s)
        def G3(fn):
            @block.gpsimd
            def _(g):
                ph.wait(g)
                fn(g)

        @block.gpsimd
        def _(g):
            ph.wait(g)
            for w_dram, w_sb, nch in ((wq, wqs, 256), (wk, wks, 256), (wv, wvs, 256),
                                      (fw1, fw1s, 128), (gw1, gw1s, 128), (bw, bws, 2)):
                src = w_dram.rearrange("(j p) n -> p j n", p=128)
                dst = w_sb[:, :].rearrange("p (j n) -> p j n", n=nch)
                ph.inc(g.dma_start(out=dst, in_=src), 16)
            for w_dram, w_sb in ((fw2, fw2s), (gw2, gw2s), (cw, cws)):
                ph.inc(g.dma_start(out=w_sb[:, :], in_=w_dram), 16)
            srcw = owp.rearrange("(j p) n -> p j n", p=128)
            dstw = ows2[:, :].rearrange("p (j n) -> p j n", n=D)
            ph.inc(g.dma_start(out=dstw, in_=srcw), 16)

        @block.sync
        def _(sy):
            ph.wait(sy)
            ph.inc(sy.dma_start(out=gb2s[:, :], in_=gb2), 16)
            ph.inc(sy.dma_start(out=alogs[:, :], in_=alog), 16)
            ph.inc(sy.dma_start(out=dts[:, :], in_=dtb), 16)
            ph.inc(sy.dma_start(out=onws[:, :], in_=onw), 16)
            for k in CST:
                ph.inc(sy.dma_start(out=cs[k][:, :], in_=cin[k]), 16)

        def setup1(v):
            v.memset(onesc[:, :], 1.0)
            v.memset(ones1b[:, :], 1.0)
            v.memset(onecol[:, :], 1.0)
            v.memset(onebias[:, :], 1.0)
            v.memset(epsl2[:, :], 1e-6)
            for h in range(2):
                v.memset(S_f[h][:, :], 0.0)
                v.memset(S_b[h][:, :], 0.0)
            ph.inc(v.engine_nop())
        V(setup1)
        SC(lambda s: ph.inc(s.activation(out=alogs[:, :], in_=alogs[:, :], func=AF.Exp)))
        V(lambda v: ph.inc(v.tensor_scalar_mul(out=alogs[:, :], in0=alogs[:, :], scalar1=-1.0)))
        T(lambda t: ph.inc(t.matmul(out=pA[:, 0:2], lhsT=onesc[:, :], rhs=alogs[:, :],
                                    start=True, stop=True)))
        V(lambda v: ph.inc(v.tensor_copy(out=negA[:, :], in_=pA[:, 0:2])))

        for g in range(NG):
            bstart = (g % max(NG // 4, 8) == 0)
            xbuf = xng

            def loadxn(sy, g=g, xbuf=xbuf):
                rk = (g * GT) // TPC
                cb = (g * GT) % TPC
                for j in range(NFC):
                    ins = sy.dma_start(out=xbuf[:, j * GT:(j + 1) * GT],
                                       in_=xn_g[rk * D + j * 128:rk * D + (j + 1) * 128,
                                                cb:cb + GT])
                    ph.inc(ins, 16)
            @block.sync
            def _(sy):
                ph.wait(sy)
                loadxn(sy)

            for pi, wsb in enumerate((wqs, wks, wvs)):
                for ht in range(2):
                    nm = ("q0q1", "k0k1", "v0v1")[pi][ht * 2:ht * 2 + 2]
                    rb = raw[nm]
                    def mmproj(t, wsb=wsb, ht=ht, xbuf=xbuf):
                        for j in range(NFC):
                            ins = t.matmul(out=pA[:, :],
                                           lhsT=wsb[:, j * 256 + ht * 128:j * 256 + ht * 128 + 128],
                                           rhs=xbuf[:, j * GT:(j + 1) * GT],
                                           start=(j == 0), stop=(j == NFC - 1))
                        ph.inc(ins)
                    T(mmproj)
                    def convf(v, rb=rb, pi=pi, ht=ht, bstart=bstart):
                        v.tensor_copy(out=rb[:, 3:3 + GT], in_=pA[:, :])
                        if bstart:
                            v.memset(rb[:, 0:3], 0.0)
                        v.engine_nop()
                        ci = (pi * 2 + ht) * 4
                        v.tensor_scalar(out=spc[:, :], in0=rb[:, 0:GT],
                                        scalar1=cws[:, ci:ci + 1], scalar2=None, op0=ALU.mult)
                        for i in range(1, 4):
                            v.scalar_tensor_tensor(out=spc[:, :], in0=rb[:, i:i + GT],
                                                   scalar=cws[:, ci + i:ci + i + 1],
                                                   in1=spc[:, :], op0=ALU.mult, op1=ALU.add)
                        ph.inc(v.tensor_copy(out=rb[:, 0:3], in_=rb[:, GT:GT + 3]))
                    V(convf)
                    dst = (qh, kh, vh)[pi][ht]
                    SC(lambda s: ph.inc(s.activation(out=sp1[:, :], in_=spc[:, :],
                                                     func=AF.Tanh, scale=0.5)))
                    def silu_v(v, dst=dst):
                        v.tensor_scalar(out=spb2[:, :], in0=sp1[:, :],
                                        scalar1=0.5, scalar2=0.5, op0=ALU.mult, op1=ALU.add)
                        v.engine_nop()
                        ph.inc(v.tensor_mul(out=dst[:, :], in0=spc[:, :], in1=spb2[:, :]))
                    V(silu_v)

            for pi in range(2):
                for ht in range(2):
                    src = (qh, kh)[pi][ht]
                    SC(lambda s, src=src: ph.inc(s.activation(out=sp1[:, :], in_=src[:, :],
                                                              func=AF.Square)))
                    T(lambda t: ph.inc(t.matmul(out=pC[0:1, :], lhsT=onecol[:, 0:1],
                                                rhs=sp1[:, :], start=True, stop=True)))
                    SC(lambda s: ph.inc(s.activation(out=nsc[0:1, :], in_=pC[0:1, :],
                                                     func=AF.Sqrt, bias=epsl2[0:1, 0:1])))
                    V(lambda v: ph.inc(v.reciprocal(out=nsc[0:1, :], in_=nsc[0:1, :])))
                    T(lambda t: ph.inc(t.matmul(out=pA[:, :], lhsT=onesc[:, :],
                                                rhs=nsc[0:1, :], start=True, stop=True)))
                    scl = (128.0 ** -0.5) if pi == 0 else 1.0
                    def l2fin(v, src=src, scl=scl):
                        v.tensor_scalar_mul(out=spb2[:, :], in0=pA[:, :], scalar1=scl)
                        v.engine_nop()
                        ph.inc(v.tensor_mul(out=src[:, :], in0=src[:, :], in1=spb2[:, :]))
                    V(l2fin)

            for wsb, dst in ((fw1s, innf), (gw1s, inng)):
                def mminner(t, wsb=wsb, xbuf=xbuf):
                    for j in range(NFC):
                        ins = t.matmul(out=pA[:, :], lhsT=wsb[:, j * 128:(j + 1) * 128],
                                       rhs=xbuf[:, j * GT:(j + 1) * GT],
                                       start=(j == 0), stop=(j == NFC - 1))
                    ph.inc(ins)
                T(mminner)
                V(lambda v, dst=dst: ph.inc(v.tensor_copy(out=dst[:, :], in_=pA[:, :])))

            def mmbeta(t, xbuf=xbuf):
                for j in range(NFC):
                    ins = t.matmul(out=pC[:, :], lhsT=bws[:, j * 2:(j + 1) * 2],
                                   rhs=xbuf[:, j * GT:(j + 1) * GT],
                                   start=(j == 0), stop=(j == NFC - 1))
                ph.inc(ins)
            T(mmbeta)
            SC(lambda s: ph.inc(s.activation(out=betac[:, :], in_=pC[:, :],
                                             func=AF.Tanh, scale=0.5)))
            V(lambda v: ph.inc(v.tensor_scalar(out=betac[:, :], in0=betac[:, :],
                                               scalar1=0.5, scalar2=0.5,
                                               op0=ALU.mult, op1=ALU.add)))
            def trbeta(t):
                for ch in range(NCH):
                    ins = t.transpose(out=pSolve[:, ch * 256:ch * 256 + 2],
                                      in_=betac[:, ch * 128:(ch + 1) * 128],
                                      identity=cs["identf"][0:2, 0:2])
                ph.inc(ins)
            T(trbeta)
            def betaev(v):
                for ch in range(NCH):
                    v.tensor_copy(out=beta_f[:, ch * 2:(ch + 1) * 2],
                                  in_=pSolve[:, ch * 256:ch * 256 + 2])
                for ch in range(NCH):
                    ins = v.tensor_scalar_mul(out=beta_fn[:, ch * 2:(ch + 1) * 2],
                                              in0=pSolve[:, ch * 256:ch * 256 + 2],
                                              scalar1=-1.0)
                ph.inc(ins)
            V(betaev)

            for cp in range(NCH // 2):
                def mmgate(t, cp=cp):
                    for q in range(2):
                        ch = cp * 2 + q
                        t.matmul(out=pA[:, q * 256:q * 256 + 256],
                                 lhsT=inng[:, ch * 128:(ch + 1) * 128],
                                 rhs=gw2s[:, :], start=True, stop=False)
                        ins = t.matmul(out=pA[:, q * 256:q * 256 + 256], lhsT=onesc[:, :],
                                       rhs=gb2s[:, :], start=False, stop=True,
                                       skip_group_check=True)
                    ph.inc(ins)
                T(mmgate)
                SC(lambda s: ph.inc(s.activation(out=sp1[:, :], in_=pA[:, :],
                                                 func=AF.Tanh, scale=0.5)))
                def gatefin(v, cp=cp):
                    ph.inc(v.tensor_scalar(out=gate_tm[:, cp * 512:(cp + 1) * 512],
                                           in0=sp1[:, :], scalar1=0.5, scalar2=0.5,
                                           op0=ALU.mult, op1=ALU.add))
                V(gatefin)

            for ht in range(2):
                T(lambda t, ht=ht: ph.inc(t.matmul(out=pA[:, :],
                                                   lhsT=fw2s[:, ht * 128:(ht + 1) * 128],
                                                   rhs=innf[:, :], start=True, stop=True)))
                V(lambda v, ht=ht: ph.inc(v.tensor_scalar_add(out=sp1[:, :], in0=pA[:, :],
                                                              scalar1=dts[:, ht:ht + 1])))
                SC(lambda s: ph.inc(s.activation(out=gkf[:, :], in_=sp1[:, :], func=AF.Exp)))
                SC(lambda s: ph.inc(s.activation(out=sp1[:, :], in_=gkf[:, :], func=AF.Ln,
                                                 bias=onebias[:, 0:1])))
                V(lambda v, ht=ht: ph.inc(v.tensor_scalar(out=gkf[:, :], in0=sp1[:, :],
                                                          scalar1=negA[:, ht:ht + 1],
                                                          scalar2=-CLAMP,
                                                          op0=ALU.mult, op1=ALU.max)))

                def scan_map(rm, dst):
                    V(lambda v, rm=rm, dst=dst: ph.inc(
                        v.tensor_tensor_scan(out=dst[:, :], data0=cs[rm][:, :],
                                             data1=gkf[:, :], initial=0.0,
                                             op0=ALU.mult, op1=ALU.add)))
                def sufd(v, src, dst, last, step):
                    nb = GT // step
                    gath = src[:, last::step].unsqueeze(2).broadcast_to([128, nb, step])
                    s3 = src[:, :].rearrange("p (a b) -> p a b", b=step)
                    d3 = dst[:, :].rearrange("p (a b) -> p a b", b=step)
                    return v.tensor_sub(out=d3, in0=gath, in1=s3)
                def expm(s, src, dst, scale=1.0):
                    return s.activation(out=maps[dst][:, :], in_=src[:, :], func=AF.Exp,
                                        scale=scale)

                scan_map("rm128", Gf)
                V(lambda v: ph.inc(sufd(v, Gf, SDf, 127, 128)))
                def exps128(s):
                    expm(s, Gf, "EQ")
                    expm(s, SDf, "ED")
                    ph.inc(s.activation(out=eGend[:, 0:NCH], in_=Gf[:, 127::128],
                                        func=AF.Exp))
                SC(exps128)
                scan_map("rm8", RDf)
                def d8(v):
                    sufd(v, RDf, SDf, 7, 8)
                    gath = RDf[:, 3::8].unsqueeze(2).broadcast_to([128, GT // 8, 8])
                    s3 = RDf[:, :].rearrange("p (a b) -> p a b", b=8)
                    d3 = ERf[:, :].rearrange("p (a b) -> p a b", b=8)
                    ph.inc(v.tensor_sub(out=d3, in0=s3, in1=gath))
                V(d8)
                def exps8(s):
                    expm(s, RDf, "r8")
                    expm(s, SDf, "c8")
                    expm(s, ERf, "rowb")
                    ph.inc(expm(s, ERf, "colb", scale=-1.0))
                SC(exps8)
                for m, rn, cn in ((16, "r16", "c16"), (32, "r32", "c32"), (64, "r64", "c64")):
                    scan_map(f"rm{m}", RDf)
                    V(lambda v, m=m: ph.inc(sufd(v, RDf, SDf, m - 1, m)))
                    def expsm(s, rn=rn, cn=cn):
                        expm(s, RDf, rn)
                        ph.inc(expm(s, SDf, cn))
                    SC(expsm)

                def optiles(v, ht=ht):
                    q = qh[ht]; k = kh[ht]
                    v.tensor_mul(out=Qs[:, :], in0=q[:, :], in1=maps["EQ"][:, :])
                    v.tensor_mul(out=Ws[:, :], in0=k[:, :], in1=maps["EQ"][:, :])
                    v.tensor_mul(out=KD[:, :], in0=k[:, :], in1=maps["ED"][:, :])
                    v.tensor_mul(out=C1[:, :], in0=k[:, :], in1=maps["colb"][:, :])
                    v.tensor_mul(out=C2[:, :], in0=C1[:, :], in1=cs["odd8"][:, :])
                    v.tensor_mul(out=C1[:, :], in0=C1[:, :], in1=cs["evn8"][:, :])
                    v.tensor_mul(out=RBe[:, 0:GT], in0=q[:, :], in1=maps["rowb"][:, :])
                    v.tensor_mul(out=RBe[:, GT:2 * GT], in0=k[:, :], in1=maps["rowb"][:, :])
                    v.tensor_mul(out=RBo[:, 0:GT], in0=RBe[:, 0:GT], in1=cs["odd8"][:, :])
                    v.tensor_mul(out=RBo[:, GT:2 * GT], in0=RBe[:, GT:2 * GT], in1=cs["odd8"][:, :])
                    v.tensor_mul(out=RBe[:, 0:GT], in0=RBe[:, 0:GT], in1=cs["evn8"][:, :])
                    v.tensor_mul(out=RBe[:, GT:2 * GT], in0=RBe[:, GT:2 * GT], in1=cs["evn8"][:, :])
                    v.tensor_mul(out=C3[:, :], in0=k[:, :], in1=maps["c8"][:, :])
                    v.tensor_mul(out=C3[:, :], in0=C3[:, :], in1=cs["evn8"][:, :])
                    v.tensor_mul(out=R8[:, 0:GT], in0=q[:, :], in1=maps["r8"][:, :])
                    v.tensor_mul(out=R8[:, GT:2 * GT], in0=k[:, :], in1=maps["r8"][:, :])
                    v.tensor_mul(out=R8[:, 0:GT], in0=R8[:, 0:GT], in1=cs["odd8"][:, :])
                    v.tensor_mul(out=R8[:, GT:2 * GT], in0=R8[:, GT:2 * GT], in1=cs["odd8"][:, :])
                    for Cx, Rx, rn, cn in ((C4, R16, "r16", "c16"), (C5, R32, "r32", "c32"),
                                           (C6, R64, "r64", "c64")):
                        v.tensor_mul(out=Cx[:, :], in0=k[:, :], in1=maps[cn][:, :])
                        v.tensor_mul(out=Rx[:, 0:GT], in0=q[:, :], in1=maps[rn][:, :])
                        v.tensor_mul(out=Rx[:, GT:2 * GT], in0=k[:, :], in1=maps[rn][:, :])
                    ph.inc(v.engine_nop())
                V(optiles)

                def scoresb(t):
                    for ch in range(NCH):
                        c0 = ch * 128
                        def rhs2(Rt, lo, hi):
                            r3 = Rt[:, :].rearrange("p (h t) -> p h t", h=2)
                            return r3[:, :, c0 + lo:c0 + hi]
                        ps3 = pScoreB[:, ch * 256:(ch + 1) * 256].rearrange(
                            "p (h t) -> p h t", h=2)
                        t.matmul(out=ps3[:, :, :], lhsT=C1[:, c0:c0 + 128],
                                 rhs=rhs2(RBe, 0, 128), start=True, stop=False)
                        t.matmul(out=ps3[:, :, :], lhsT=C2[:, c0:c0 + 128],
                                 rhs=rhs2(RBo, 0, 128), start=False, stop=False)
                        t.matmul(out=ps3[:, :, :], lhsT=C3[:, c0:c0 + 128],
                                 rhs=rhs2(R8, 0, 128), start=False, stop=True)
                        for p in range(3):
                            t.matmul(out=ps3[32 * p:32 * p + 16, :, 32 * p + 16:32 * p + 32],
                                     lhsT=C4[:, c0 + 32 * p:c0 + 32 * p + 16],
                                     rhs=rhs2(R16, 32 * p + 16, 32 * p + 32),
                                     start=True, stop=True)
                        st3 = pStageB[:, ch * 32:(ch + 1) * 32].rearrange(
                            "p (h t) -> p h t", h=2)
                        t.matmul(out=st3[0:16, :, 0:16], lhsT=C4[:, c0 + 96:c0 + 112],
                                 rhs=rhs2(R16, 112, 128), start=True, stop=True)
                        for p in range(2):
                            t.matmul(out=ps3[64 * p:64 * p + 32, :, 64 * p + 32:64 * p + 64],
                                     lhsT=C5[:, c0 + 64 * p:c0 + 64 * p + 32],
                                     rhs=rhs2(R32, 64 * p + 32, 64 * p + 64),
                                     start=True, stop=True)
                        ins = t.matmul(out=ps3[0:64, :, 64:128], lhsT=C6[:, c0:c0 + 64],
                                       rhs=rhs2(R64, 64, 128), start=True, stop=True)
                    ph.inc(ins)
                T(scoresb)

                def evacb(v, ht=ht):
                    for ch in range(NCH):
                        cA = ch * 128
                        v.tensor_tensor(out=mskd[:, :],
                                        in0=pScoreB[:, ch * 256:ch * 256 + 128]
                                        .bitcast(mybir.dt.int32),
                                        in1=cs["maskAb"][:, :], op=ALU.bitwise_and)
                        v.engine_nop()
                        v.tensor_copy(out=Aqt4[:, cA:cA + 128], in_=mskd[:, :].bitcast(F32))
                        v.tensor_tensor(out=mskd[:, :],
                                        in0=pScoreB[:, ch * 256 + 128:ch * 256 + 256]
                                        .bitcast(mybir.dt.int32),
                                        in1=cs["maskMb"][:, :], op=ALU.bitwise_and)
                        v.engine_nop()
                        v.tensor_scalar_mul(out=Ntn4[:, cA:cA + 128],
                                            in0=mskd[:, :].bitcast(F32),
                                            scalar1=beta_fn[:, ch * 2 + ht:ch * 2 + ht + 1])
                        v.tensor_copy(out=Aqt4[96:112, cA + 112:cA + 128],
                                      in_=pStageB[0:16, ch * 32:ch * 32 + 16])
                        ins = v.tensor_scalar_mul(
                            out=Ntn4[96:112, cA + 112:cA + 128],
                            in0=pStageB[0:16, ch * 32 + 16:ch * 32 + 32],
                            scalar1=beta_fn[96:112, ch * 2 + ht:ch * 2 + ht + 1])
                    ph.inc(ins)
                V(evacb)

                def trz(t):
                    for ch in range(NCH):
                        ins = t.transpose(out=pT4[:, ch * 128:(ch + 1) * 128],
                                          in_=Ntn4[:, ch * 128:(ch + 1) * 128],
                                          identity=cs["identb"][:, :])
                    ph.inc(ins)
                T(trz)
                def cpz(v):
                    for ch in range(NCH):
                        ins = v.tensor_copy(out=pown4[0][:, ch * 128:(ch + 1) * 128],
                                            in_=pT4[:, ch * 128:(ch + 1) * 128])
                    ph.inc(ins)
                V(cpz)
                for j in range(1, 7):
                    def sqrb(t, j=j):
                        for ch in range(NCH):
                            sl = slice(ch * 128, (ch + 1) * 128)
                            lastT = Ntn4[:, sl] if j == 1 else powT4[j - 2][:, sl]
                            lastN = pown4[j - 1][:, sl]
                            ins = t.matmul(out=pA[:, sl], lhsT=lastN, rhs=lastT,
                                           start=True, stop=True)
                            if j < 6:
                                ins = t.matmul(out=pSolve[:, sl], lhsT=lastT, rhs=lastN,
                                               start=True, stop=True)
                        ph.inc(ins)
                    T(sqrb)
                    def sqevb(v, j=j):
                        for ch in range(NCH):
                            sl = slice(ch * 128, (ch + 1) * 128)
                            ins = v.tensor_copy(out=powT4[j - 1][:, sl], in_=pA[:, sl])
                            if j < 6:
                                ins = v.tensor_copy(out=pown4[j][:, sl], in_=pSolve[:, sl])
                        ph.inc(ins)
                    V(sqevb)
                def trv(t, ht=ht):
                    for ch in range(NCH):
                        ins = t.transpose(out=pT4[:, ch * 128:(ch + 1) * 128],
                                          in_=vh[ht][:, ch * 128:(ch + 1) * 128],
                                          identity=cs["identb"][:, :])
                    ph.inc(ins)
                T(trv)
                def cpv(v):
                    for ch in range(NCH):
                        ins = v.tensor_copy(out=rhsb4[:, ch * 256:ch * 256 + 128],
                                            in_=pT4[:, ch * 128:(ch + 1) * 128])
                    ph.inc(ins)
                V(cpv)
                def trw(t):
                    for ch in range(NCH):
                        ins = t.transpose(out=pT4[:, ch * 128:(ch + 1) * 128],
                                          in_=Ws[:, ch * 128:(ch + 1) * 128],
                                          identity=cs["identb"][:, :])
                    ph.inc(ins)
                T(trw)
                def cpw(v):
                    for ch in range(NCH):
                        ins = v.tensor_copy(out=rhsb4[:, ch * 256 + 128:ch * 256 + 256],
                                            in_=pT4[:, ch * 128:(ch + 1) * 128])
                    ph.inc(ins)
                V(cpw)
                ydl = spb[:, :].bitcast(BF16)
                ycur = rhsb4
                for j in range(6, -1, -1):
                    ynext = ybuf4[j % 2]
                    def chmm(t, j=j, ycur=ycur):
                        for ch in range(NCH):
                            p = Ntn4 if j == 0 else powT4[j - 1]
                            ins = t.matmul(out=pSolve[:, ch * 256:(ch + 1) * 256],
                                           lhsT=p[:, ch * 128:(ch + 1) * 128],
                                           rhs=ycur[:, ch * 256:(ch + 1) * 256],
                                           start=True, stop=True)
                        ph.inc(ins)
                    T(chmm)
                    def chadd(v, ycur=ycur, ynext=ynext):
                        for ch in range(NCH):
                            v.tensor_copy(out=ydl[:, ch * 256:(ch + 1) * 256],
                                          in_=pSolve[:, ch * 256:(ch + 1) * 256])
                        v.engine_nop()
                        for ch in range(NCH):
                            ins = v.tensor_add(out=ynext[:, ch * 256:(ch + 1) * 256],
                                               in0=ycur[:, ch * 256:(ch + 1) * 256],
                                               in1=ydl[:, ch * 256:(ch + 1) * 256])
                        ph.inc(ins)
                    V(chadd)
                    ycur = ynext
                U4 = ycur

                for ch in range(NCH):
                    c0 = ch * 128
                    def trs(t, ch=ch, c0=c0, U4=U4):
                        t.transpose(out=pT4[:, 0:128], in_=U4[:, ch * 256 + 128:ch * 256 + 256],
                                    identity=cs["identb"][:, :])
                        ph.inc(t.transpose(out=pT4[:, 128:256], in_=KD[:, c0:c0 + 128],
                                           identity=cs["identb"][:, :]))
                    T(trs)
                    def cps(v):
                        v.tensor_copy(out=UWc[:, :], in_=pT4[:, 0:128])
                        ph.inc(v.tensor_copy(out=KDt[:, :], in_=pT4[:, 128:256]))
                    V(cps)
                    T(lambda t, ht=ht: ph.inc(t.matmul(out=pSolve[:, 0:128], lhsT=UWc[:, :],
                                                       rhs=S_b[ht][:, :],
                                                       start=True, stop=True)))
                    def useqf(v, U4=U4, ch=ch, ht=ht):
                        v.tensor_copy(out=xdel[:, :], in_=pSolve[:, 0:128])
                        v.engine_nop()
                        v.tensor_sub(out=spb[:, 0:128], in0=U4[:, ch * 256:ch * 256 + 128],
                                     in1=xdel[:, :])
                        v.engine_nop()
                        ph.inc(v.tensor_scalar_mul(out=useq[:, :], in0=spb[:, 0:128],
                                                   scalar1=beta_f[:, ch * 2 + ht:ch * 2 + ht + 1]))
                    V(useqf)
                    def omm3(t, c0=c0, ht=ht):
                        t.matmul(out=pSolve[:, 256:384], lhsT=Qs[:, c0:c0 + 128],
                                 rhs=S_b[ht][:, :], start=True, stop=False)
                        t.matmul(out=pSolve[:, 256:384], lhsT=Aqt4[:, c0:c0 + 128],
                                 rhs=useq[:, :], start=False, stop=True)
                        ph.inc(t.matmul(out=pSolve[:, 128:256], lhsT=KDt[:, :],
                                        rhs=useq[:, :], start=True, stop=True))
                    T(omm3)
                    def supdf(v, ch=ch, ht=ht):
                        v.scalar_tensor_tensor(out=S_f[ht][:, :], in0=S_f[ht][:, :],
                                               scalar=eGend[:, ch:ch + 1],
                                               in1=pSolve[:, 128:256],
                                               op0=ALU.mult, op1=ALU.add)
                        v.tensor_copy(out=S_b[ht][:, :], in_=S_f[ht][:, :])
                        ph.inc(v.tensor_copy(out=sp1[:, c0:c0 + 128],
                                             in_=pSolve[:, 256:384]))
                    V(supdf)

                def episq(s):
                    for ch in range(NCH):
                        ins = s.activation(out=spb[:, ch * 128:(ch + 1) * 128],
                                           in_=sp1[:, ch * 128:(ch + 1) * 128],
                                           func=AF.Square, accum_out=ssq[:, ch:ch + 1])
                    ph.inc(ins)
                SC(episq)
                V(lambda v: ph.inc(v.tensor_scalar(out=ssq[:, 4:8], in0=ssq[:, 0:4],
                                                   scalar1=1.0 / 128, scalar2=1e-5,
                                                   op0=ALU.mult, op1=ALU.add)))
                SC(lambda s: ph.inc(s.activation(out=ssq[:, 8:12], in_=ssq[:, 4:8],
                                                 func=AF.Sqrt)))
                V(lambda v: ph.inc(v.reciprocal(out=ssq[:, 8:12], in_=ssq[:, 8:12])))
                def ofinb(v, ht=ht):
                    for ch in range(NCH):
                        v.tensor_scalar_mul(out=spb2[:, ch * 128:(ch + 1) * 128],
                                            in0=sp1[:, ch * 128:(ch + 1) * 128],
                                            scalar1=ssq[:, 8 + ch:9 + ch])
                    v.engine_nop()
                    for ch in range(NCH):
                        dst = obuf[:, ch * 256 + ht * 128:ch * 256 + ht * 128 + 128]
                        ins = v.tensor_mul(out=dst, in0=spb2[:, ch * 128:(ch + 1) * 128],
                                           in1=gate_tm[:, ch * 256 + ht * 128:
                                                       ch * 256 + ht * 128 + 128])
                    ph.inc(ins)
                V(ofinb)

            for fc in range(2):
                def trob(t, fc=fc):
                    for ch in range(NCH):
                        ins = t.transpose(
                            out=pT4[:, ch * 128:(ch + 1) * 128],
                            in_=obuf[:, ch * 256 + fc * 128:ch * 256 + fc * 128 + 128],
                            identity=cs["identb"][:, :])
                    ph.inc(ins)
                T(trob)
                def cpob(v, fc=fc):
                    for ch in range(NCH):
                        ins = v.tensor_scalar_mul(
                            out=ofcm[:, fc * GT + ch * 128:fc * GT + (ch + 1) * 128],
                            in0=pT4[:, ch * 128:(ch + 1) * 128], scalar1=onws[:, 0:1])
                    ph.inc(ins)
                V(cpob)
            dstb = (g * GT) // TPC
            cb0 = (g * GT) % TPC
            for jp in range(NFC // 2):
                def pmm(t, jp=jp):
                    for q in range(2):
                        jo = jp * 2 + q
                        pdst = pA[:, 0:GT] if q == 0 else pScoreB[:, 0:GT]
                        t.matmul(out=pdst,
                                 lhsT=ows2[:, 0 * D + jo * 128:0 * D + jo * 128 + 128],
                                 rhs=ofcm[:, 0:GT], start=True, stop=False)
                        ins = t.matmul(out=pdst,
                                       lhsT=ows2[:, 1 * D + jo * 128:1 * D + jo * 128 + 128],
                                       rhs=ofcm[:, GT:2 * GT], start=False, stop=True)
                    ph.inc(ins)
                T(pmm)
                def cppat(v):
                    v.tensor_copy(out=patb2[:, 0:GT], in_=pA[:, 0:GT])
                    ph.inc(v.tensor_copy(out=patb2[:, GT:2 * GT], in_=pScoreB[:, 0:GT]))
                V(cppat)
                def stp(gq, jp=jp, dstb=dstb, cb0=cb0):
                    for q in range(2):
                        jo = jp * 2 + q
                        ins = gq.dma_start(
                            out=pattn[dstb * D + jo * 128:dstb * D + (jo + 1) * 128,
                                      cb0:cb0 + GT],
                            in_=patb2[:, q * GT:(q + 1) * GT])
                        ph.inc(ins, 16)
                G3(stp)

        @block.gpsimd
        def _(gq):
            ph.wait(gq)


def emit_mid_f(C, nc, attn_sl, xns, xs, x1o, xn1o):
    debug = False
    TT = 512
    NT = TPC // TT
    E, I = 8, 1408
    NI = I // 128
    rout = nc.dram_tensor("rout", [D, 8], F32, kind="ExternalInput").ap()
    wug = nc.dram_tensor("wug", [E, D, 2 * I], F32, kind="ExternalInput").ap()
    wdn = nc.dram_tensor("wdn", [E, I, D], F32, kind="ExternalInput").ap()
    swu = nc.dram_tensor("swu", [D, 2 * 5632], F32, kind="ExternalInput").ap()
    swd = nc.dram_tensor("swd", [5632, D], F32, kind="ExternalInput").ap()
    identf_in = nc.dram_tensor("identf_m", [128, 128], F32, kind="ExternalInput").ap()
    esel_in = nc.dram_tensor("esel", [8, 8 * 128], BF16, kind="ExternalInput").ap()
    if debug:
        dbg_w8 = nc.dram_tensor("dbg_w8", [8, TPC], F32, kind="ExternalOutput").ap()
        dbg_y = nc.dram_tensor("dbg_y", [128, TPC], F32, kind="ExternalOutput").ap()
        dbg_lg = nc.dram_tensor("dbg_lg", [8, TPC], F32, kind="ExternalOutput").ap()
        dbg_h = nc.dram_tensor("dbg_h", [128, TPC], F32, kind="ExternalOutput").ap()

    with ExitStack() as st:
        def sbuf(n, sh, dt=F32):
            return st.enter_context(nc.sbuf_tensor("m_" + n, sh, dt))
        ph = C.ph
        block = C.block

        def T(fn):
            @block.tensor
            def _(t):
                ph.wait(t)
                fn(t)
        def V(fn):
            @block.vector
            def _(v):
                ph.wait(v)
                fn(v)
        def SC(fn):
            @block.scalar
            def _(s):
                ph.wait(s)
                fn(s)
        def SY(fn):
            @block.sync
            def _(sy):
                ph.wait(sy)
                fn(sy)
        def GQ(fn):
            @block.gpsimd
            def _(g):
                ph.wait(g)
                fn(g)

        y_sb = sbuf("y_sb", [128, NFC * TPC], BF16)
        idf = sbuf("idf", [128, 128], F32)
        onesc = sbuf("onesc", [1, 128], F32)
        onecol = sbuf("onecol", [128, 1], F32)
        epsc = sbuf("epsc", [1, 1], F32)
        gtile = sbuf("gtile", [128, TT], BF16)
        util = sbuf("util", [128, TT], BF16)
        wbt = sbuf("wbt", [128, TT], BF16)
        f32t = sbuf("f32t", [128, TT], F32)
        atb = sbuf("atb", [128, TT], BF16)
        pG = st.enter_context(nc.psum_tensor("m_pG", [128, TT], F32))
        pU = st.enter_context(nc.psum_tensor("m_pU", [128, TT], F32))
        pL = st.enter_context(nc.psum_tensor("m_pL", [8, TT], F32))
        pW = st.enter_context(nc.psum_tensor("m_pW", [128, TT], F32))
        pS = st.enter_context(nc.psum_tensor("m_pS", [1, TT], F32))
        pQ = st.enter_context(nc.psum_tensor("m_pQ", [1, TT], F32))
        pM = st.enter_context(nc.psum_tensor("m_pM", [128, TT], F32))
        pR = st.enter_context(nc.psum_tensor("m_pR", [128, TT], F32))

        def setup(v):
            v.memset(onesc[:, :], 1.0)
            v.memset(onecol[:, :], 1.0)
            v.memset(epsc[:, :], 1e-5)
            for j in range(NFC):
                v.memset(y_sb[:, j * TPC:(j + 1) * TPC], 0.0)
            ph.inc(v.engine_nop())
        V(setup)

        with nc.sbuf_tensor("m_xnsb", [128, NFC * TPC], BF16) as xnsb, \
             nc.sbuf_tensor("m_hprime", [128, NI * TPC], BF16) as hprime, \
             nc.sbuf_tensor("m_routs", [128, NFC * 8], BF16) as routs, \
             nc.sbuf_tensor("m_wga", [128, NFC * 128], BF16) as wga, \
             nc.sbuf_tensor("m_wgb", [128, NFC * 128], BF16) as wgb, \
             nc.sbuf_tensor("m_wdb", [128, NI * 128], BF16) as wdb, \
             nc.sbuf_tensor("m_lgt", [128, 8], F32) as lgt, \
             nc.sbuf_tensor("m_msk8", [128, 8], F32) as msk8, \
             nc.sbuf_tensor("m_negbig", [128, 1], F32) as negbig, \
             nc.sbuf_tensor("m_lg2", [128, 8], F32) as lg2, \
             nc.sbuf_tensor("m_sm", [128, 6], F32) as sm, \
             nc.sbuf_tensor("m_lgcm", [8, TPC], F32) as lgcm, \
             nc.sbuf_tensor("m_w8cm", [8, TPC], BF16) as w8cm, \
             nc.sbuf_tensor("m_esel_sb", [8, 8 * 128], BF16) as esel:
            def ldxn(sy):
                for j in range(NFC):
                    ph.inc(sy.dma_start(out=xnsb[:, j * TPC:(j + 1) * TPC],
                                        in_=xns[j * 128:(j + 1) * 128, :]), 16)
            SY(ldxn)
            def ldid(sy):
                ph.inc(sy.dma_start(out=idf[:, :], in_=identf_in), 16)
            SY(ldid)
            def ldrt(g):
                src = rout.rearrange("(j p) n -> p j n", p=128)
                dst = routs[:, :].rearrange("p (j n) -> p j n", n=8)
                ph.inc(g.dma_start(out=dst, in_=src), 16)
            GQ(ldrt)
            SY(lambda sy: ph.inc(sy.dma_start(out=esel[:, :], in_=esel_in), 16))
            V(lambda v: ph.inc(v.memset(negbig[:, :], -1e30)))
            for tt in range(NT):
                def mmlg(t, tt=tt):
                    for j in range(NFC):
                        ins = t.matmul(out=pL[:, :], lhsT=routs[:, j * 8:(j + 1) * 8],
                                       rhs=xnsb[:, j * TPC + tt * TT:j * TPC + (tt + 1) * TT],
                                       start=(j == 0), stop=(j == NFC - 1))
                    ph.inc(ins)
                T(mmlg)
                V(lambda v, tt=tt: ph.inc(v.tensor_copy(out=lgcm[:, tt * TT:(tt + 1) * TT],
                                                  in_=pL[:, :])))
                for cq in range(TT // 128):
                    def trlg(t, tt=tt, cq=cq):
                        ph.inc(t.transpose(
                            out=pW[:, 0:8],
                            in_=lgcm[:, tt * TT + cq * 128:tt * TT + (cq + 1) * 128],
                            identity=idf[0:8, 0:8]))
                    T(trlg)
                    V(lambda v: ph.inc(v.tensor_copy(out=lgt[:, :], in_=pW[:, 0:8])))
                    V(lambda v: ph.inc(v.reduce_max(out=sm[:, 0:1], in_=lgt[:, :],
                                                    axis=mybir.AxisListType.X)))
                    V(lambda v: ph.inc(v.tensor_scalar(out=msk8[:, :], in0=lgt[:, :],
                                                       scalar1=sm[:, 0:1], scalar2=None,
                                                       op0=ALU.is_ge)))
                    V(lambda v: ph.inc(v.scalar_tensor_tensor(out=lg2[:, :], in0=msk8[:, :],
                                                              scalar=negbig[:, 0:1],
                                                              in1=lgt[:, :], op0=ALU.mult,
                                                              op1=ALU.add)))
                    V(lambda v: ph.inc(v.reduce_max(out=sm[:, 1:2], in_=lg2[:, :],
                                                    axis=mybir.AxisListType.X)))
                    V(lambda v: ph.inc(v.tensor_sub(out=sm[:, 2:3], in0=sm[:, 1:2],
                                                    in1=sm[:, 0:1])))
                    V(lambda v: ph.inc(v.tensor_scalar_mul(out=sm[:, 3:4], in0=sm[:, 0:1],
                                                           scalar1=-1.0)))
                    def top2b(s):
                        s.activation(out=lg2[:, :], in_=lgt[:, :], func=AF.Exp, bias=sm[:, 3:4])
                        ph.inc(s.activation(out=sm[:, 4:5], in_=sm[:, 2:3], func=AF.Exp))
                    SC(top2b)
                    V(lambda v: ph.inc(v.tensor_scalar_add(out=sm[:, 4:5], in0=sm[:, 4:5],
                                                           scalar1=1.0)))
                    V(lambda v: ph.inc(v.reciprocal(out=sm[:, 5:6], in_=sm[:, 4:5])))
                    V(lambda v: ph.inc(v.tensor_scalar(out=lg2[:, :], in0=lg2[:, :],
                                                       scalar1=sm[:, 5:6], scalar2=None,
                                                       op0=ALU.mult)))
                    V(lambda v: ph.inc(v.tensor_scalar(out=lgt[:, :], in0=lgt[:, :],
                                                       scalar1=sm[:, 1:2], scalar2=None,
                                                       op0=ALU.is_ge)))
                    V(lambda v: ph.inc(v.tensor_mul(out=lg2[:, :], in0=lg2[:, :],
                                                    in1=lgt[:, :])))
                    def trw(t):
                        ph.inc(t.transpose(out=pW[0:8, 0:128], in_=lg2[:, :],
                                           identity=idf[:, :]))
                    T(trw)
                    V(lambda v, tt=tt, cq=cq: ph.inc(
                        v.tensor_copy(out=w8cm[:, tt * TT + cq * 128:tt * TT + (cq + 1) * 128],
                                      in_=pW[0:8, 0:128])))

            if debug:
                def dmpw(g):
                    for tt in range(NT):
                        ph.inc(g.dma_start(out=dbg_w8[:, tt * TT:(tt + 1) * TT].bitcast(BF16)[:, 0:TT] if False else dbg_w8[:, tt * TT:(tt + 1) * TT],
                                           in_=w8cm[:, tt * TT:(tt + 1) * TT]), 16)
                    ph.inc(g.dma_start(out=dbg_lg[:, :], in_=lgcm[:, :]), 16)
                GQ(dmpw)
            passes = [("e", e) for e in range(E)] + [("s", g_) for g_ in range(4)]
            for kind, ei in passes:
                for icp in range(NI):
                    def ldw(g, kind=kind, ei=ei, icp=icp):
                        if kind == "e":
                            gsrc = wug[ei, :, icp * 128:(icp + 1) * 128]
                            usrc = wug[ei, :, I + icp * 128:I + (icp + 1) * 128]
                        else:
                            base = ei * NI + icp
                            gsrc = swu[:, base * 128:(base + 1) * 128]
                            usrc = swu[:, 5632 + base * 128:5632 + (base + 1) * 128]
                        ph.inc(g.dma_start(out=wga[:, :].rearrange("p (j n) -> p j n", n=128),
                                           in_=gsrc.rearrange("(j p) n -> p j n", p=128)), 16)
                        ph.inc(g.dma_start(out=wgb[:, :].rearrange("p (j n) -> p j n", n=128),
                                           in_=usrc.rearrange("(j p) n -> p j n", p=128)), 16)
                    GQ(ldw)
                    for tt in range(NT):
                        def mmup(t, tt=tt):
                            for j in range(NFC):
                                t.matmul(out=pG[:, :], lhsT=wga[:, j * 128:(j + 1) * 128],
                                         rhs=xnsb[:, j * TPC + tt * TT:j * TPC + (tt + 1) * TT],
                                         start=(j == 0), stop=(j == NFC - 1))
                            for j in range(NFC):
                                ins = t.matmul(out=pU[:, :], lhsT=wgb[:, j * 128:(j + 1) * 128],
                                               rhs=xnsb[:, j * TPC + tt * TT:
                                                        j * TPC + (tt + 1) * TT],
                                               start=(j == 0), stop=(j == NFC - 1))
                            ph.inc(ins)
                        T(mmup)
                        SC(lambda s: ph.inc(s.activation(out=gtile[:, :], in_=pG[:, :],
                                                         func=AF.Silu)))
                        V(lambda v: ph.inc(v.tensor_copy(out=util[:, :], in_=pU[:, :])))
                        if kind == "e":
                            def mkwb(t, ei=ei, tt=tt):
                                ph.inc(t.matmul(out=pW[:, :],
                                                lhsT=esel[:, ei * 128:(ei + 1) * 128],
                                                rhs=w8cm[0:8, tt * TT:(tt + 1) * TT],
                                                start=True, stop=True))
                            T(mkwb)
                            V(lambda v: ph.inc(v.tensor_copy(out=wbt[:, :], in_=pW[:, :])))
                            def hmul(v, icp=icp, tt=tt):
                                v.tensor_mul(out=gtile[:, :], in0=gtile[:, :], in1=util[:, :])
                                ph.inc(v.tensor_mul(
                                    out=hprime[:, icp * TPC + tt * TT:
                                               icp * TPC + (tt + 1) * TT],
                                    in0=gtile[:, :], in1=wbt[:, :]))
                            V(hmul)
                        else:
                            def hmul2(v, icp=icp, tt=tt):
                                ph.inc(v.tensor_mul(
                                    out=hprime[:, icp * TPC + tt * TT:
                                               icp * TPC + (tt + 1) * TT],
                                    in0=gtile[:, :], in1=util[:, :]))
                            V(hmul2)
                for jo in range(NFC):
                    def lddn(g, kind=kind, ei=ei, jo=jo):
                        if kind == "e":
                            src = wdn[ei, :, jo * 128:(jo + 1) * 128]
                        else:
                            src = swd[ei * I:(ei + 1) * I, jo * 128:(jo + 1) * 128]
                        ph.inc(g.dma_start(out=wdb[:, :].rearrange("p (j n) -> p j n", n=128),
                                           in_=src.rearrange("(j p) n -> p j n", p=128)), 16)
                    GQ(lddn)
                    for tt in range(NT):
                        def mmdn(t, tt=tt):
                            for ic in range(NI):
                                ins = t.matmul(out=pM[:, :], lhsT=wdb[:, ic * 128:(ic + 1) * 128],
                                               rhs=hprime[:, ic * TPC + tt * TT:
                                                          ic * TPC + (tt + 1) * TT],
                                               start=(ic == 0), stop=(ic == NI - 1))
                            ph.inc(ins)
                        T(mmdn)
                        V(lambda v: ph.inc(v.tensor_copy(out=util[:, :], in_=pM[:, :])))
                        def yadd(v, jo=jo, tt=tt):
                            dst = y_sb[:, jo * TPC + tt * TT:jo * TPC + (tt + 1) * TT]
                            ph.inc(v.tensor_add(out=dst, in0=dst, in1=util[:, :]))
                        V(yadd)

            if debug:
                def dmph(g):
                    ph.inc(g.dma_start(out=dbg_h[:, :], in_=hprime[:, 0:TPC]), 16)
                GQ(dmph)
        if debug:
            def dmpy(g):
                ph.inc(g.dma_start(out=dbg_y[:, :], in_=y_sb[:, 0:TPC]), 16)
            GQ(dmpy)
        with nc.sbuf_tensor("m_x1b", [128, NFC * TT], F32) as x1b, \
             nc.sbuf_tensor("m_sqf", [128, TT], F32) as sqf, \
             nc.sbuf_tensor("m_statm", [1, 6 * TT], F32) as statm, \
             nc.sbuf_tensor("m_xn1b", [128, TT], BF16) as xn1b:
            for tt in range(NT):
                def ldx(sy, tt=tt):
                    for j in range(NFC):
                        ph.inc(sy.dma_start(out=x1b[:, j * TT:(j + 1) * TT],
                                            in_=xs[j * 128:(j + 1) * 128,
                                                   tt * TT:(tt + 1) * TT]), 16)
                SY(ldx)
                for j in range(NFC):
                    def ldat(sy, tt=tt, j=j):
                        ph.inc(sy.dma_start(out=atb[:, :],
                                            in_=attn_sl[j * 128:(j + 1) * 128,
                                                        tt * TT:(tt + 1) * TT]), 16)
                    SY(ldat)
                    V(lambda v, j=j: ph.inc(v.tensor_copy(out=f32t[:, :], in_=atb[:, :])))
                    V(lambda v, j=j: ph.inc(v.tensor_add(out=x1b[:, j * TT:(j + 1) * TT],
                                                         in0=x1b[:, j * TT:(j + 1) * TT],
                                                         in1=f32t[:, :])))
                    V(lambda v, tt=tt, j=j: ph.inc(
                        v.tensor_copy(out=f32t[:, :],
                                      in_=y_sb[:, j * TPC + tt * TT:j * TPC + (tt + 1) * TT])))
                    V(lambda v, j=j: ph.inc(v.tensor_add(out=x1b[:, j * TT:(j + 1) * TT],
                                                         in0=x1b[:, j * TT:(j + 1) * TT],
                                                         in1=f32t[:, :])))
                def stx1(g, tt=tt):
                    for j in range(NFC):
                        ph.inc(g.dma_start(out=x1o[j * 128:(j + 1) * 128,
                                                   tt * TT:(tt + 1) * TT],
                                           in_=x1b[:, j * TT:(j + 1) * TT]), 16)
                GQ(stx1)
                def mmsum(t, tt=tt):
                    for j in range(NFC):
                        ins = t.matmul(out=pS[:, :], lhsT=onecol[:, 0:1],
                                       rhs=x1b[:, j * TT:(j + 1) * TT],
                                       start=(j == 0), stop=(j == NFC - 1))
                    ph.inc(ins)
                T(mmsum)
                for j in range(NFC):
                    SC(lambda s, j=j: ph.inc(s.activation(out=sqf[:, :],
                                                          in_=x1b[:, j * TT:(j + 1) * TT],
                                                          func=AF.Square)))
                    def mmsq(t, j=j):
                        ph.inc(t.matmul(out=pQ[:, :], lhsT=onecol[:, 0:1], rhs=sqf[:, :],
                                        start=(j == 0), stop=(j == NFC - 1)))
                    T(mmsq)
                def stats(v):
                    v.tensor_scalar_mul(out=statm[:, 0 * TT:1 * TT], in0=pS[:, :], scalar1=1.0 / D)
                    v.tensor_scalar_mul(out=statm[:, 1 * TT:2 * TT], in0=pQ[:, :], scalar1=1.0 / D)
                    v.tensor_mul(out=statm[:, 2 * TT:3 * TT], in0=statm[:, 0 * TT:1 * TT],
                                 in1=statm[:, 0 * TT:1 * TT])
                    ph.inc(v.tensor_sub(out=statm[:, 3 * TT:4 * TT], in0=statm[:, 1 * TT:2 * TT],
                                        in1=statm[:, 2 * TT:3 * TT]))
                V(stats)
                SC(lambda s: ph.inc(s.activation(out=statm[:, 4 * TT:5 * TT],
                                                 in_=statm[:, 3 * TT:4 * TT],
                                                 func=AF.Sqrt, bias=epsc[0:1, 0:1])))
                V(lambda v: ph.inc(v.reciprocal(out=statm[:, 5 * TT:6 * TT],
                                                in_=statm[:, 4 * TT:5 * TT])))
                def bcst(t):
                    t.matmul(out=pM[:, :], lhsT=onesc[:, :], rhs=statm[:, 0 * TT:1 * TT],
                             start=True, stop=True)
                    ph.inc(t.matmul(out=pR[:, :], lhsT=onesc[:, :],
                                    rhs=statm[:, 5 * TT:6 * TT], start=True, stop=True))
                T(bcst)
                for j in range(NFC):
                    def norm(v, j=j):
                        v.tensor_sub(out=sqf[:, :], in0=x1b[:, j * TT:(j + 1) * TT],
                                     in1=pM[:, :])
                        ph.inc(v.tensor_mul(out=xn1b[:, :], in0=sqf[:, :], in1=pR[:, :]))
                    V(norm)
                    def stxn(g, tt=tt, j=j):
                        ph.inc(g.dma_start(out=xn1o[j * 128:(j + 1) * 128,
                                                    tt * TT:(tt + 1) * TT],
                                           in_=xn1b[:, :]), 16)
                    GQ(stxn)

        @block.gpsimd
        def _(g):
            ph.wait(g)


def emit_l1attn_f(C, nc, xn1_g, pattn2):
    core = 0
    S = 4096
    NB = 4
    NQT = S // 128
    wq = nc.dram_tensor("wq1", [D, 256], F32, kind="ExternalInput").ap()
    wk = nc.dram_tensor("wk1", [D, 64], F32, kind="ExternalInput").ap()
    wv = nc.dram_tensor("wv1", [D, 64], F32, kind="ExternalInput").ap()
    CST = l1attn_consts(core)
    cin = {}
    for k, v in CST.items():
        dt = BF16 if v.dtype == BF16NP else F32
        cin[k] = nc.dram_tensor(k, list(v.shape), dt, kind="ExternalInput").ap()
    wop = nc.dram_tensor("wop", [256, D], F32, kind="ExternalInput").ap()
    qtm_d = nc.dram_tensor("l_qtm_d", [S, 384], BF16, kind="Internal").ap()

    with ExitStack() as st:
        def sbuf(n, sh, dt=F32):
            return st.enter_context(nc.sbuf_tensor("l_" + n, sh, dt))
        ph = C.ph
        block = C.block

        def T(fn):
            @block.tensor
            def _(t):
                ph.wait(t)
                fn(t)
        def V(fn):
            @block.vector
            def _(v):
                ph.wait(v)
                fn(v)
        def SC(fn):
            @block.scalar
            def _(s):
                ph.wait(s)
                fn(s)
        def SY(fn):
            @block.sync
            def _(sy):
                ph.wait(sy)
                fn(sy)
        def GQ(fn):
            @block.gpsimd
            def _(g):
                ph.wait(g)
                fn(g)

        wqs = sbuf("wqs", [128, NFC * 256], BF16)
        wkv = sbuf("wkv", [128, NFC * 128], BF16)
        cs = {}
        for k, v in CST.items():
            dt = BF16 if v.dtype == BF16NP else F32
            cs[k] = sbuf("c3_" + k, list(v.shape), dt)
        xnt = sbuf("xnt", [128, NFC * 128], BF16)
        qtmb = sbuf("qtmb", [128, 256], BF16)
        ktmb = sbuf("ktmb", [128, 64], BF16)
        rt1 = sbuf("rt1", [128, 128], BF16)
        rt2 = sbuf("rt2", [128, 128], BF16)
        vaug = sbuf("vaug", [128, NQT * 65], BF16)
        q4cm = sbuf("q4cm", [64, 4 * S], BF16)
        qx2 = sbuf("qx2", [128, S], BF16)
        kcmx = sbuf("kcmx", [128, S], BF16)
        st1 = sbuf("st1", [128, 512], F32)
        ptile = sbuf("ptile", [128, 512], BF16)
        neg30 = sbuf("neg30", [128, 1], F32)
        ctxb = sbuf("ctxb", [65, 512], BF16)
        rnb = sbuf("rnb", [64, 512], BF16)
        rsum = sbuf("rsum", [1, 512], BF16)
        ones64 = sbuf("ones64", [1, 64], BF16)
        ctxf = sbuf("ctxf", [64, 512], BF16)
        ctx2g = sbuf("ctx2g", [128, 2 * 512], BF16)
        wos2 = sbuf("wos2", [128, 2 * D], BF16)
        pat2 = sbuf("pat2", [128, 512], BF16)
        pPt = st.enter_context(nc.psum_tensor("l_pPt", [128, 512], F32))
        pP = st.enter_context(nc.psum_tensor("l_pP", [128, 256], F32))
        pKV = st.enter_context(nc.psum_tensor("l_pKV", [128, 128], F32))
        pSc = st.enter_context(nc.psum_tensor("l_pSc", [128, 512], F32))
        pCtx = st.enter_context(nc.psum_tensor("l_pCtx", [65, 512], F32))
        pBc = st.enter_context(nc.psum_tensor("l_pBc", [64, 512], F32))

        def ld0(g):
            srcw = wop.rearrange("(j p) n -> p j n", p=128)
            dstw = wos2[:, :].rearrange("p (j n) -> p j n", n=D)
            ph.inc(g.dma_start(out=dstw, in_=srcw), 16)
            src = wq.rearrange("(j p) n -> p j n", p=128)
            ph.inc(g.dma_start(out=wqs[:, :].rearrange("p (j n) -> p j n", n=256), in_=src), 16)
            ph.inc(g.dma_start(out=wkv[:, :].rearrange("p (j n) -> p j n", n=128)[:, :, 0:64],
                               in_=wk.rearrange("(j p) n -> p j n", p=128)), 16)
            ph.inc(g.dma_start(out=wkv[:, :].rearrange("p (j n) -> p j n", n=128)[:, :, 64:128],
                               in_=wv.rearrange("(j p) n -> p j n", p=128)), 16)
        GQ(ld0)
        def ld1(sy):
            for k in CST:
                ph.inc(sy.dma_start(out=cs[k][:, :], in_=cin[k]), 16)
        SY(ld1)
        def st0(v):
            v.memset(neg30[:, :], -30.0)
            ph.inc(v.memset(ones64[:, :], 1.0))
        V(st0)

        for b in range(NB):
            for qt in range(NQT):
                tok0 = b * S + qt * 128
                def ldxn(sy, tok0=tok0):
                    rk = tok0 // TPC
                    cb = tok0 % TPC
                    for j in range(NFC):
                        ph.inc(sy.dma_start(out=xnt[:, j * 128:(j + 1) * 128],
                                            in_=xn1_g[rk * D + j * 128:rk * D + (j + 1) * 128,
                                                      cb:cb + 128]), 16)
                SY(ldxn)
                def mmq(t):
                    for j in range(NFC):
                        t.matmul(out=pP[:, :], lhsT=xnt[:, j * 128:(j + 1) * 128],
                                 rhs=wqs[:, j * 256:(j + 1) * 256],
                                 start=(j == 0), stop=(j == NFC - 1))
                    for j in range(NFC):
                        ins = t.matmul(out=pKV[:, :], lhsT=xnt[:, j * 128:(j + 1) * 128],
                                       rhs=wkv[:, j * 128:(j + 1) * 128],
                                       start=(j == 0), stop=(j == NFC - 1))
                    ph.inc(ins)
                T(mmq)
                def ropeq(v, qt=qt):
                    _lp = st.enter_context(nc.allow_low_precision(
                        reason="bf16 rope elementwise, blessed tolerance")) if False else None
                    cosb = cs["cos_t"][:, qt * 32:(qt + 1) * 32]
                    sinb = cs["sin_t"][:, qt * 32:(qt + 1) * 32]
                    q3 = pP[:, :].rearrange("p (h d) -> p h d", h=4)
                    o3 = qtmb[:, :].rearrange("p (h d) -> p h d", h=4)
                    c3 = cosb.unsqueeze(1).broadcast_to([128, 4, 32])
                    s3 = sinb.unsqueeze(1).broadcast_to([128, 4, 32])
                    v.tensor_mul(out=o3[:, :, 0::2], in0=q3[:, :, 1::2], in1=s3)
                    v.engine_nop()
                    v.tensor_mul(out=rt1[:, :].rearrange("p (h d) -> p h d", h=4)[:, :, 0:32],
                                 in0=q3[:, :, 0::2], in1=c3)
                    v.engine_nop()
                    v.tensor_sub(out=o3[:, :, 0::2],
                                 in0=rt1[:, :].rearrange("p (h d) -> p h d", h=4)[:, :, 0:32],
                                 in1=o3[:, :, 0::2])
                    v.tensor_mul(out=o3[:, :, 1::2], in0=q3[:, :, 0::2], in1=s3)
                    v.engine_nop()
                    v.tensor_mul(out=rt1[:, :].rearrange("p (h d) -> p h d", h=4)[:, :, 0:32],
                                 in0=q3[:, :, 1::2], in1=c3)
                    v.engine_nop()
                    ph.inc(v.tensor_add(out=o3[:, :, 1::2],
                                        in0=rt1[:, :].rearrange("p (h d) -> p h d", h=4)[:, :, 0:32],
                                        in1=o3[:, :, 1::2]))
                V(ropeq)
                def ropek(v, qt=qt):
                    cosb = cs["cos_t"][:, qt * 32:(qt + 1) * 32]
                    sinb = cs["sin_t"][:, qt * 32:(qt + 1) * 32]
                    k2 = pKV[:, 0:64]
                    o2 = ktmb[:, :]
                    v.tensor_mul(out=o2[:, 0::2], in0=k2[:, 1::2], in1=sinb)
                    v.engine_nop()
                    v.tensor_mul(out=rt2[:, 0:32], in0=k2[:, 0::2], in1=cosb)
                    v.engine_nop()
                    v.tensor_sub(out=o2[:, 0::2], in0=rt2[:, 0:32], in1=o2[:, 0::2])
                    v.tensor_mul(out=o2[:, 1::2], in0=k2[:, 0::2], in1=sinb)
                    v.engine_nop()
                    v.tensor_mul(out=rt2[:, 0:32], in0=k2[:, 1::2], in1=cosb)
                    v.engine_nop()
                    v.tensor_add(out=o2[:, 1::2], in0=rt2[:, 0:32], in1=o2[:, 1::2])
                    v.tensor_copy(out=vaug[:, qt * 65:qt * 65 + 64], in_=pKV[:, 64:128])
                    ph.inc(v.memset(vaug[:, qt * 65 + 64:qt * 65 + 65], 1.0))
                V(ropek)
                def stqk(g, qt=qt):
                    ph.inc(g.dma_start(out=qtm_d[qt * 128:(qt + 1) * 128, 0:256],
                                       in_=qtmb[:, :]), 16)
                    ph.inc(g.dma_start(out=qtm_d[qt * 128:(qt + 1) * 128, 256:320],
                                       in_=ktmb[:, :]), 16)
                GQ(stqk)
            for hp in range(2):
                def trq(sy, hp=hp):
                    ph.inc(sy.dma_start_transpose(out=qx2[:, :],
                                                  in_=qtm_d[:, hp * 128:(hp + 1) * 128]), 16)
                SY(trq)
                def spl(v, hp=hp):
                    v.tensor_copy(out=q4cm[:, (2 * hp) * S:(2 * hp + 1) * S], in_=qx2[0:64, :])
                    ph.inc(v.tensor_copy(out=q4cm[:, (2 * hp + 1) * S:(2 * hp + 2) * S],
                                         in_=qx2[64:128, :]))
                V(spl)
            def trk(sy):
                ph.inc(sy.dma_start_transpose(out=kcmx[:, :],
                                              in_=qtm_d[:, 256:384]), 16)
            SY(trk)
            for qt in range(NQT):
                for kt in range(qt + 1):
                    def mmsc(t, qt=qt, kt=kt):
                        q3 = q4cm[:, :].rearrange("p (h t) -> p h t", h=4)
                        ps3 = pSc[:, :].rearrange("p (h t) -> p h t", h=4)
                        t.matmul(out=ps3[:, :, :], lhsT=kcmx[0:64, kt * 128:(kt + 1) * 128],
                                 rhs=q3[:, :, qt * 128:(qt + 1) * 128], start=True, stop=False)
                        qa3 = cs["qaug"][:, :].rearrange("p (h t) -> p h t", h=4)
                        ph.inc(t.matmul(out=ps3[:, :, :],
                                        lhsT=cs["kpos"][:, kt * 128:(kt + 1) * 128],
                                        rhs=qa3[:, :, qt * 128:(qt + 1) * 128],
                                        start=False, stop=True, skip_group_check=True))
                    T(mmsc)
                    def capexp(s, qt=qt, kt=kt):
                        s.activation(out=st1[:, :], in_=pSc[:, :], func=AF.Tanh,
                                     scale=1.0 / 240.0)
                        ph.inc(s.activation(out=ptile[:, :], in_=st1[:, :], func=AF.Exp,
                                            scale=30.0, bias=neg30[:, 0:1]))
                    SC(capexp)
                    if kt == qt:
                        def dmask(v):
                            p3 = ptile[:, :].rearrange("p (h t) -> p h t", h=4)
                            m3 = cs["causal"][:, :].unsqueeze(1).broadcast_to([128, 4, 128])
                            ph.inc(v.tensor_tensor(out=p3, in0=p3, in1=m3, op=ALU.mult))
                        V(dmask)
                    def mmpv(t, qt=qt, kt=kt):
                        ph.inc(t.matmul(out=pCtx[:, :], lhsT=vaug[:, kt * 65:(kt + 1) * 65],
                                        rhs=ptile[:, :], start=(kt == 0), stop=(kt == qt),
                                        skip_group_check=True))
                    T(mmpv)
                V(lambda v: ph.inc(v.tensor_copy(out=ctxb[:, :], in_=pCtx[:, :])))
                V(lambda v: ph.inc(v.tensor_copy(out=rsum[:, :], in_=ctxb[64:65, :])))
                V(lambda v: ph.inc(v.reciprocal(out=rsum[:, :], in_=rsum[:, :])))
                def bcn(t):
                    ph.inc(t.matmul(out=pBc[:, :], lhsT=ones64[:, :],
                                    rhs=rsum[:, :], start=True, stop=True))
                T(bcn)
                V(lambda v: ph.inc(v.tensor_copy(out=rnb[:, :], in_=pBc[:, :])))
                def cnorm(v):
                    ph.inc(v.tensor_mul(out=ctxf[:, :], in0=ctxb[0:64, :], in1=rnb[:, :]))
                V(cnorm)
                def asm2(v, qt=qt):
                    q4 = qt % 4
                    for hp in range(2):
                        v.tensor_copy(out=ctx2g[0:64, hp * 512 + q4 * 128:
                                                hp * 512 + (q4 + 1) * 128],
                                      in_=ctxf[:, (2 * hp) * 128:(2 * hp + 1) * 128])
                        v.tensor_copy(out=ctx2g[64:128, hp * 512 + q4 * 128:
                                                hp * 512 + (q4 + 1) * 128],
                                      in_=ctxf[:, (2 * hp + 1) * 128:(2 * hp + 2) * 128])
                    ph.inc(v.engine_nop())
                V(asm2)
                if qt % 4 == 3:
                    tok0 = b * S + (qt - 3) * 128
                    dstb = tok0 // TPC
                    cb0 = tok0 % TPC
                    for jo in range(NFC):
                        def pmm2(t, jo=jo):
                            t.matmul(out=pPt[:, :],
                                     lhsT=wos2[:, 0 * D + jo * 128:0 * D + jo * 128 + 128],
                                     rhs=ctx2g[:, 0:512], start=True, stop=False)
                            ph.inc(t.matmul(
                                out=pPt[:, :],
                                lhsT=wos2[:, 1 * D + jo * 128:1 * D + jo * 128 + 128],
                                rhs=ctx2g[:, 512:1024], start=False, stop=True))
                        T(pmm2)
                        V(lambda v: ph.inc(v.tensor_copy(out=pat2[:, :], in_=pPt[:, :])))
                        def stp2(g, jo=jo, dstb=dstb, cb0=cb0):
                            ph.inc(g.dma_start(
                                out=pattn2[dstb * D + jo * 128:dstb * D + (jo + 1) * 128,
                                           cb0:cb0 + 512],
                                in_=pat2[:, :]), 16)
                        GQ(stp2)

        @block.gpsimd
        def _(g):
            ph.wait(g)


def emit_fin_f(C, nc, attn2_sl, xn1s, x1s, x2o):
    TT = 512
    NT = TPC // TT
    NIC = 8192 // 128
    bo = nc.dram_tensor("bo", [128, NFC], F32, kind="ExternalInput").ap()
    wg8 = nc.dram_tensor("wg8", [D, 8192], I8, kind="ExternalInput").ap()
    wu8 = nc.dram_tensor("wu8", [D, 8192], I8, kind="ExternalInput").ap()
    wd8 = nc.dram_tensor("wd8", [8192, D], I8, kind="ExternalInput").ap()
    sg = nc.dram_tensor("sg", [128, NIC], F32, kind="ExternalInput").ap()
    su = nc.dram_tensor("su", [128, NIC], F32, kind="ExternalInput").ap()
    sd = nc.dram_tensor("sd", [128, NFC], F32, kind="ExternalInput").ap()

    with ExitStack() as st:
        def sbuf(n, sh, dt=F32):
            return st.enter_context(nc.sbuf_tensor("n_" + n, sh, dt))
        ph = C.ph
        block = C.block
        def T(fn):
            @block.tensor
            def _(t):
                ph.wait(t)
                fn(t)
        def V(fn):
            @block.vector
            def _(v):
                ph.wait(v)
                fn(v)
        def SC(fn):
            @block.scalar
            def _(s):
                ph.wait(s)
                fn(s)
        def SY(fn):
            @block.sync
            def _(sy):
                ph.wait(sy)
                fn(sy)
        def GQ(fn):
            @block.gpsimd
            def _(g):
                ph.wait(g)
                fn(g)

        sgs = sbuf("sgs", [128, NIC], F32)
        sus = sbuf("sus", [128, NIC], F32)
        sds = sbuf("sds", [128, NFC], F32)
        bos = sbuf("bos", [128, NFC], F32)
        f32t = sbuf("f32t", [128, TT], F32)
        f32u = sbuf("f32u", [128, TT], F32)
        gt = sbuf("gt", [128, TT], BF16)
        at2b = sbuf("at2b", [128, TT], BF16)
        ut = sbuf("ut", [128, TT], BF16)
        pM = st.enter_context(nc.psum_tensor("n_pM", [128, TT], F32))
        pG = st.enter_context(nc.psum_tensor("n_pG", [128, TT], F32))
        pU = st.enter_context(nc.psum_tensor("n_pU", [128, TT], F32))

        def lds(sy):
            for src, dst in ((sg, sgs), (su, sus), (sd, sds), (bo, bos)):
                ph.inc(sy.dma_start(out=dst[:, :], in_=src), 16)
        SY(lds)

        with nc.sbuf_tensor("n_xn1sb", [128, NFC * TPC], BF16) as xn1sb, \
             nc.sbuf_tensor("n_hst", [128, NIC * TT], BF16) as hst, \
             nc.sbuf_tensor("n_wgc", [128, NFC * 128], BF16) as wgc, \
             nc.sbuf_tensor("n_wuc", [128, NFC * 128], BF16) as wuc, \
             nc.sbuf_tensor("n_wdc", [128, NIC * 128], BF16) as wdc, \
             nc.sbuf_tensor("n_x2b", [128, TT], F32) as x2b:
            def ldxn(sy):
                for j in range(NFC):
                    ph.inc(sy.dma_start(out=xn1sb[:, j * TPC:(j + 1) * TPC],
                                        in_=xn1s[j * 128:(j + 1) * 128, :]), 16)
            SY(ldxn)
            for tt in range(NT):
                for ic in range(NIC):
                    def ldw(g, ic=ic):
                        ph.inc(g.dma_start(out=wgc[:, :].rearrange("p (j n) -> p j n", n=128),
                                           in_=wg8[:, ic * 128:(ic + 1) * 128]
                                           .rearrange("(j p) n -> p j n", p=128)), 16)
                        ph.inc(g.dma_start(out=wuc[:, :].rearrange("p (j n) -> p j n", n=128),
                                           in_=wu8[:, ic * 128:(ic + 1) * 128]
                                           .rearrange("(j p) n -> p j n", p=128)), 16)
                    GQ(ldw)
                    def mmgu(t, tt=tt):
                        for j in range(NFC):
                            t.matmul(out=pG[:, :], lhsT=wgc[:, j * 128:(j + 1) * 128],
                                     rhs=xn1sb[:, j * TPC + tt * TT:j * TPC + (tt + 1) * TT],
                                     start=(j == 0), stop=(j == NFC - 1))
                        for j in range(NFC):
                            ins = t.matmul(out=pU[:, :], lhsT=wuc[:, j * 128:(j + 1) * 128],
                                           rhs=xn1sb[:, j * TPC + tt * TT:j * TPC + (tt + 1) * TT],
                                           start=(j == 0), stop=(j == NFC - 1))
                        ph.inc(ins)
                    T(mmgu)
                    SC(lambda s, ic=ic: ph.inc(s.activation(out=gt[:, :], in_=pG[:, :],
                                                            func=AF.Silu,
                                                            scale=sgs[:, ic:ic + 1])))
                    V(lambda v, ic=ic: ph.inc(v.tensor_scalar_mul(out=ut[:, :], in0=pU[:, :],
                                                                  scalar1=sus[:, ic:ic + 1])))
                    V(lambda v, ic=ic: ph.inc(v.tensor_mul(
                        out=hst[:, ic * TT:(ic + 1) * TT], in0=gt[:, :], in1=ut[:, :])))
                for jo in range(NFC):
                    def lddn(g, jo=jo):
                        ph.inc(g.dma_start(out=wdc[:, :].rearrange("p (j n) -> p j n", n=128),
                                           in_=wd8[:, jo * 128:(jo + 1) * 128]
                                           .rearrange("(j p) n -> p j n", p=128)), 16)
                    GQ(lddn)
                    def mmdn(t):
                        for ic in range(NIC):
                            ins = t.matmul(out=pM[:, :], lhsT=wdc[:, ic * 128:(ic + 1) * 128],
                                           rhs=hst[:, ic * TT:(ic + 1) * TT],
                                           start=(ic == 0), stop=(ic == NIC - 1))
                        ph.inc(ins)
                    T(mmdn)
                    def ldx1(sy, tt=tt, jo=jo):
                        ph.inc(sy.dma_start(out=x2b[:, :],
                                            in_=x1s[jo * 128:(jo + 1) * 128,
                                                    tt * TT:(tt + 1) * TT]), 16)
                        ph.inc(sy.dma_start(out=at2b[:, :],
                                            in_=attn2_sl[jo * 128:(jo + 1) * 128,
                                                         tt * TT:(tt + 1) * TT]), 16)
                    SY(ldx1)
                    V(lambda v, jo=jo: ph.inc(v.tensor_scalar_add(
                        out=f32u[:, :], in0=at2b[:, :], scalar1=bos[:, jo:jo + 1])))
                    def fin(v, jo=jo):
                        v.tensor_scalar_mul(out=f32t[:, :], in0=pM[:, :],
                                            scalar1=sds[:, jo:jo + 1])
                        v.engine_nop()
                        v.tensor_add(out=x2b[:, :], in0=x2b[:, :], in1=f32u[:, :])
                        v.engine_nop()
                        ph.inc(v.tensor_add(out=x2b[:, :], in0=x2b[:, :], in1=f32t[:, :]))
                    V(fin)
                    def stx2(g, tt=tt, jo=jo):
                        ph.inc(g.dma_start(out=x2o[jo * 128:(jo + 1) * 128,
                                                   tt * TT:(tt + 1) * TT],
                                           in_=x2b[:, :]), 16)
                    GQ(stx2)

        @block.gpsimd
        def _(g):
            ph.wait(g)



def fused_inmap(core, ins):
    c = core
    x = ins['L0_attn_hidden_states'].reshape(TOK, D)
    x_fm = np.ascontiguousarray(np.asarray(x, np.float32).T)
    m = {"x_sl": np.ascontiguousarray(x_fm[:, c * TPC:(c + 1) * TPC])}
    m0 = l0attn_inmap(c, None, ins)
    m0.pop("xn", None)
    m.update(m0)
    m["owp"] = np.ascontiguousarray(ins["L0_attn_o_w"][256 * c:256 * (c + 1), :])
    m["onw"] = np.ascontiguousarray(ins["L0_attn_o_norm_w"][:, None])
    m1 = mid_inmap(c, None, None, None, ins)
    for k in ("ofm", "xns", "xs", "ow", "onw"):
        m1.pop(k, None)
    m1["identf_m"] = m1.pop("identf")
    m.update(m1)
    m2 = l1attn_inmap(c, None, ins)
    m2.pop("xn1", None)
    m2["wq1"] = m2.pop("wq"); m2["wk1"] = m2.pop("wk"); m2["wv1"] = m2.pop("wv")
    m.update(m2)
    m["wop"] = np.ascontiguousarray(ins["L1_attn_Wo"][256 * c:256 * (c + 1), :])
    m3 = fin_inmap(c, None, None, None, ins)
    for k in ("ctx", "xn1s", "x1s", "wo"):
        m3.pop(k, None)
    m.update(m3)
    return m


def kernel_fused(upto=99, **inputs):
    nc = build_fused(upto=upto)
    maps = [fused_inmap(c, inputs) for c in range(NCORES)]
    res = run_bass_kernel_spmd(nc, maps, core_ids=list(range(NCORES)))
    if upto < 8:
        return [np.asarray(r["dbgo"]) for r in res.results]
    x2_fm = np.concatenate([np.asarray(r["x2_sl"]) for r in res.results], axis=1)
    return np.ascontiguousarray(x2_fm.T).reshape(4, 4096, D)



def kernel(**inputs):
    nc = build_fused(upto=8)
    maps = [fused_inmap(c, inputs) for c in range(NCORES)]
    res = run_bass_kernel_spmd(nc, maps, core_ids=list(range(NCORES)))
    x2_fm = np.concatenate([np.asarray(r["x2_sl"]) for r in res.results], axis=1)
    return np.ascontiguousarray(x2_fm.T).reshape(4, 4096, D)


# version 06
# speedup vs baseline: 1.6775x; 1.1222x over previous
import numpy as np
import ml_dtypes
from contextlib import ExitStack
from types import SimpleNamespace
import concourse.bass as bass
import concourse.mybir as mybir
from concourse.bass_utils import run_bass_kernel_spmd

F32 = mybir.dt.float32
BF16 = mybir.dt.bfloat16
I8 = mybir.dt.int8
AF = mybir.ActivationFunctionType
ALU = mybir.AluOpType
BF16NP = ml_dtypes.bfloat16

NCORES = 8
D = 2048
TOK = 16384
TPC = TOK // NCORES
NFC = D // 128
CLAMP = 21.0


class Ctr:
    def __init__(self, handle):
        self.h = handle
        self.n = 0

    def inc(self, instr, k=1):
        self.n += k
        instr.then_inc(self.h, k)
        return instr

    def wait(self, eng, n=None):
        eng.wait_ge(self.h, self.n if n is None else n)



def l0attn_consts():
    GT = 512
    col = np.arange(GT)
    c = {}
    for m in (8, 16, 32, 64, 128):
        c[f"rm{m}"] = np.broadcast_to((col % m != 0).astype(np.float32), (128, GT)).copy()
    c["evn8"] = np.broadcast_to(((col // 8) % 2 == 0).astype(BF16NP), (128, GT)).copy()
    c["odd8"] = np.broadcast_to(((col // 8) % 2 == 1).astype(BF16NP), (128, GT)).copy()
    s = np.arange(128)[:, None]; t = np.arange(128)[None, :]
    c["maskAb"] = np.where(s <= t, -1, 0).astype(np.int32)
    c["maskMb"] = np.where(s < t, -1, 0).astype(np.int32)
    c["identb"] = np.eye(128, dtype=BF16NP)
    c["identf"] = np.eye(128, dtype=np.float32)
    return c


def l0attn_inmap(core, xn_fm_bf16, ins):
    c = core
    hs = slice(256 * c, 256 * c + 256)
    CST = l0attn_consts()
    cwq = ins["L0_attn_q_conv_w"][hs].reshape(2, 128, 4)
    cwk = ins["L0_attn_k_conv_w"][hs].reshape(2, 128, 4)
    cwv = ins["L0_attn_v_conv_w"][hs].reshape(2, 128, 4)
    cw = np.concatenate([cwq[0], cwq[1], cwk[0], cwk[1], cwv[0], cwv[1]],
                        axis=1)
    m = {
        "wq": np.ascontiguousarray(ins["L0_attn_q_w"][:, hs]),
        "wk": np.ascontiguousarray(ins["L0_attn_k_w"][:, hs]),
        "wv": np.ascontiguousarray(ins["L0_attn_v_w"][:, hs]),
        "cw": np.ascontiguousarray(cw),
        "fw1": ins["L0_attn_f_w1"],
        "fw2": np.ascontiguousarray(ins["L0_attn_f_w2"][:, hs]),
        "gw1": ins["L0_attn_g_w1"],
        "gw2": np.ascontiguousarray(ins["L0_attn_g_w2"][:, hs]),
        "gb2": np.ascontiguousarray(ins["L0_attn_g_b2"][hs][None, :]),
        "bw": np.ascontiguousarray(ins["L0_attn_b_w"][:, 2 * c:2 * c + 2]),
        "alog": np.ascontiguousarray(ins["L0_attn_A_log"][2 * c:2 * c + 2][None, :]),
        "dtb": np.ascontiguousarray(ins["L0_attn_dt_bias"].reshape(16, 128)
                                    [2 * c:2 * c + 2].T),
    }
    for k, v in CST.items():
        m[k] = v
    return m


def mid_inmap(core, ofeat_full_bf16, xn_fm_bf16, x_fm_f32, ins):
    c = core
    sl = slice(TPC * c, TPC * (c + 1))
    CST = l0attn_consts()
    m_ = {}
    if ofeat_full_bf16 is not None:
        m_["ofm"] = np.ascontiguousarray(ofeat_full_bf16.T[:, sl])
        m_["xns"] = np.ascontiguousarray(xn_fm_bf16[:, sl])
        m_["xs"] = np.ascontiguousarray(x_fm_f32[:, sl])
    return {
        **m_,
        "onw": np.ascontiguousarray(ins["L0_attn_o_norm_w"][:, None]),
        "ow": ins["L0_attn_o_w"],
        "rout": ins["L0_ffn_router"],
        "wug": ins["L0_ffn_w_up_gate"],
        "wdn": ins["L0_ffn_w_down"],
        "swu": ins["L0_ffn_shared_w_up_gate"],
        "swd": ins["L0_ffn_shared_w_down"],
        "identf": CST["identf"],
        "esel": np.ascontiguousarray(
            np.concatenate([np.tile((np.arange(8) == e).astype(BF16NP)[:, None],
                                    (1, 128)) for e in range(8)], axis=1)),
    }


def l1attn_consts(core):
    S = 4096
    t = np.arange(S, dtype=np.float32)
    inv = 1.0 / 10000.0 ** (np.arange(0, 64, 2) / 64.0)
    fr = t[:, None] * inv[None, :]
    cos = np.cos(fr).astype(np.float32)
    sin = np.sin(fr).astype(np.float32)
    cos_t = np.ascontiguousarray(cos.reshape(32, 128, 32).transpose(1, 0, 2).reshape(128, 32 * 32))
    sin_t = np.ascontiguousarray(sin.reshape(32, 128, 32).transpose(1, 0, 2).reshape(128, 32 * 32))
    slopes = 2.0 ** (-(np.arange(1, 33) * 8.0 / 32))
    qaug = np.zeros((2, 4 * S), np.float32)
    for h in range(4):
        sl = slopes[4 * core + h]
        qaug[0, h * S:(h + 1) * S] = -sl * t
        qaug[1, h * S:(h + 1) * S] = sl
    kpos = np.stack([np.ones(S, np.float32), t]).astype(np.float32)
    sq = np.arange(128)[:, None]; tq = np.arange(128)[None, :]
    causal = (sq <= tq).astype(BF16NP)
    return {"cos_t": cos_t, "sin_t": sin_t, "qaug": qaug, "kpos": kpos, "causal": causal}


def l1attn_inmap(core, xn1_fm_bf16, ins):
    c = core
    CST = l1attn_consts(c)
    m = {
        "wq": np.ascontiguousarray(ins["L1_attn_Wq"][:, 256 * c:256 * (c + 1)]),
        "wk": np.ascontiguousarray(ins["L1_attn_Wk"][:, 64 * c:64 * (c + 1)]),
        "wv": np.ascontiguousarray(ins["L1_attn_Wv"][:, 64 * c:64 * (c + 1)]),
    }
    m.update(CST)
    if xn1_fm_bf16 is not None:
        m["xn1"] = xn1_fm_bf16
    return m


def fin_inmap(core, ctx_fm_bf16, xn1_fm_bf16, x1_fm_f32, ins):
    c = core
    sl = slice(TPC * c, TPC * (c + 1))
    m_ = {}
    if ctx_fm_bf16 is not None:
        m_["ctx"] = np.ascontiguousarray(ctx_fm_bf16[:, sl])
        m_["xn1s"] = np.ascontiguousarray(xn1_fm_bf16[:, sl])
        m_["x1s"] = np.ascontiguousarray(x1_fm_f32[:, sl])
    return {
        **m_,
        "wo": ins["L1_attn_Wo"],
        "bo": np.ascontiguousarray(ins["L1_attn_bo"].reshape(NFC, 128).T),
        "wg8": ins["L1_ffn_w_gate"],
        "wu8": ins["L1_ffn_w_up"],
        "wd8": ins["L1_ffn_w_down"],
        "sg": np.ascontiguousarray(ins["L1_ffn_s_gate"].reshape(64, 128).T),
        "su": np.ascontiguousarray(ins["L1_ffn_s_up"].reshape(64, 128).T),
        "sd": np.ascontiguousarray(ins["L1_ffn_s_down"].reshape(NFC, 128).T),
    }


I32F = mybir.dt.int32


def mkctx(nc, st, block, ph, pfx):
    C = SimpleNamespace(nc=nc, st=st, block=block, ph=ph, pfx=pfx)

    def sbuf(n, sh, dt=F32):
        return st.enter_context(nc.sbuf_tensor(pfx + n, sh, dt))

    def psum(n, sh, dt=F32):
        return st.enter_context(nc.psum_tensor(pfx + n, sh, dt))

    def T(fn):
        @block.tensor
        def _(t):
            ph.wait(t)
            fn(t)

    def V(fn):
        @block.vector
        def _(v):
            ph.wait(v)
            fn(v)

    def SC(fn):
        @block.scalar
        def _(s):
            ph.wait(s)
            fn(s)

    def SY(fn):
        @block.sync
        def _(sy):
            ph.wait(sy)
            fn(sy)

    def GQ(fn):
        @block.gpsimd
        def _(g):
            ph.wait(g)
            fn(g)

    C.sbuf, C.psum = sbuf, psum
    C.T, C.V, C.SC, C.SY, C.GQ = T, V, SC, SY, GQ
    return C


def emit_coll(C, kind, in_ap, out_ap):
    ph = C.ph
    op = ALU.add if kind == "ReduceScatter" else ALU.bypass

    @C.block.gpsimd
    def _(g):
        ph.wait(g)
        ins = g.collective_compute(
            kind, op,
            replica_groups=[list(range(NCORES))],
            ins=[in_ap.opt()], outs=[out_ap.opt()])
        ph.inc(ins, 1)


def emit_ln(C, x_in, xn_out, eps=1e-5):
    TT = 512
    NT = TPC // TT
    nc, ph = C.nc, C.ph
    with ExitStack() as st2:
        def sb(n, sh, dt=F32):
            return st2.enter_context(nc.sbuf_tensor(C.pfx + n, sh, dt))
        xb = sb("xb", [128, NFC * TT])
        sqf = sb("sqf", [128, TT])
        onecol = sb("onecol", [128, 1])
        onesc = sb("onesc", [1, 128])
        epsc = sb("epsc", [1, 1])
        statm = sb("statm", [1, 6 * TT])
        xnb = sb("xnb", [128, TT], BF16)
        pS = st2.enter_context(nc.psum_tensor(C.pfx + "pS", [1, TT], F32))
        pQ = st2.enter_context(nc.psum_tensor(C.pfx + "pQ", [1, TT], F32))
        pM = st2.enter_context(nc.psum_tensor(C.pfx + "pM", [128, TT], F32))
        pR = st2.enter_context(nc.psum_tensor(C.pfx + "pR", [128, TT], F32))

        def setup(v):
            v.memset(onecol[:, :], 1.0)
            v.memset(onesc[:, :], 1.0)
            ph.inc(v.memset(epsc[:, :], eps))
        C.V(setup)
        for tt in range(NT):
            def ldx(sy, tt=tt):
                for j in range(NFC):
                    ph.inc(sy.dma_start(out=xb[:, j * TT:(j + 1) * TT],
                                        in_=x_in[j * 128:(j + 1) * 128,
                                                 tt * TT:(tt + 1) * TT]), 16)
            C.SY(ldx)
            def mmsum(t):
                for j in range(NFC):
                    ins = t.matmul(out=pS[:, :], lhsT=onecol[:, 0:1],
                                   rhs=xb[:, j * TT:(j + 1) * TT],
                                   start=(j == 0), stop=(j == NFC - 1))
                ph.inc(ins)
            C.T(mmsum)
            for j in range(NFC):
                C.SC(lambda s, j=j: ph.inc(s.activation(out=sqf[:, :],
                                                        in_=xb[:, j * TT:(j + 1) * TT],
                                                        func=AF.Square)))
                C.T(lambda t, j=j: ph.inc(t.matmul(out=pQ[:, :], lhsT=onecol[:, 0:1],
                                                   rhs=sqf[:, :], start=(j == 0),
                                                   stop=(j == NFC - 1))))
            def stats(v):
                v.tensor_scalar_mul(out=statm[:, 0 * TT:1 * TT], in0=pS[:, :], scalar1=1.0 / D)
                v.tensor_scalar_mul(out=statm[:, 1 * TT:2 * TT], in0=pQ[:, :], scalar1=1.0 / D)
                v.tensor_mul(out=statm[:, 2 * TT:3 * TT], in0=statm[:, 0 * TT:1 * TT],
                             in1=statm[:, 0 * TT:1 * TT])
                ph.inc(v.tensor_sub(out=statm[:, 3 * TT:4 * TT], in0=statm[:, 1 * TT:2 * TT],
                                    in1=statm[:, 2 * TT:3 * TT]))
            C.V(stats)
            C.SC(lambda s: ph.inc(s.activation(out=statm[:, 4 * TT:5 * TT],
                                               in_=statm[:, 3 * TT:4 * TT],
                                               func=AF.Sqrt, bias=epsc[0:1, 0:1])))
            C.V(lambda v: ph.inc(v.reciprocal(out=statm[:, 5 * TT:6 * TT],
                                              in_=statm[:, 4 * TT:5 * TT])))
            def bcst(t):
                t.matmul(out=pM[:, :], lhsT=onesc[:, :], rhs=statm[:, 0 * TT:1 * TT],
                         start=True, stop=True)
                ph.inc(t.matmul(out=pR[:, :], lhsT=onesc[:, :],
                                rhs=statm[:, 5 * TT:6 * TT], start=True, stop=True))
            C.T(bcst)
            for j in range(NFC):
                def norm(v, j=j):
                    v.tensor_sub(out=sqf[:, :], in0=xb[:, j * TT:(j + 1) * TT], in1=pM[:, :])
                    ph.inc(v.tensor_mul(out=xnb[:, :], in0=sqf[:, :], in1=pR[:, :]))
                C.V(norm)
                def stxn(g, tt=tt, j=j):
                    ph.inc(g.dma_start(out=xn_out[j * 128:(j + 1) * 128,
                                                  tt * TT:(tt + 1) * TT],
                                       in_=xnb[:, :]), 16)
                C.GQ(stxn)


def build_fused(core_check=None, upto=99):
    nc = bass.Bass("TRN2", target_bir_lowering=False, debug=False)
    EI = lambda n, sh, dt=F32: nc.dram_tensor(n, sh, dt, kind="ExternalInput").ap()
    IN = lambda n, sh, dt=F32: nc.dram_tensor(n, sh, dt, kind="Internal").ap()
    SH = lambda n, sh, dt=F32: nc.dram_tensor(n, sh, dt, kind="Internal", addr_space="Shared").ap()
    x_sl = EI("x_sl", [D, TPC])
    xn_sl = IN("xn_sl", [D, TPC], BF16)
    xn_g = IN("xn_g", [NCORES * D, TPC], BF16)
    pattn = IN("pattn", [NCORES * D, TPC], BF16)
    attn_sl = IN("attn_sl", [D, TPC], BF16)
    x1_sl = IN("x1_sl", [D, TPC], F32)
    xn1_sl = IN("xn1_sl", [D, TPC], BF16)
    xn1_g = IN("xn1_g", [NCORES * D, TPC], BF16)
    pattn2 = IN("pattn2", [NCORES * D, TPC], BF16)
    attn2_sl = IN("attn2_sl", [D, TPC], BF16)
    x2_sl = nc.dram_tensor("x2_sl", [D, TPC], F32, kind="ExternalOutput").ap()

    with ExitStack() as st:
        ph = Ctr(st.enter_context(nc.semaphore("ph")))
        block = st.enter_context(nc.Block())
        st.enter_context(nc.allow_low_precision(
            reason="bf16 elementwise per problem statement (honest bf16 blessed)"))
        C = mkctx(nc, st, block, ph, "f_")

        emit_ln(C, x_sl, xn_sl)
        if upto >= 1:
            emit_coll(C, "AllGather", xn_sl, xn_g)
        if upto >= 2:
            emit_l0attn_f(C, nc, xn_g, pattn)
        if upto >= 3:
            emit_coll(C, "ReduceScatter", pattn, attn_sl)
        if upto >= 4:
            emit_mid_f(C, nc, attn_sl, xn_sl, x_sl, x1_sl, xn1_sl)
        if upto >= 5:
            emit_coll(C, "AllGather", xn1_sl, xn1_g)
        if upto >= 6:
            emit_l1attn_f(C, nc, xn1_g, pattn2)
        if upto >= 7:
            emit_coll(C, "ReduceScatter", pattn2, attn2_sl)
        if upto >= 8:
            emit_fin_f(C, nc, attn2_sl, xn1_sl, x1_sl, x2_sl)
        if upto < 8:
            fin_map = {0: xn_sl, 1: xn_g, 2: pattn, 3: attn_sl, 4: xn1_sl,
                       5: xn1_g, 6: pattn2, 7: attn2_sl}
            srcap = fin_map[upto]
            dbgo = nc.dram_tensor("dbgo", list(srcap.shape), srcap.dtype,
                                  kind="ExternalOutput").ap()

            @block.gpsimd
            def _(g):
                ph.wait(g)
                rows = srcap.shape[0]
                step = min(2048, rows)
                for r0 in range(0, rows, step):
                    ph.inc(g.dma_start(out=dbgo[r0:r0 + step, :],
                                       in_=srcap[r0:r0 + step, :]), 16)

        @block.gpsimd
        def _(g):
            ph.wait(g)
    return nc


def emit_l0attn_f(C, nc, xn_g, pattn):
    GT = 512
    NG = TOK // GT
    NCH = GT // 128
    wq = nc.dram_tensor("wq", [D, 256], F32, kind="ExternalInput").ap()
    wk = nc.dram_tensor("wk", [D, 256], F32, kind="ExternalInput").ap()
    wv = nc.dram_tensor("wv", [D, 256], F32, kind="ExternalInput").ap()
    cw = nc.dram_tensor("cw", [128, 24], F32, kind="ExternalInput").ap()
    fw1 = nc.dram_tensor("fw1", [D, 128], F32, kind="ExternalInput").ap()
    fw2 = nc.dram_tensor("fw2", [128, 256], F32, kind="ExternalInput").ap()
    gw1 = nc.dram_tensor("gw1", [D, 128], F32, kind="ExternalInput").ap()
    gw2 = nc.dram_tensor("gw2", [128, 256], F32, kind="ExternalInput").ap()
    gb2 = nc.dram_tensor("gb2", [1, 256], F32, kind="ExternalInput").ap()
    bw = nc.dram_tensor("bw", [D, 2], F32, kind="ExternalInput").ap()
    alog = nc.dram_tensor("alog", [1, 2], F32, kind="ExternalInput").ap()
    dtb = nc.dram_tensor("dtb", [128, 2], F32, kind="ExternalInput").ap()
    owp = nc.dram_tensor("owp", [256, D], F32, kind="ExternalInput").ap()
    onw = nc.dram_tensor("onw", [128, 1], F32, kind="ExternalInput").ap()
    CST = l0attn_consts()
    I32 = mybir.dt.int32
    cin = {}
    for k, v in CST.items():
        dt = BF16 if v.dtype == BF16NP else (I32 if v.dtype == np.int32 else F32)
        cin[k] = nc.dram_tensor(k, list(v.shape), dt, kind="ExternalInput").ap()

    with ExitStack() as st:
        def sbuf(n, sh, dt=F32):
            return st.enter_context(nc.sbuf_tensor("a_" + n, sh, dt))
        wqs = sbuf("wqs", [128, NFC * 256], BF16)
        wks = sbuf("wks", [128, NFC * 256], BF16)
        wvs = sbuf("wvs", [128, NFC * 256], BF16)
        fw1s = sbuf("fw1s", [128, NFC * 128], BF16)
        gw1s = sbuf("gw1s", [128, NFC * 128], BF16)
        fw2s = sbuf("fw2s", [128, 256], BF16)
        gw2s = sbuf("gw2s", [128, 256], BF16)
        gb2s = sbuf("gb2s", [1, 256], F32)
        bws = sbuf("bws", [128, NFC * 2], BF16)
        cws = sbuf("cws", [128, 24], F32)
        alogs = sbuf("alogs", [1, 2], F32)
        negA = sbuf("negA", [128, 2], F32)
        dts = sbuf("dts", [128, 2], F32)
        onesc = sbuf("onesc", [1, 128], F32)
        ones1b = sbuf("ones1b", [1, 128], BF16)
        onecol = sbuf("onecol", [128, 1], F32)
        onebias = sbuf("onebias", [128, 1], F32)
        epsl2 = sbuf("epsl2", [1, 1], F32)
        cs = {}
        for k, v in CST.items():
            dt = BF16 if v.dtype == BF16NP else (mybir.dt.int32 if v.dtype == np.int32 else F32)
            cs[k] = sbuf("c2_" + k, list(v.shape), dt)
        mskd = sbuf("mskd", [128, 128], mybir.dt.int32)
        xng = sbuf("xng0", [128, NFC * GT], BF16)
        raw = {nm: sbuf("raw" + nm, [128, GT + 3], BF16)
               for nm in ("q0", "q1", "k0", "k1", "v0", "v1")}
        qh = [sbuf(f"qh{h}", [128, GT], BF16) for h in range(2)]
        kh = [sbuf(f"kh{h}", [128, GT], BF16) for h in range(2)]
        vh = [sbuf(f"vh{h}", [128, GT], BF16) for h in range(2)]
        innf = sbuf("innf", [128, GT], BF16)
        inng = sbuf("inng", [128, GT], BF16)
        betac = sbuf("betac", [2, GT], F32)
        nsc = sbuf("nsc", [1, GT], F32)
        beta_f = sbuf("beta_f", [128, NCH * 2], F32)
        beta_fn = sbuf("beta_fn", [128, NCH * 2], F32)
        gate_tm = sbuf("gate_tm", [128, NCH * 256], BF16)
        Gf = sbuf("Gf", [128, GT], F32)
        RDf = sbuf("RDf", [128, GT], F32)
        SDf = sbuf("SDf", [128, GT], F32)
        ERf = sbuf("ERf", [128, GT], F32)
        gkf = sbuf("gkf", [128, GT], F32)
        sp1 = sbuf("sp1", [128, GT], F32)
        spb = sbuf("spb", [128, GT], F32)
        spc = sbuf("spc", [128, GT], BF16)
        spb2 = sbuf("spb2", [128, GT], BF16)
        mapn = ["EQ", "ED", "rowb", "colb", "r8", "c8", "r16", "c16", "r32", "c32", "r64", "c64"]
        maps = {k: sbuf("m_" + k, [128, GT], BF16) for k in mapn}
        C1 = sbuf("C1", [128, GT], BF16); C2 = sbuf("C2", [128, GT], BF16)
        C3 = sbuf("C3", [128, GT], BF16); C4 = sbuf("C4", [128, GT], BF16)
        C5 = sbuf("C5", [128, GT], BF16); C6 = sbuf("C6", [128, GT], BF16)
        RBe = sbuf("RBe", [128, 2 * GT], BF16); RBo = sbuf("RBo", [128, 2 * GT], BF16)
        R8 = sbuf("R8", [128, 2 * GT], BF16); R16 = sbuf("R16", [128, 2 * GT], BF16)
        R32 = sbuf("R32", [128, 2 * GT], BF16); R64 = sbuf("R64", [128, 2 * GT], BF16)
        Qs = sbuf("Qs", [128, GT], BF16); Ws = sbuf("Ws", [128, GT], BF16)
        KD = sbuf("KD", [128, GT], BF16)
        eGend = sbuf("eGend", [128, NCH], F32)
        Aqt4 = sbuf("Aqt4", [128, NCH * 128], BF16)
        Ntn4 = sbuf("Ntn4", [128, NCH * 128], BF16)
        powT4 = [sbuf(f"powT4_{i}", [128, NCH * 128], BF16) for i in range(6)]
        pown4 = [sbuf(f"pown4_{i}", [128, NCH * 128], BF16) for i in range(6)]
        ybuf4 = [sbuf(f"ybuf4_{i}", [128, NCH * 256], BF16) for i in range(2)]
        rhsb4 = sbuf("rhsb4", [128, NCH * 256], BF16)
        S_f = [sbuf(f"S_f{h}", [128, 128], F32) for h in range(2)]
        S_b = [sbuf(f"S_b{h}", [128, 128], BF16) for h in range(2)]
        UWc = sbuf("UWc", [128, 128], BF16)
        KDt = sbuf("KDt", [128, 128], BF16)
        xdel = sbuf("xdel", [128, 128], BF16)
        useq = sbuf("useq", [128, 128], BF16)
        ssq = sbuf("ssq", [128, 12], F32)
        obuf = sbuf("obuf", [128, NCH * 256], BF16)
        ows2 = sbuf("ows2", [128, 2 * D], BF16)
        ofcm = sbuf("ofcm", [128, 2 * GT], BF16)
        patb2 = sbuf("patb2", [128, 2 * GT], BF16)
        onws = sbuf("onws", [128, 1], F32)
        pA = st.enter_context(nc.psum_tensor("a_pA", [128, GT], F32))
        pC = st.enter_context(nc.psum_tensor("a_pC", [2, GT], F32))
        pScoreB = st.enter_context(nc.psum_tensor("a_pScoreB", [128, NCH * 256], F32))
        pStageB = st.enter_context(nc.psum_tensor("a_pStageB", [16, NCH * 32], F32))
        pT4 = st.enter_context(nc.psum_tensor("a_pT4", [128, NCH * 128], BF16))
        pSolve = st.enter_context(nc.psum_tensor("a_pSolve", [128, NCH * 256], F32))
        ph = C.ph
        block = C.block

        def T(fn):
            @block.tensor
            def _(t):
                ph.wait(t)
                fn(t)
        def V(fn):
            @block.vector
            def _(v):
                ph.wait(v)
                fn(v)
        def SC(fn):
            @block.scalar
            def _(s):
                ph.wait(s)
                fn(s)
        def G3(fn):
            @block.gpsimd
            def _(g):
                ph.wait(g)
                fn(g)

        @block.gpsimd
        def _(g):
            ph.wait(g)
            for w_dram, w_sb, nch in ((wq, wqs, 256), (wk, wks, 256), (wv, wvs, 256),
                                      (fw1, fw1s, 128), (gw1, gw1s, 128), (bw, bws, 2)):
                src = w_dram.rearrange("(j p) n -> p j n", p=128)
                dst = w_sb[:, :].rearrange("p (j n) -> p j n", n=nch)
                ph.inc(g.dma_start(out=dst, in_=src), 16)
            for w_dram, w_sb in ((fw2, fw2s), (gw2, gw2s), (cw, cws)):
                ph.inc(g.dma_start(out=w_sb[:, :], in_=w_dram), 16)
            srcw = owp.rearrange("(j p) n -> p j n", p=128)
            dstw = ows2[:, :].rearrange("p (j n) -> p j n", n=D)
            ph.inc(g.dma_start(out=dstw, in_=srcw), 16)

        @block.sync
        def _(sy):
            ph.wait(sy)
            ph.inc(sy.dma_start(out=gb2s[:, :], in_=gb2), 16)
            ph.inc(sy.dma_start(out=alogs[:, :], in_=alog), 16)
            ph.inc(sy.dma_start(out=dts[:, :], in_=dtb), 16)
            ph.inc(sy.dma_start(out=onws[:, :], in_=onw), 16)
            for k in CST:
                ph.inc(sy.dma_start(out=cs[k][:, :], in_=cin[k]), 16)

        def setup1(v):
            v.memset(onesc[:, :], 1.0)
            v.memset(ones1b[:, :], 1.0)
            v.memset(onecol[:, :], 1.0)
            v.memset(onebias[:, :], 1.0)
            v.memset(epsl2[:, :], 1e-6)
            for h in range(2):
                v.memset(S_f[h][:, :], 0.0)
                v.memset(S_b[h][:, :], 0.0)
            ph.inc(v.engine_nop())
        V(setup1)
        SC(lambda s: ph.inc(s.activation(out=alogs[:, :], in_=alogs[:, :], func=AF.Exp)))
        V(lambda v: ph.inc(v.tensor_scalar_mul(out=alogs[:, :], in0=alogs[:, :], scalar1=-1.0)))
        T(lambda t: ph.inc(t.matmul(out=pA[:, 0:2], lhsT=onesc[:, :], rhs=alogs[:, :],
                                    start=True, stop=True)))
        V(lambda v: ph.inc(v.tensor_copy(out=negA[:, :], in_=pA[:, 0:2])))

        for g in range(NG):
            bstart = (g % max(NG // 4, 8) == 0)
            xbuf = xng

            def loadxn(sy, g=g, xbuf=xbuf):
                rk = (g * GT) // TPC
                cb = (g * GT) % TPC
                for j in range(NFC):
                    ins = sy.dma_start(out=xbuf[:, j * GT:(j + 1) * GT],
                                       in_=xn_g[rk * D + j * 128:rk * D + (j + 1) * 128,
                                                cb:cb + GT])
                    ph.inc(ins, 16)
            @block.sync
            def _(sy):
                ph.wait(sy)
                loadxn(sy)

            for pi, wsb in enumerate((wqs, wks, wvs)):
                for ht in range(2):
                    nm = ("q0q1", "k0k1", "v0v1")[pi][ht * 2:ht * 2 + 2]
                    rb = raw[nm]
                    def mmproj(t, wsb=wsb, ht=ht, xbuf=xbuf):
                        for j in range(NFC):
                            ins = t.matmul(out=pA[:, :],
                                           lhsT=wsb[:, j * 256 + ht * 128:j * 256 + ht * 128 + 128],
                                           rhs=xbuf[:, j * GT:(j + 1) * GT],
                                           start=(j == 0), stop=(j == NFC - 1))
                        ph.inc(ins)
                    T(mmproj)
                    def convf(v, rb=rb, pi=pi, ht=ht, bstart=bstart):
                        v.tensor_copy(out=rb[:, 3:3 + GT], in_=pA[:, :])
                        if bstart:
                            v.memset(rb[:, 0:3], 0.0)
                        v.engine_nop()
                        ci = (pi * 2 + ht) * 4
                        v.tensor_scalar(out=spc[:, :], in0=rb[:, 0:GT],
                                        scalar1=cws[:, ci:ci + 1], scalar2=None, op0=ALU.mult)
                        for i in range(1, 4):
                            v.scalar_tensor_tensor(out=spc[:, :], in0=rb[:, i:i + GT],
                                                   scalar=cws[:, ci + i:ci + i + 1],
                                                   in1=spc[:, :], op0=ALU.mult, op1=ALU.add)
                        ph.inc(v.tensor_copy(out=rb[:, 0:3], in_=rb[:, GT:GT + 3]))
                    V(convf)
                    dst = (qh, kh, vh)[pi][ht]
                    SC(lambda s: ph.inc(s.activation(out=sp1[:, :], in_=spc[:, :],
                                                     func=AF.Tanh, scale=0.5)))
                    def silu_v(v, dst=dst):
                        v.tensor_scalar(out=spb2[:, :], in0=sp1[:, :],
                                        scalar1=0.5, scalar2=0.5, op0=ALU.mult, op1=ALU.add)
                        v.engine_nop()
                        ph.inc(v.tensor_mul(out=dst[:, :], in0=spc[:, :], in1=spb2[:, :]))
                    V(silu_v)

            for pi in range(2):
                for ht in range(2):
                    src = (qh, kh)[pi][ht]
                    SC(lambda s, src=src: ph.inc(s.activation(out=sp1[:, :], in_=src[:, :],
                                                              func=AF.Square)))
                    T(lambda t: ph.inc(t.matmul(out=pC[0:1, :], lhsT=onecol[:, 0:1],
                                                rhs=sp1[:, :], start=True, stop=True)))
                    SC(lambda s: ph.inc(s.activation(out=nsc[0:1, :], in_=pC[0:1, :],
                                                     func=AF.Sqrt, bias=epsl2[0:1, 0:1])))
                    V(lambda v: ph.inc(v.reciprocal(out=nsc[0:1, :], in_=nsc[0:1, :])))
                    T(lambda t: ph.inc(t.matmul(out=pA[:, :], lhsT=onesc[:, :],
                                                rhs=nsc[0:1, :], start=True, stop=True)))
                    scl = (128.0 ** -0.5) if pi == 0 else 1.0
                    def l2fin(v, src=src, scl=scl):
                        v.tensor_scalar_mul(out=spb2[:, :], in0=pA[:, :], scalar1=scl)
                        v.engine_nop()
                        ph.inc(v.tensor_mul(out=src[:, :], in0=src[:, :], in1=spb2[:, :]))
                    V(l2fin)

            for wsb, dst in ((fw1s, innf), (gw1s, inng)):
                def mminner(t, wsb=wsb, xbuf=xbuf):
                    for j in range(NFC):
                        ins = t.matmul(out=pA[:, :], lhsT=wsb[:, j * 128:(j + 1) * 128],
                                       rhs=xbuf[:, j * GT:(j + 1) * GT],
                                       start=(j == 0), stop=(j == NFC - 1))
                    ph.inc(ins)
                T(mminner)
                V(lambda v, dst=dst: ph.inc(v.tensor_copy(out=dst[:, :], in_=pA[:, :])))

            def mmbeta(t, xbuf=xbuf):
                for j in range(NFC):
                    ins = t.matmul(out=pC[:, :], lhsT=bws[:, j * 2:(j + 1) * 2],
                                   rhs=xbuf[:, j * GT:(j + 1) * GT],
                                   start=(j == 0), stop=(j == NFC - 1))
                ph.inc(ins)
            T(mmbeta)
            SC(lambda s: ph.inc(s.activation(out=betac[:, :], in_=pC[:, :],
                                             func=AF.Tanh, scale=0.5)))
            V(lambda v: ph.inc(v.tensor_scalar(out=betac[:, :], in0=betac[:, :],
                                               scalar1=0.5, scalar2=0.5,
                                               op0=ALU.mult, op1=ALU.add)))
            def trbeta(t):
                for ch in range(NCH):
                    ins = t.transpose(out=pSolve[:, ch * 256:ch * 256 + 2],
                                      in_=betac[:, ch * 128:(ch + 1) * 128],
                                      identity=cs["identf"][0:2, 0:2])
                ph.inc(ins)
            T(trbeta)
            def betaev(v):
                for ch in range(NCH):
                    v.tensor_copy(out=beta_f[:, ch * 2:(ch + 1) * 2],
                                  in_=pSolve[:, ch * 256:ch * 256 + 2])
                for ch in range(NCH):
                    ins = v.tensor_scalar_mul(out=beta_fn[:, ch * 2:(ch + 1) * 2],
                                              in0=pSolve[:, ch * 256:ch * 256 + 2],
                                              scalar1=-1.0)
                ph.inc(ins)
            V(betaev)

            for cp in range(NCH // 2):
                def mmgate(t, cp=cp):
                    for q in range(2):
                        ch = cp * 2 + q
                        t.matmul(out=pA[:, q * 256:q * 256 + 256],
                                 lhsT=inng[:, ch * 128:(ch + 1) * 128],
                                 rhs=gw2s[:, :], start=True, stop=False)
                        ins = t.matmul(out=pA[:, q * 256:q * 256 + 256], lhsT=onesc[:, :],
                                       rhs=gb2s[:, :], start=False, stop=True,
                                       skip_group_check=True)
                    ph.inc(ins)
                T(mmgate)
                SC(lambda s: ph.inc(s.activation(out=sp1[:, :], in_=pA[:, :],
                                                 func=AF.Tanh, scale=0.5)))
                def gatefin(v, cp=cp):
                    ph.inc(v.tensor_scalar(out=gate_tm[:, cp * 512:(cp + 1) * 512],
                                           in0=sp1[:, :], scalar1=0.5, scalar2=0.5,
                                           op0=ALU.mult, op1=ALU.add))
                V(gatefin)

            for ht in range(2):
                T(lambda t, ht=ht: ph.inc(t.matmul(out=pA[:, :],
                                                   lhsT=fw2s[:, ht * 128:(ht + 1) * 128],
                                                   rhs=innf[:, :], start=True, stop=True)))
                V(lambda v, ht=ht: ph.inc(v.tensor_scalar_add(out=sp1[:, :], in0=pA[:, :],
                                                              scalar1=dts[:, ht:ht + 1])))
                SC(lambda s: ph.inc(s.activation(out=gkf[:, :], in_=sp1[:, :], func=AF.Exp)))
                SC(lambda s: ph.inc(s.activation(out=sp1[:, :], in_=gkf[:, :], func=AF.Ln,
                                                 bias=onebias[:, 0:1])))
                V(lambda v, ht=ht: ph.inc(v.tensor_scalar(out=gkf[:, :], in0=sp1[:, :],
                                                          scalar1=negA[:, ht:ht + 1],
                                                          scalar2=-CLAMP,
                                                          op0=ALU.mult, op1=ALU.max)))

                def scan_map(rm, dst):
                    V(lambda v, rm=rm, dst=dst: ph.inc(
                        v.tensor_tensor_scan(out=dst[:, :], data0=cs[rm][:, :],
                                             data1=gkf[:, :], initial=0.0,
                                             op0=ALU.mult, op1=ALU.add)))
                def sufd(v, src, dst, last, step):
                    nb = GT // step
                    gath = src[:, last::step].unsqueeze(2).broadcast_to([128, nb, step])
                    s3 = src[:, :].rearrange("p (a b) -> p a b", b=step)
                    d3 = dst[:, :].rearrange("p (a b) -> p a b", b=step)
                    return v.tensor_sub(out=d3, in0=gath, in1=s3)
                def expm(s, src, dst, scale=1.0):
                    return s.activation(out=maps[dst][:, :], in_=src[:, :], func=AF.Exp,
                                        scale=scale)

                scan_map("rm128", Gf)
                V(lambda v: ph.inc(sufd(v, Gf, SDf, 127, 128)))
                def exps128(s):
                    expm(s, Gf, "EQ")
                    expm(s, SDf, "ED")
                    ph.inc(s.activation(out=eGend[:, 0:NCH], in_=Gf[:, 127::128],
                                        func=AF.Exp))
                SC(exps128)
                scan_map("rm8", RDf)
                def d8(v):
                    sufd(v, RDf, SDf, 7, 8)
                    gath = RDf[:, 3::8].unsqueeze(2).broadcast_to([128, GT // 8, 8])
                    s3 = RDf[:, :].rearrange("p (a b) -> p a b", b=8)
                    d3 = ERf[:, :].rearrange("p (a b) -> p a b", b=8)
                    ph.inc(v.tensor_sub(out=d3, in0=s3, in1=gath))
                V(d8)
                def exps8(s):
                    expm(s, RDf, "r8")
                    expm(s, SDf, "c8")
                    expm(s, ERf, "rowb")
                    ph.inc(expm(s, ERf, "colb", scale=-1.0))
                SC(exps8)
                for m, rn, cn in ((16, "r16", "c16"), (32, "r32", "c32"), (64, "r64", "c64")):
                    scan_map(f"rm{m}", RDf)
                    V(lambda v, m=m: ph.inc(sufd(v, RDf, SDf, m - 1, m)))
                    def expsm(s, rn=rn, cn=cn):
                        expm(s, RDf, rn)
                        ph.inc(expm(s, SDf, cn))
                    SC(expsm)

                def optiles(v, ht=ht):
                    q = qh[ht]; k = kh[ht]
                    v.tensor_mul(out=Qs[:, :], in0=q[:, :], in1=maps["EQ"][:, :])
                    v.tensor_mul(out=Ws[:, :], in0=k[:, :], in1=maps["EQ"][:, :])
                    v.tensor_mul(out=KD[:, :], in0=k[:, :], in1=maps["ED"][:, :])
                    v.tensor_mul(out=C1[:, :], in0=k[:, :], in1=maps["colb"][:, :])
                    v.tensor_mul(out=C2[:, :], in0=C1[:, :], in1=cs["odd8"][:, :])
                    v.tensor_mul(out=C1[:, :], in0=C1[:, :], in1=cs["evn8"][:, :])
                    v.tensor_mul(out=RBe[:, 0:GT], in0=q[:, :], in1=maps["rowb"][:, :])
                    v.tensor_mul(out=RBe[:, GT:2 * GT], in0=k[:, :], in1=maps["rowb"][:, :])
                    v.tensor_mul(out=RBo[:, 0:GT], in0=RBe[:, 0:GT], in1=cs["odd8"][:, :])
                    v.tensor_mul(out=RBo[:, GT:2 * GT], in0=RBe[:, GT:2 * GT], in1=cs["odd8"][:, :])
                    v.tensor_mul(out=RBe[:, 0:GT], in0=RBe[:, 0:GT], in1=cs["evn8"][:, :])
                    v.tensor_mul(out=RBe[:, GT:2 * GT], in0=RBe[:, GT:2 * GT], in1=cs["evn8"][:, :])
                    v.tensor_mul(out=C3[:, :], in0=k[:, :], in1=maps["c8"][:, :])
                    v.tensor_mul(out=C3[:, :], in0=C3[:, :], in1=cs["evn8"][:, :])
                    v.tensor_mul(out=R8[:, 0:GT], in0=q[:, :], in1=maps["r8"][:, :])
                    v.tensor_mul(out=R8[:, GT:2 * GT], in0=k[:, :], in1=maps["r8"][:, :])
                    v.tensor_mul(out=R8[:, 0:GT], in0=R8[:, 0:GT], in1=cs["odd8"][:, :])
                    v.tensor_mul(out=R8[:, GT:2 * GT], in0=R8[:, GT:2 * GT], in1=cs["odd8"][:, :])
                    for Cx, Rx, rn, cn in ((C4, R16, "r16", "c16"), (C5, R32, "r32", "c32"),
                                           (C6, R64, "r64", "c64")):
                        v.tensor_mul(out=Cx[:, :], in0=k[:, :], in1=maps[cn][:, :])
                        v.tensor_mul(out=Rx[:, 0:GT], in0=q[:, :], in1=maps[rn][:, :])
                        v.tensor_mul(out=Rx[:, GT:2 * GT], in0=k[:, :], in1=maps[rn][:, :])
                    ph.inc(v.engine_nop())
                V(optiles)

                def scoresb(t):
                    for ch in range(NCH):
                        c0 = ch * 128
                        def rhs2(Rt, lo, hi):
                            r3 = Rt[:, :].rearrange("p (h t) -> p h t", h=2)
                            return r3[:, :, c0 + lo:c0 + hi]
                        ps3 = pScoreB[:, ch * 256:(ch + 1) * 256].rearrange(
                            "p (h t) -> p h t", h=2)
                        t.matmul(out=ps3[:, :, :], lhsT=C1[:, c0:c0 + 128],
                                 rhs=rhs2(RBe, 0, 128), start=True, stop=False)
                        t.matmul(out=ps3[:, :, :], lhsT=C2[:, c0:c0 + 128],
                                 rhs=rhs2(RBo, 0, 128), start=False, stop=False)
                        t.matmul(out=ps3[:, :, :], lhsT=C3[:, c0:c0 + 128],
                                 rhs=rhs2(R8, 0, 128), start=False, stop=True)
                        for p in range(3):
                            t.matmul(out=ps3[32 * p:32 * p + 16, :, 32 * p + 16:32 * p + 32],
                                     lhsT=C4[:, c0 + 32 * p:c0 + 32 * p + 16],
                                     rhs=rhs2(R16, 32 * p + 16, 32 * p + 32),
                                     start=True, stop=True)
                        st3 = pStageB[:, ch * 32:(ch + 1) * 32].rearrange(
                            "p (h t) -> p h t", h=2)
                        t.matmul(out=st3[0:16, :, 0:16], lhsT=C4[:, c0 + 96:c0 + 112],
                                 rhs=rhs2(R16, 112, 128), start=True, stop=True)
                        for p in range(2):
                            t.matmul(out=ps3[64 * p:64 * p + 32, :, 64 * p + 32:64 * p + 64],
                                     lhsT=C5[:, c0 + 64 * p:c0 + 64 * p + 32],
                                     rhs=rhs2(R32, 64 * p + 32, 64 * p + 64),
                                     start=True, stop=True)
                        ins = t.matmul(out=ps3[0:64, :, 64:128], lhsT=C6[:, c0:c0 + 64],
                                       rhs=rhs2(R64, 64, 128), start=True, stop=True)
                    ph.inc(ins)
                T(scoresb)

                def evacb(v, ht=ht):
                    for ch in range(NCH):
                        cA = ch * 128
                        v.tensor_tensor(out=mskd[:, :],
                                        in0=pScoreB[:, ch * 256:ch * 256 + 128]
                                        .bitcast(mybir.dt.int32),
                                        in1=cs["maskAb"][:, :], op=ALU.bitwise_and)
                        v.engine_nop()
                        v.tensor_copy(out=Aqt4[:, cA:cA + 128], in_=mskd[:, :].bitcast(F32))
                        v.tensor_tensor(out=mskd[:, :],
                                        in0=pScoreB[:, ch * 256 + 128:ch * 256 + 256]
                                        .bitcast(mybir.dt.int32),
                                        in1=cs["maskMb"][:, :], op=ALU.bitwise_and)
                        v.engine_nop()
                        v.tensor_scalar_mul(out=Ntn4[:, cA:cA + 128],
                                            in0=mskd[:, :].bitcast(F32),
                                            scalar1=beta_fn[:, ch * 2 + ht:ch * 2 + ht + 1])
                        v.tensor_copy(out=Aqt4[96:112, cA + 112:cA + 128],
                                      in_=pStageB[0:16, ch * 32:ch * 32 + 16])
                        ins = v.tensor_scalar_mul(
                            out=Ntn4[96:112, cA + 112:cA + 128],
                            in0=pStageB[0:16, ch * 32 + 16:ch * 32 + 32],
                            scalar1=beta_fn[96:112, ch * 2 + ht:ch * 2 + ht + 1])
                    ph.inc(ins)
                V(evacb)

                def trz(t):
                    for ch in range(NCH):
                        ins = t.transpose(out=pT4[:, ch * 128:(ch + 1) * 128],
                                          in_=Ntn4[:, ch * 128:(ch + 1) * 128],
                                          identity=cs["identb"][:, :])
                    ph.inc(ins)
                T(trz)
                def cpz(v):
                    for ch in range(NCH):
                        ins = v.tensor_copy(out=pown4[0][:, ch * 128:(ch + 1) * 128],
                                            in_=pT4[:, ch * 128:(ch + 1) * 128])
                    ph.inc(ins)
                V(cpz)
                for j in range(1, 7):
                    def sqrb(t, j=j):
                        for ch in range(NCH):
                            sl = slice(ch * 128, (ch + 1) * 128)
                            lastT = Ntn4[:, sl] if j == 1 else powT4[j - 2][:, sl]
                            lastN = pown4[j - 1][:, sl]
                            ins = t.matmul(out=pA[:, sl], lhsT=lastN, rhs=lastT,
                                           start=True, stop=True)
                            if j < 6:
                                ins = t.matmul(out=pSolve[:, sl], lhsT=lastT, rhs=lastN,
                                               start=True, stop=True)
                        ph.inc(ins)
                    T(sqrb)
                    def sqevb(v, j=j):
                        for ch in range(NCH):
                            sl = slice(ch * 128, (ch + 1) * 128)
                            ins = v.tensor_copy(out=powT4[j - 1][:, sl], in_=pA[:, sl])
                            if j < 6:
                                ins = v.tensor_copy(out=pown4[j][:, sl], in_=pSolve[:, sl])
                        ph.inc(ins)
                    V(sqevb)
                def trv(t, ht=ht):
                    for ch in range(NCH):
                        ins = t.transpose(out=pT4[:, ch * 128:(ch + 1) * 128],
                                          in_=vh[ht][:, ch * 128:(ch + 1) * 128],
                                          identity=cs["identb"][:, :])
                    ph.inc(ins)
                T(trv)
                def cpv(v):
                    for ch in range(NCH):
                        ins = v.tensor_copy(out=rhsb4[:, ch * 256:ch * 256 + 128],
                                            in_=pT4[:, ch * 128:(ch + 1) * 128])
                    ph.inc(ins)
                V(cpv)
                def trw(t):
                    for ch in range(NCH):
                        ins = t.transpose(out=pT4[:, ch * 128:(ch + 1) * 128],
                                          in_=Ws[:, ch * 128:(ch + 1) * 128],
                                          identity=cs["identb"][:, :])
                    ph.inc(ins)
                T(trw)
                def cpw(v):
                    for ch in range(NCH):
                        ins = v.tensor_copy(out=rhsb4[:, ch * 256 + 128:ch * 256 + 256],
                                            in_=pT4[:, ch * 128:(ch + 1) * 128])
                    ph.inc(ins)
                V(cpw)
                ydl = spb[:, :].bitcast(BF16)
                ycur = rhsb4
                for j in range(6, -1, -1):
                    ynext = ybuf4[j % 2]
                    def chmm(t, j=j, ycur=ycur):
                        for ch in range(NCH):
                            p = Ntn4 if j == 0 else powT4[j - 1]
                            ins = t.matmul(out=pSolve[:, ch * 256:(ch + 1) * 256],
                                           lhsT=p[:, ch * 128:(ch + 1) * 128],
                                           rhs=ycur[:, ch * 256:(ch + 1) * 256],
                                           start=True, stop=True)
                        ph.inc(ins)
                    T(chmm)
                    def chadd(v, ycur=ycur, ynext=ynext):
                        for ch in range(NCH):
                            v.tensor_copy(out=ydl[:, ch * 256:(ch + 1) * 256],
                                          in_=pSolve[:, ch * 256:(ch + 1) * 256])
                        v.engine_nop()
                        for ch in range(NCH):
                            ins = v.tensor_add(out=ynext[:, ch * 256:(ch + 1) * 256],
                                               in0=ycur[:, ch * 256:(ch + 1) * 256],
                                               in1=ydl[:, ch * 256:(ch + 1) * 256])
                        ph.inc(ins)
                    V(chadd)
                    ycur = ynext
                U4 = ycur

                for ch in range(NCH):
                    c0 = ch * 128
                    def trs(t, ch=ch, c0=c0, U4=U4):
                        t.transpose(out=pT4[:, 0:128], in_=U4[:, ch * 256 + 128:ch * 256 + 256],
                                    identity=cs["identb"][:, :])
                        ph.inc(t.transpose(out=pT4[:, 128:256], in_=KD[:, c0:c0 + 128],
                                           identity=cs["identb"][:, :]))
                    T(trs)
                    def cps(v):
                        v.tensor_copy(out=UWc[:, :], in_=pT4[:, 0:128])
                        ph.inc(v.tensor_copy(out=KDt[:, :], in_=pT4[:, 128:256]))
                    V(cps)
                    T(lambda t, ht=ht: ph.inc(t.matmul(out=pSolve[:, 0:128], lhsT=UWc[:, :],
                                                       rhs=S_b[ht][:, :],
                                                       start=True, stop=True)))
                    def useqf(v, U4=U4, ch=ch, ht=ht):
                        v.tensor_copy(out=xdel[:, :], in_=pSolve[:, 0:128])
                        v.engine_nop()
                        v.tensor_sub(out=spb[:, 0:128], in0=U4[:, ch * 256:ch * 256 + 128],
                                     in1=xdel[:, :])
                        v.engine_nop()
                        ph.inc(v.tensor_scalar_mul(out=useq[:, :], in0=spb[:, 0:128],
                                                   scalar1=beta_f[:, ch * 2 + ht:ch * 2 + ht + 1]))
                    V(useqf)
                    def omm3(t, c0=c0, ht=ht):
                        t.matmul(out=pSolve[:, 256:384], lhsT=Qs[:, c0:c0 + 128],
                                 rhs=S_b[ht][:, :], start=True, stop=False)
                        t.matmul(out=pSolve[:, 256:384], lhsT=Aqt4[:, c0:c0 + 128],
                                 rhs=useq[:, :], start=False, stop=True)
                        ph.inc(t.matmul(out=pSolve[:, 128:256], lhsT=KDt[:, :],
                                        rhs=useq[:, :], start=True, stop=True))
                    T(omm3)
                    def supdf(v, ch=ch, ht=ht):
                        v.scalar_tensor_tensor(out=S_f[ht][:, :], in0=S_f[ht][:, :],
                                               scalar=eGend[:, ch:ch + 1],
                                               in1=pSolve[:, 128:256],
                                               op0=ALU.mult, op1=ALU.add)
                        v.tensor_copy(out=S_b[ht][:, :], in_=S_f[ht][:, :])
                        ph.inc(v.tensor_copy(out=sp1[:, c0:c0 + 128],
                                             in_=pSolve[:, 256:384]))
                    V(supdf)

                def episq(s):
                    for ch in range(NCH):
                        ins = s.activation(out=spb[:, ch * 128:(ch + 1) * 128],
                                           in_=sp1[:, ch * 128:(ch + 1) * 128],
                                           func=AF.Square, accum_out=ssq[:, ch:ch + 1])
                    ph.inc(ins)
                SC(episq)
                V(lambda v: ph.inc(v.tensor_scalar(out=ssq[:, 4:8], in0=ssq[:, 0:4],
                                                   scalar1=1.0 / 128, scalar2=1e-5,
                                                   op0=ALU.mult, op1=ALU.add)))
                SC(lambda s: ph.inc(s.activation(out=ssq[:, 8:12], in_=ssq[:, 4:8],
                                                 func=AF.Sqrt)))
                V(lambda v: ph.inc(v.reciprocal(out=ssq[:, 8:12], in_=ssq[:, 8:12])))
                def ofinb(v, ht=ht):
                    for ch in range(NCH):
                        v.tensor_scalar_mul(out=spb2[:, ch * 128:(ch + 1) * 128],
                                            in0=sp1[:, ch * 128:(ch + 1) * 128],
                                            scalar1=ssq[:, 8 + ch:9 + ch])
                    v.engine_nop()
                    for ch in range(NCH):
                        dst = obuf[:, ch * 256 + ht * 128:ch * 256 + ht * 128 + 128]
                        ins = v.tensor_mul(out=dst, in0=spb2[:, ch * 128:(ch + 1) * 128],
                                           in1=gate_tm[:, ch * 256 + ht * 128:
                                                       ch * 256 + ht * 128 + 128])
                    ph.inc(ins)
                V(ofinb)

            for fc in range(2):
                def trob(t, fc=fc):
                    for ch in range(NCH):
                        ins = t.transpose(
                            out=pT4[:, ch * 128:(ch + 1) * 128],
                            in_=obuf[:, ch * 256 + fc * 128:ch * 256 + fc * 128 + 128],
                            identity=cs["identb"][:, :])
                    ph.inc(ins)
                T(trob)
                def cpob(v, fc=fc):
                    for ch in range(NCH):
                        ins = v.tensor_scalar_mul(
                            out=ofcm[:, fc * GT + ch * 128:fc * GT + (ch + 1) * 128],
                            in0=pT4[:, ch * 128:(ch + 1) * 128], scalar1=onws[:, 0:1])
                    ph.inc(ins)
                V(cpob)
            dstb = (g * GT) // TPC
            cb0 = (g * GT) % TPC
            for jp in range(NFC // 2):
                def pmm(t, jp=jp):
                    for q in range(2):
                        jo = jp * 2 + q
                        pdst = pA[:, 0:GT] if q == 0 else pScoreB[:, 0:GT]
                        t.matmul(out=pdst,
                                 lhsT=ows2[:, 0 * D + jo * 128:0 * D + jo * 128 + 128],
                                 rhs=ofcm[:, 0:GT], start=True, stop=False)
                        ins = t.matmul(out=pdst,
                                       lhsT=ows2[:, 1 * D + jo * 128:1 * D + jo * 128 + 128],
                                       rhs=ofcm[:, GT:2 * GT], start=False, stop=True)
                    ph.inc(ins)
                T(pmm)
                def cppat(v):
                    v.tensor_copy(out=patb2[:, 0:GT], in_=pA[:, 0:GT])
                    ph.inc(v.tensor_copy(out=patb2[:, GT:2 * GT], in_=pScoreB[:, 0:GT]))
                V(cppat)
                def stp(gq, jp=jp, dstb=dstb, cb0=cb0):
                    for q in range(2):
                        jo = jp * 2 + q
                        ins = gq.dma_start(
                            out=pattn[dstb * D + jo * 128:dstb * D + (jo + 1) * 128,
                                      cb0:cb0 + GT],
                            in_=patb2[:, q * GT:(q + 1) * GT])
                        ph.inc(ins, 16)
                G3(stp)

        @block.gpsimd
        def _(gq):
            ph.wait(gq)


def emit_mid_f(C, nc, attn_sl, xns, xs, x1o, xn1o):
    debug = False
    TT = 512
    NT = TPC // TT
    E, I = 8, 1408
    NI = I // 128
    rout = nc.dram_tensor("rout", [D, 8], F32, kind="ExternalInput").ap()
    wug = nc.dram_tensor("wug", [E, D, 2 * I], F32, kind="ExternalInput").ap()
    wdn = nc.dram_tensor("wdn", [E, I, D], F32, kind="ExternalInput").ap()
    swu = nc.dram_tensor("swu", [D, 2 * 5632], F32, kind="ExternalInput").ap()
    swd = nc.dram_tensor("swd", [5632, D], F32, kind="ExternalInput").ap()
    identf_in = nc.dram_tensor("identf_m", [128, 128], F32, kind="ExternalInput").ap()
    esel_in = nc.dram_tensor("esel", [8, 8 * 128], BF16, kind="ExternalInput").ap()
    if debug:
        dbg_w8 = nc.dram_tensor("dbg_w8", [8, TPC], F32, kind="ExternalOutput").ap()
        dbg_y = nc.dram_tensor("dbg_y", [128, TPC], F32, kind="ExternalOutput").ap()
        dbg_lg = nc.dram_tensor("dbg_lg", [8, TPC], F32, kind="ExternalOutput").ap()
        dbg_h = nc.dram_tensor("dbg_h", [128, TPC], F32, kind="ExternalOutput").ap()

    with ExitStack() as st:
        def sbuf(n, sh, dt=F32):
            return st.enter_context(nc.sbuf_tensor("m_" + n, sh, dt))
        ph = C.ph
        block = C.block

        def T(fn):
            @block.tensor
            def _(t):
                ph.wait(t)
                fn(t)
        def V(fn):
            @block.vector
            def _(v):
                ph.wait(v)
                fn(v)
        def SC(fn):
            @block.scalar
            def _(s):
                ph.wait(s)
                fn(s)
        def SY(fn):
            @block.sync
            def _(sy):
                ph.wait(sy)
                fn(sy)
        def GQ(fn):
            @block.gpsimd
            def _(g):
                ph.wait(g)
                fn(g)

        y_sb = sbuf("y_sb", [128, NFC * TPC], BF16)
        idf = sbuf("idf", [128, 128], F32)
        onesc = sbuf("onesc", [1, 128], F32)
        onecol = sbuf("onecol", [128, 1], F32)
        epsc = sbuf("epsc", [1, 1], F32)
        gt2 = [sbuf(f"gtile{i}", [128, TT], BF16) for i in range(2)]
        util = sbuf("util", [128, TT], BF16)
        wbt = sbuf("wbt", [128, TT], BF16)
        pA6 = [st.enter_context(nc.psum_tensor(f"m_pA{i}", [128, TT], F32))
               for i in range(6)]
        pL8 = st.enter_context(nc.psum_tensor("m_pL8", [8, TT], F32))
        s_w = st.enter_context(nc.semaphore("m_sw"))
        s_t = st.enter_context(nc.semaphore("m_st"))
        s_sc = st.enter_context(nc.semaphore("m_ssc"))
        s_v = st.enter_context(nc.semaphore("m_sv"))
        s_wd = st.enter_context(nc.semaphore("m_swd"))
        s_dt = st.enter_context(nc.semaphore("m_sdt"))
        s_dv = st.enter_context(nc.semaphore("m_sdv"))

        def setup(v):
            v.memset(onesc[:, :], 1.0)
            v.memset(onecol[:, :], 1.0)
            v.memset(epsc[:, :], 1e-5)
            for j in range(NFC):
                v.memset(y_sb[:, j * TPC:(j + 1) * TPC], 0.0)
            ph.inc(v.engine_nop())
        V(setup)

        with ExitStack() as stB:
            xnsb = stB.enter_context(nc.sbuf_tensor("m_xnsb", [128, NFC * TPC], BF16))
            wga0 = stB.enter_context(nc.sbuf_tensor("m_wga0", [128, NFC * 128], BF16))
            wgb0 = stB.enter_context(nc.sbuf_tensor("m_wgb0", [128, NFC * 128], BF16))
            wga1 = stB.enter_context(nc.sbuf_tensor("m_wga1", [128, NFC * 128], BF16))
            wgb1 = stB.enter_context(nc.sbuf_tensor("m_wgb1", [128, NFC * 128], BF16))
            wdb_a = stB.enter_context(nc.sbuf_tensor("m_wdb", [128, NI * 128], BF16))
            wdb_b = stB.enter_context(nc.sbuf_tensor("m_wdb2", [128, NI * 128], BF16))
            w8cm = stB.enter_context(nc.sbuf_tensor("m_w8cm", [8, TPC], BF16))
            esel = stB.enter_context(nc.sbuf_tensor("m_esel_sb", [8, 8 * 128], BF16))
            stR = stB.enter_context(ExitStack())
            routs = stR.enter_context(nc.sbuf_tensor("m_routs", [128, NFC * 8], BF16))
            lgt = stR.enter_context(nc.sbuf_tensor("m_lgt", [128, 8], F32))
            msk8 = stR.enter_context(nc.sbuf_tensor("m_msk8", [128, 8], F32))
            negbig = stR.enter_context(nc.sbuf_tensor("m_negbig", [128, 1], F32))
            lg2 = stR.enter_context(nc.sbuf_tensor("m_lg2", [128, 8], F32))
            sm = stR.enter_context(nc.sbuf_tensor("m_sm", [128, 6], F32))
            lgcm = stR.enter_context(nc.sbuf_tensor("m_lgcm", [8, TPC], F32))
            def ldxn(sy):
                for j in range(NFC):
                    ph.inc(sy.dma_start(out=xnsb[:, j * TPC:(j + 1) * TPC],
                                        in_=xns[j * 128:(j + 1) * 128, :]), 16)
            SY(ldxn)
            def ldid(sy):
                ph.inc(sy.dma_start(out=idf[:, :], in_=identf_in), 16)
            SY(ldid)
            def ldrt(g):
                src = rout.rearrange("(j p) n -> p j n", p=128)
                dst = routs[:, :].rearrange("p (j n) -> p j n", n=8)
                ph.inc(g.dma_start(out=dst, in_=src), 16)
            GQ(ldrt)
            SY(lambda sy: ph.inc(sy.dma_start(out=esel[:, :], in_=esel_in), 16))
            V(lambda v: ph.inc(v.memset(negbig[:, :], -1e30)))
            for tt in range(NT):
                def mmlg(t, tt=tt):
                    for j in range(NFC):
                        ins = t.matmul(out=pL8[:, :], lhsT=routs[:, j * 8:(j + 1) * 8],
                                       rhs=xnsb[:, j * TPC + tt * TT:j * TPC + (tt + 1) * TT],
                                       start=(j == 0), stop=(j == NFC - 1))
                    ph.inc(ins)
                T(mmlg)
                V(lambda v, tt=tt: ph.inc(v.tensor_copy(out=lgcm[:, tt * TT:(tt + 1) * TT],
                                                  in_=pL8[:, :])))
                for cq in range(TT // 128):
                    def trlg(t, tt=tt, cq=cq):
                        ph.inc(t.transpose(
                            out=pA6[4][:, 0:8],
                            in_=lgcm[:, tt * TT + cq * 128:tt * TT + (cq + 1) * 128],
                            identity=idf[0:8, 0:8]))
                    T(trlg)
                    V(lambda v: ph.inc(v.tensor_copy(out=lgt[:, :], in_=pA6[4][:, 0:8])))
                    V(lambda v: ph.inc(v.reduce_max(out=sm[:, 0:1], in_=lgt[:, :],
                                                    axis=mybir.AxisListType.X)))
                    V(lambda v: ph.inc(v.tensor_scalar(out=msk8[:, :], in0=lgt[:, :],
                                                       scalar1=sm[:, 0:1], scalar2=None,
                                                       op0=ALU.is_ge)))
                    V(lambda v: ph.inc(v.scalar_tensor_tensor(out=lg2[:, :], in0=msk8[:, :],
                                                              scalar=negbig[:, 0:1],
                                                              in1=lgt[:, :], op0=ALU.mult,
                                                              op1=ALU.add)))
                    V(lambda v: ph.inc(v.reduce_max(out=sm[:, 1:2], in_=lg2[:, :],
                                                    axis=mybir.AxisListType.X)))
                    V(lambda v: ph.inc(v.tensor_sub(out=sm[:, 2:3], in0=sm[:, 1:2],
                                                    in1=sm[:, 0:1])))
                    V(lambda v: ph.inc(v.tensor_scalar_mul(out=sm[:, 3:4], in0=sm[:, 0:1],
                                                           scalar1=-1.0)))
                    def top2b(s):
                        s.activation(out=lg2[:, :], in_=lgt[:, :], func=AF.Exp, bias=sm[:, 3:4])
                        ph.inc(s.activation(out=sm[:, 4:5], in_=sm[:, 2:3], func=AF.Exp))
                    SC(top2b)
                    V(lambda v: ph.inc(v.tensor_scalar_add(out=sm[:, 4:5], in0=sm[:, 4:5],
                                                           scalar1=1.0)))
                    V(lambda v: ph.inc(v.reciprocal(out=sm[:, 5:6], in_=sm[:, 4:5])))
                    V(lambda v: ph.inc(v.tensor_scalar(out=lg2[:, :], in0=lg2[:, :],
                                                       scalar1=sm[:, 5:6], scalar2=None,
                                                       op0=ALU.mult)))
                    V(lambda v: ph.inc(v.tensor_scalar(out=lgt[:, :], in0=lgt[:, :],
                                                       scalar1=sm[:, 1:2], scalar2=None,
                                                       op0=ALU.is_ge)))
                    V(lambda v: ph.inc(v.tensor_mul(out=lg2[:, :], in0=lg2[:, :],
                                                    in1=lgt[:, :])))
                    def trw(t):
                        ph.inc(t.transpose(out=pA6[4][0:8, 0:128], in_=lg2[:, :],
                                           identity=idf[:, :]))
                    T(trw)
                    V(lambda v, tt=tt, cq=cq: ph.inc(
                        v.tensor_copy(out=w8cm[:, tt * TT + cq * 128:tt * TT + (cq + 1) * 128],
                                      in_=pA6[4][0:8, 0:128])))

            if debug:
                def dmpw(g):
                    for tt in range(NT):
                        ph.inc(g.dma_start(out=dbg_w8[:, tt * TT:(tt + 1) * TT].bitcast(BF16)[:, 0:TT] if False else dbg_w8[:, tt * TT:(tt + 1) * TT],
                                           in_=w8cm[:, tt * TT:(tt + 1) * TT]), 16)
                    ph.inc(g.dma_start(out=dbg_lg[:, :], in_=lgcm[:, :]), 16)
                GQ(dmpw)
            stR.close()
            hprime = stB.enter_context(nc.sbuf_tensor("m_hprime", [128, NI * TPC], BF16))
            passes = [("e", e) for e in range(E)] + [("s", g_) for g_ in range(4)]
            UPT = NI * NT
            DNT = NFC * NT
            wgp = [(wga0, wgb0), (wga1, wgb1)]
            wdp = [wdb_a, wdb_b]
            for pidx, (kind, ei) in enumerate(passes):
                for icp in range(NI):
                    wl = pidx * NI + icp

                    @block.gpsimd
                    def _(g, kind=kind, ei=ei, icp=icp, wl=wl):
                        if wl == 0:
                            ph.wait(g)
                        if wl >= 2:
                            g.wait_ge(s_t, 4 * (wl - 1))
                        if kind == "e":
                            gsrc = wug[ei, :, icp * 128:(icp + 1) * 128]
                            usrc = wug[ei, :, I + icp * 128:I + (icp + 1) * 128]
                        else:
                            base = ei * NI + icp
                            gsrc = swu[:, base * 128:(base + 1) * 128]
                            usrc = swu[:, 5632 + base * 128:5632 + (base + 1) * 128]
                        wga, wgb = wgp[wl % 2]
                        g.dma_start(out=wga[:, :].rearrange("p (j n) -> p j n", n=128),
                                    in_=gsrc.rearrange("(j p) n -> p j n", p=128)
                                    ).then_inc(s_w, 16)
                        g.dma_start(out=wgb[:, :].rearrange("p (j n) -> p j n", n=128),
                                    in_=usrc.rearrange("(j p) n -> p j n", p=128)
                                    ).then_inc(s_w, 16)
                    for tt in range(NT):
                        k = pidx * UPT + icp * NT + tt
                        q = k % 2

                        @block.tensor
                        def _(t, kind=kind, ei=ei, tt=tt, wl=wl, k=k, q=q, pidx=pidx):
                            if k == 0:
                                ph.wait(t)
                            t.wait_ge(s_w, 32 * (wl + 1))
                            if k >= 2:
                                t.wait_ge(s_v, k - 1)
                            if pidx >= 1:
                                t.wait_ge(s_dv, 64 * pidx)
                            wga, wgb = wgp[wl % 2]
                            for j in range(NFC):
                                t.matmul(out=pA6[2 * q][:, :],
                                         lhsT=wga[:, j * 128:(j + 1) * 128],
                                         rhs=xnsb[:, j * TPC + tt * TT:j * TPC + (tt + 1) * TT],
                                         start=(j == 0), stop=(j == NFC - 1))
                            for j in range(NFC):
                                ins = t.matmul(out=pA6[2 * q + 1][:, :],
                                               lhsT=wgb[:, j * 128:(j + 1) * 128],
                                               rhs=xnsb[:, j * TPC + tt * TT:
                                                        j * TPC + (tt + 1) * TT],
                                               start=(j == 0), stop=(j == NFC - 1))
                            if kind == "e":
                                ins = t.matmul(out=pA6[4 + q][:, :],
                                               lhsT=esel[:, ei * 128:(ei + 1) * 128],
                                               rhs=w8cm[0:8, tt * TT:(tt + 1) * TT],
                                               start=True, stop=True)
                            ins.then_inc(s_t, 1)

                        @block.scalar
                        def _(s, k=k, q=q):
                            if k == 0:
                                ph.wait(s)
                            s.wait_ge(s_t, k + 1)
                            if k >= 2:
                                s.wait_ge(s_v, k - 1)
                            s.activation(out=gt2[q][:, :], in_=pA6[2 * q][:, :],
                                         func=AF.Silu).then_inc(s_sc, 1)

                        @block.vector
                        def _(v, kind=kind, icp=icp, tt=tt, k=k, q=q, pidx=pidx):
                            if k == 0:
                                ph.wait(v)
                            v.wait_ge(s_sc, k + 1)
                            if pidx >= 1:
                                v.wait_ge(s_dt, 64 * pidx)
                            v.tensor_copy(out=util[:, :], in_=pA6[2 * q + 1][:, :])
                            if kind == "e":
                                v.tensor_copy(out=wbt[:, :], in_=pA6[4 + q][:, :])
                            v.engine_nop()
                            dst = hprime[:, icp * TPC + tt * TT:icp * TPC + (tt + 1) * TT]
                            if kind == "e":
                                v.tensor_mul(out=gt2[q][:, :], in0=gt2[q][:, :],
                                             in1=util[:, :])
                                v.tensor_mul(out=dst, in0=gt2[q][:, :],
                                             in1=wbt[:, :]).then_inc(s_v, 1)
                            else:
                                v.tensor_mul(out=dst, in0=gt2[q][:, :],
                                             in1=util[:, :]).then_inc(s_v, 1)
                for jo in range(NFC):
                    dl = pidx * NFC + jo

                    @block.gpsimd
                    def _(g, kind=kind, ei=ei, jo=jo, dl=dl):
                        if dl >= 2:
                            g.wait_ge(s_dt, 4 * (dl - 1))
                        if kind == "e":
                            src = wdn[ei, :, jo * 128:(jo + 1) * 128]
                        else:
                            src = swd[ei * I:(ei + 1) * I, jo * 128:(jo + 1) * 128]
                        wdb = wdp[dl % 2]
                        g.dma_start(out=wdb[:, :].rearrange("p (j n) -> p j n", n=128),
                                    in_=src.rearrange("(j p) n -> p j n", p=128)
                                    ).then_inc(s_wd, 16)
                    for tt in range(NT):
                        m = pidx * DNT + jo * NT + tt

                        @block.tensor
                        def _(t, tt=tt, dl=dl, m=m, pidx=pidx):
                            t.wait_ge(s_wd, 16 * (dl + 1))
                            if m >= 4:
                                t.wait_ge(s_dv, m - 3)
                            t.wait_ge(s_v, UPT * (pidx + 1))
                            t.wait_ge(s_sc, UPT * (pidx + 1))
                            wdb = wdp[dl % 2]
                            for ic in range(NI):
                                ins = t.matmul(out=pA6[m % 4][:, :],
                                               lhsT=wdb[:, ic * 128:(ic + 1) * 128],
                                               rhs=hprime[:, ic * TPC + tt * TT:
                                                          ic * TPC + (tt + 1) * TT],
                                               start=(ic == 0), stop=(ic == NI - 1))
                            ins.then_inc(s_dt, 1)

                        @block.vector
                        def _(v, jo=jo, tt=tt, m=m):
                            v.wait_ge(s_dt, m + 1)
                            v.tensor_copy(out=util[:, :], in_=pA6[m % 4][:, :])
                            v.engine_nop()
                            dst = y_sb[:, jo * TPC + tt * TT:jo * TPC + (tt + 1) * TT]
                            v.tensor_add(out=dst, in0=dst, in1=util[:, :]).then_inc(s_dv, 1)

            @block.vector
            def _(v):
                ph.inc(v.engine_nop())

            if debug:
                def dmph(g):
                    ph.inc(g.dma_start(out=dbg_h[:, :], in_=hprime[:, 0:TPC]), 16)
                GQ(dmph)
        if debug:
            def dmpy(g):
                ph.inc(g.dma_start(out=dbg_y[:, :], in_=y_sb[:, 0:TPC]), 16)
            GQ(dmpy)
        with nc.sbuf_tensor("m_x1b", [128, NFC * TT], F32) as x1b, \
             nc.sbuf_tensor("m_sqf", [128, TT], F32) as sqf, \
             nc.sbuf_tensor("m_statm", [1, 6 * TT], F32) as statm, \
             nc.sbuf_tensor("m_xn1b", [128, TT], BF16) as xn1b, \
             nc.sbuf_tensor("m_f32t", [128, TT], F32) as f32t, \
             nc.sbuf_tensor("m_atb", [128, TT], BF16) as atb:
            for tt in range(NT):
                def ldx(sy, tt=tt):
                    for j in range(NFC):
                        ph.inc(sy.dma_start(out=x1b[:, j * TT:(j + 1) * TT],
                                            in_=xs[j * 128:(j + 1) * 128,
                                                   tt * TT:(tt + 1) * TT]), 16)
                SY(ldx)
                for j in range(NFC):
                    def ldat(sy, tt=tt, j=j):
                        ph.inc(sy.dma_start(out=atb[:, :],
                                            in_=attn_sl[j * 128:(j + 1) * 128,
                                                        tt * TT:(tt + 1) * TT]), 16)
                    SY(ldat)
                    V(lambda v, j=j: ph.inc(v.tensor_copy(out=f32t[:, :], in_=atb[:, :])))
                    V(lambda v, j=j: ph.inc(v.tensor_add(out=x1b[:, j * TT:(j + 1) * TT],
                                                         in0=x1b[:, j * TT:(j + 1) * TT],
                                                         in1=f32t[:, :])))
                    V(lambda v, tt=tt, j=j: ph.inc(
                        v.tensor_copy(out=f32t[:, :],
                                      in_=y_sb[:, j * TPC + tt * TT:j * TPC + (tt + 1) * TT])))
                    V(lambda v, j=j: ph.inc(v.tensor_add(out=x1b[:, j * TT:(j + 1) * TT],
                                                         in0=x1b[:, j * TT:(j + 1) * TT],
                                                         in1=f32t[:, :])))
                def stx1(g, tt=tt):
                    for j in range(NFC):
                        ph.inc(g.dma_start(out=x1o[j * 128:(j + 1) * 128,
                                                   tt * TT:(tt + 1) * TT],
                                           in_=x1b[:, j * TT:(j + 1) * TT]), 16)
                GQ(stx1)
                def mmsum(t, tt=tt):
                    for j in range(NFC):
                        ins = t.matmul(out=pA6[0][0:1, :], lhsT=onecol[:, 0:1],
                                       rhs=x1b[:, j * TT:(j + 1) * TT],
                                       start=(j == 0), stop=(j == NFC - 1))
                    ph.inc(ins)
                T(mmsum)
                for j in range(NFC):
                    SC(lambda s, j=j: ph.inc(s.activation(out=sqf[:, :],
                                                          in_=x1b[:, j * TT:(j + 1) * TT],
                                                          func=AF.Square)))
                    def mmsq(t, j=j):
                        ph.inc(t.matmul(out=pA6[1][0:1, :], lhsT=onecol[:, 0:1], rhs=sqf[:, :],
                                        start=(j == 0), stop=(j == NFC - 1)))
                    T(mmsq)
                def stats(v):
                    v.tensor_scalar_mul(out=statm[:, 0 * TT:1 * TT], in0=pA6[0][0:1, :], scalar1=1.0 / D)
                    v.tensor_scalar_mul(out=statm[:, 1 * TT:2 * TT], in0=pA6[1][0:1, :], scalar1=1.0 / D)
                    v.tensor_mul(out=statm[:, 2 * TT:3 * TT], in0=statm[:, 0 * TT:1 * TT],
                                 in1=statm[:, 0 * TT:1 * TT])
                    ph.inc(v.tensor_sub(out=statm[:, 3 * TT:4 * TT], in0=statm[:, 1 * TT:2 * TT],
                                        in1=statm[:, 2 * TT:3 * TT]))
                V(stats)
                SC(lambda s: ph.inc(s.activation(out=statm[:, 4 * TT:5 * TT],
                                                 in_=statm[:, 3 * TT:4 * TT],
                                                 func=AF.Sqrt, bias=epsc[0:1, 0:1])))
                V(lambda v: ph.inc(v.reciprocal(out=statm[:, 5 * TT:6 * TT],
                                                in_=statm[:, 4 * TT:5 * TT])))
                def bcst(t):
                    t.matmul(out=pA6[2][:, :], lhsT=onesc[:, :], rhs=statm[:, 0 * TT:1 * TT],
                             start=True, stop=True)
                    ph.inc(t.matmul(out=pA6[3][:, :], lhsT=onesc[:, :],
                                    rhs=statm[:, 5 * TT:6 * TT], start=True, stop=True))
                T(bcst)
                for j in range(NFC):
                    def norm(v, j=j):
                        v.tensor_sub(out=sqf[:, :], in0=x1b[:, j * TT:(j + 1) * TT],
                                     in1=pA6[2][:, :])
                        ph.inc(v.tensor_mul(out=xn1b[:, :], in0=sqf[:, :], in1=pA6[3][:, :]))
                    V(norm)
                    def stxn(g, tt=tt, j=j):
                        ph.inc(g.dma_start(out=xn1o[j * 128:(j + 1) * 128,
                                                    tt * TT:(tt + 1) * TT],
                                           in_=xn1b[:, :]), 16)
                    GQ(stxn)

        @block.gpsimd
        def _(g):
            ph.wait(g)


def emit_l1attn_f(C, nc, xn1_g, pattn2):
    core = 0
    S = 4096
    NB = 4
    NQT = S // 128
    wq = nc.dram_tensor("wq1", [D, 256], F32, kind="ExternalInput").ap()
    wk = nc.dram_tensor("wk1", [D, 64], F32, kind="ExternalInput").ap()
    wv = nc.dram_tensor("wv1", [D, 64], F32, kind="ExternalInput").ap()
    CST = l1attn_consts(core)
    cin = {}
    for k, v in CST.items():
        dt = BF16 if v.dtype == BF16NP else F32
        cin[k] = nc.dram_tensor(k, list(v.shape), dt, kind="ExternalInput").ap()
    wop = nc.dram_tensor("wop", [256, D], F32, kind="ExternalInput").ap()
    qtm_d = nc.dram_tensor("l_qtm_d", [S, 384], BF16, kind="Internal").ap()

    with ExitStack() as st:
        def sbuf(n, sh, dt=F32):
            return st.enter_context(nc.sbuf_tensor("l_" + n, sh, dt))
        ph = C.ph
        block = C.block

        def T(fn):
            @block.tensor
            def _(t):
                ph.wait(t)
                fn(t)
        def V(fn):
            @block.vector
            def _(v):
                ph.wait(v)
                fn(v)
        def SC(fn):
            @block.scalar
            def _(s):
                ph.wait(s)
                fn(s)
        def SY(fn):
            @block.sync
            def _(sy):
                ph.wait(sy)
                fn(sy)
        def GQ(fn):
            @block.gpsimd
            def _(g):
                ph.wait(g)
                fn(g)

        wqs = sbuf("wqs", [128, NFC * 256], BF16)
        wkv = sbuf("wkv", [128, NFC * 128], BF16)
        cs = {}
        for k, v in CST.items():
            dt = BF16 if v.dtype == BF16NP else F32
            cs[k] = sbuf("c3_" + k, list(v.shape), dt)
        xnt = sbuf("xnt", [128, NFC * 128], BF16)
        qtmb = sbuf("qtmb", [128, 256], BF16)
        ktmb = sbuf("ktmb", [128, 64], BF16)
        rt1 = sbuf("rt1", [128, 128], BF16)
        rt2 = sbuf("rt2", [128, 128], BF16)
        vaug = sbuf("vaug", [128, NQT * 65], BF16)
        q4cm = sbuf("q4cm", [64, 4 * S], BF16)
        qx2 = sbuf("qx2", [128, S], BF16)
        kcmx = sbuf("kcmx", [128, S], BF16)
        st1 = sbuf("st1", [128, 512], F32)
        ptile = sbuf("ptile", [128, 512], BF16)
        neg30 = sbuf("neg30", [128, 1], F32)
        ctxb = sbuf("ctxb", [65, 512], BF16)
        rnb = sbuf("rnb", [64, 512], BF16)
        rsum = sbuf("rsum", [1, 512], BF16)
        ones64 = sbuf("ones64", [1, 64], BF16)
        ctxf = sbuf("ctxf", [64, 512], BF16)
        ctx2g = sbuf("ctx2g", [128, 2 * 512], BF16)
        wos2 = sbuf("wos2", [128, 2 * D], BF16)
        pat2 = sbuf("pat2", [128, 512], BF16)
        pPt = st.enter_context(nc.psum_tensor("l_pPt", [128, 512], F32))
        pP = st.enter_context(nc.psum_tensor("l_pP", [128, 256], F32))
        pKV = st.enter_context(nc.psum_tensor("l_pKV", [128, 128], F32))
        pSc = st.enter_context(nc.psum_tensor("l_pSc", [128, 512], F32))
        pCtx = st.enter_context(nc.psum_tensor("l_pCtx", [65, 512], F32))
        pBc = st.enter_context(nc.psum_tensor("l_pBc", [64, 512], F32))

        def ld0(g):
            srcw = wop.rearrange("(j p) n -> p j n", p=128)
            dstw = wos2[:, :].rearrange("p (j n) -> p j n", n=D)
            ph.inc(g.dma_start(out=dstw, in_=srcw), 16)
            src = wq.rearrange("(j p) n -> p j n", p=128)
            ph.inc(g.dma_start(out=wqs[:, :].rearrange("p (j n) -> p j n", n=256), in_=src), 16)
            ph.inc(g.dma_start(out=wkv[:, :].rearrange("p (j n) -> p j n", n=128)[:, :, 0:64],
                               in_=wk.rearrange("(j p) n -> p j n", p=128)), 16)
            ph.inc(g.dma_start(out=wkv[:, :].rearrange("p (j n) -> p j n", n=128)[:, :, 64:128],
                               in_=wv.rearrange("(j p) n -> p j n", p=128)), 16)
        GQ(ld0)
        def ld1(sy):
            for k in CST:
                ph.inc(sy.dma_start(out=cs[k][:, :], in_=cin[k]), 16)
        SY(ld1)
        def st0(v):
            v.memset(neg30[:, :], -30.0)
            ph.inc(v.memset(ones64[:, :], 1.0))
        V(st0)

        for b in range(NB):
            for qt in range(NQT):
                tok0 = b * S + qt * 128
                def ldxn(sy, tok0=tok0):
                    rk = tok0 // TPC
                    cb = tok0 % TPC
                    for j in range(NFC):
                        ph.inc(sy.dma_start(out=xnt[:, j * 128:(j + 1) * 128],
                                            in_=xn1_g[rk * D + j * 128:rk * D + (j + 1) * 128,
                                                      cb:cb + 128]), 16)
                SY(ldxn)
                def mmq(t):
                    for j in range(NFC):
                        t.matmul(out=pP[:, :], lhsT=xnt[:, j * 128:(j + 1) * 128],
                                 rhs=wqs[:, j * 256:(j + 1) * 256],
                                 start=(j == 0), stop=(j == NFC - 1))
                    for j in range(NFC):
                        ins = t.matmul(out=pKV[:, :], lhsT=xnt[:, j * 128:(j + 1) * 128],
                                       rhs=wkv[:, j * 128:(j + 1) * 128],
                                       start=(j == 0), stop=(j == NFC - 1))
                    ph.inc(ins)
                T(mmq)
                def ropeq(v, qt=qt):
                    _lp = st.enter_context(nc.allow_low_precision(
                        reason="bf16 rope elementwise, blessed tolerance")) if False else None
                    cosb = cs["cos_t"][:, qt * 32:(qt + 1) * 32]
                    sinb = cs["sin_t"][:, qt * 32:(qt + 1) * 32]
                    q3 = pP[:, :].rearrange("p (h d) -> p h d", h=4)
                    o3 = qtmb[:, :].rearrange("p (h d) -> p h d", h=4)
                    c3 = cosb.unsqueeze(1).broadcast_to([128, 4, 32])
                    s3 = sinb.unsqueeze(1).broadcast_to([128, 4, 32])
                    v.tensor_mul(out=o3[:, :, 0::2], in0=q3[:, :, 1::2], in1=s3)
                    v.engine_nop()
                    v.tensor_mul(out=rt1[:, :].rearrange("p (h d) -> p h d", h=4)[:, :, 0:32],
                                 in0=q3[:, :, 0::2], in1=c3)
                    v.engine_nop()
                    v.tensor_sub(out=o3[:, :, 0::2],
                                 in0=rt1[:, :].rearrange("p (h d) -> p h d", h=4)[:, :, 0:32],
                                 in1=o3[:, :, 0::2])
                    v.tensor_mul(out=o3[:, :, 1::2], in0=q3[:, :, 0::2], in1=s3)
                    v.engine_nop()
                    v.tensor_mul(out=rt1[:, :].rearrange("p (h d) -> p h d", h=4)[:, :, 0:32],
                                 in0=q3[:, :, 1::2], in1=c3)
                    v.engine_nop()
                    ph.inc(v.tensor_add(out=o3[:, :, 1::2],
                                        in0=rt1[:, :].rearrange("p (h d) -> p h d", h=4)[:, :, 0:32],
                                        in1=o3[:, :, 1::2]))
                V(ropeq)
                def ropek(v, qt=qt):
                    cosb = cs["cos_t"][:, qt * 32:(qt + 1) * 32]
                    sinb = cs["sin_t"][:, qt * 32:(qt + 1) * 32]
                    k2 = pKV[:, 0:64]
                    o2 = ktmb[:, :]
                    v.tensor_mul(out=o2[:, 0::2], in0=k2[:, 1::2], in1=sinb)
                    v.engine_nop()
                    v.tensor_mul(out=rt2[:, 0:32], in0=k2[:, 0::2], in1=cosb)
                    v.engine_nop()
                    v.tensor_sub(out=o2[:, 0::2], in0=rt2[:, 0:32], in1=o2[:, 0::2])
                    v.tensor_mul(out=o2[:, 1::2], in0=k2[:, 0::2], in1=sinb)
                    v.engine_nop()
                    v.tensor_mul(out=rt2[:, 0:32], in0=k2[:, 1::2], in1=cosb)
                    v.engine_nop()
                    v.tensor_add(out=o2[:, 1::2], in0=rt2[:, 0:32], in1=o2[:, 1::2])
                    v.tensor_copy(out=vaug[:, qt * 65:qt * 65 + 64], in_=pKV[:, 64:128])
                    ph.inc(v.memset(vaug[:, qt * 65 + 64:qt * 65 + 65], 1.0))
                V(ropek)
                def stqk(g, qt=qt):
                    ph.inc(g.dma_start(out=qtm_d[qt * 128:(qt + 1) * 128, 0:256],
                                       in_=qtmb[:, :]), 16)
                    ph.inc(g.dma_start(out=qtm_d[qt * 128:(qt + 1) * 128, 256:320],
                                       in_=ktmb[:, :]), 16)
                GQ(stqk)
            for hp in range(2):
                def trq(sy, hp=hp):
                    ph.inc(sy.dma_start_transpose(out=qx2[:, :],
                                                  in_=qtm_d[:, hp * 128:(hp + 1) * 128]), 16)
                SY(trq)
                def spl(v, hp=hp):
                    v.tensor_copy(out=q4cm[:, (2 * hp) * S:(2 * hp + 1) * S], in_=qx2[0:64, :])
                    ph.inc(v.tensor_copy(out=q4cm[:, (2 * hp + 1) * S:(2 * hp + 2) * S],
                                         in_=qx2[64:128, :]))
                V(spl)
            def trk(sy):
                ph.inc(sy.dma_start_transpose(out=kcmx[:, :],
                                              in_=qtm_d[:, 256:384]), 16)
            SY(trk)
            for qt in range(NQT):
                for kt in range(qt + 1):
                    def mmsc(t, qt=qt, kt=kt):
                        q3 = q4cm[:, :].rearrange("p (h t) -> p h t", h=4)
                        ps3 = pSc[:, :].rearrange("p (h t) -> p h t", h=4)
                        t.matmul(out=ps3[:, :, :], lhsT=kcmx[0:64, kt * 128:(kt + 1) * 128],
                                 rhs=q3[:, :, qt * 128:(qt + 1) * 128], start=True, stop=False)
                        qa3 = cs["qaug"][:, :].rearrange("p (h t) -> p h t", h=4)
                        ph.inc(t.matmul(out=ps3[:, :, :],
                                        lhsT=cs["kpos"][:, kt * 128:(kt + 1) * 128],
                                        rhs=qa3[:, :, qt * 128:(qt + 1) * 128],
                                        start=False, stop=True, skip_group_check=True))
                    T(mmsc)
                    def capexp(s, qt=qt, kt=kt):
                        s.activation(out=st1[:, :], in_=pSc[:, :], func=AF.Tanh,
                                     scale=1.0 / 240.0)
                        ph.inc(s.activation(out=ptile[:, :], in_=st1[:, :], func=AF.Exp,
                                            scale=30.0, bias=neg30[:, 0:1]))
                    SC(capexp)
                    if kt == qt:
                        def dmask(v):
                            p3 = ptile[:, :].rearrange("p (h t) -> p h t", h=4)
                            m3 = cs["causal"][:, :].unsqueeze(1).broadcast_to([128, 4, 128])
                            ph.inc(v.tensor_tensor(out=p3, in0=p3, in1=m3, op=ALU.mult))
                        V(dmask)
                    def mmpv(t, qt=qt, kt=kt):
                        ph.inc(t.matmul(out=pCtx[:, :], lhsT=vaug[:, kt * 65:(kt + 1) * 65],
                                        rhs=ptile[:, :], start=(kt == 0), stop=(kt == qt),
                                        skip_group_check=True))
                    T(mmpv)
                V(lambda v: ph.inc(v.tensor_copy(out=ctxb[:, :], in_=pCtx[:, :])))
                V(lambda v: ph.inc(v.tensor_copy(out=rsum[:, :], in_=ctxb[64:65, :])))
                V(lambda v: ph.inc(v.reciprocal(out=rsum[:, :], in_=rsum[:, :])))
                def bcn(t):
                    ph.inc(t.matmul(out=pBc[:, :], lhsT=ones64[:, :],
                                    rhs=rsum[:, :], start=True, stop=True))
                T(bcn)
                V(lambda v: ph.inc(v.tensor_copy(out=rnb[:, :], in_=pBc[:, :])))
                def cnorm(v):
                    ph.inc(v.tensor_mul(out=ctxf[:, :], in0=ctxb[0:64, :], in1=rnb[:, :]))
                V(cnorm)
                def asm2(v, qt=qt):
                    q4 = qt % 4
                    for hp in range(2):
                        v.tensor_copy(out=ctx2g[0:64, hp * 512 + q4 * 128:
                                                hp * 512 + (q4 + 1) * 128],
                                      in_=ctxf[:, (2 * hp) * 128:(2 * hp + 1) * 128])
                        v.tensor_copy(out=ctx2g[64:128, hp * 512 + q4 * 128:
                                                hp * 512 + (q4 + 1) * 128],
                                      in_=ctxf[:, (2 * hp + 1) * 128:(2 * hp + 2) * 128])
                    ph.inc(v.engine_nop())
                V(asm2)
                if qt % 4 == 3:
                    tok0 = b * S + (qt - 3) * 128
                    dstb = tok0 // TPC
                    cb0 = tok0 % TPC
                    for jo in range(NFC):
                        def pmm2(t, jo=jo):
                            t.matmul(out=pPt[:, :],
                                     lhsT=wos2[:, 0 * D + jo * 128:0 * D + jo * 128 + 128],
                                     rhs=ctx2g[:, 0:512], start=True, stop=False)
                            ph.inc(t.matmul(
                                out=pPt[:, :],
                                lhsT=wos2[:, 1 * D + jo * 128:1 * D + jo * 128 + 128],
                                rhs=ctx2g[:, 512:1024], start=False, stop=True))
                        T(pmm2)
                        V(lambda v: ph.inc(v.tensor_copy(out=pat2[:, :], in_=pPt[:, :])))
                        def stp2(g, jo=jo, dstb=dstb, cb0=cb0):
                            ph.inc(g.dma_start(
                                out=pattn2[dstb * D + jo * 128:dstb * D + (jo + 1) * 128,
                                           cb0:cb0 + 512],
                                in_=pat2[:, :]), 16)
                        GQ(stp2)

        @block.gpsimd
        def _(g):
            ph.wait(g)


def emit_fin_f(C, nc, attn2_sl, xn1s, x1s, x2o):
    TT = 512
    NT = TPC // TT
    NIC = 8192 // 128
    bo = nc.dram_tensor("bo", [128, NFC], F32, kind="ExternalInput").ap()
    wg8 = nc.dram_tensor("wg8", [D, 8192], I8, kind="ExternalInput").ap()
    wu8 = nc.dram_tensor("wu8", [D, 8192], I8, kind="ExternalInput").ap()
    wd8 = nc.dram_tensor("wd8", [8192, D], I8, kind="ExternalInput").ap()
    sg = nc.dram_tensor("sg", [128, NIC], F32, kind="ExternalInput").ap()
    su = nc.dram_tensor("su", [128, NIC], F32, kind="ExternalInput").ap()
    sd = nc.dram_tensor("sd", [128, NFC], F32, kind="ExternalInput").ap()

    with ExitStack() as st:
        def sbuf(n, sh, dt=F32):
            return st.enter_context(nc.sbuf_tensor("n_" + n, sh, dt))
        ph = C.ph
        block = C.block
        def T(fn):
            @block.tensor
            def _(t):
                ph.wait(t)
                fn(t)
        def V(fn):
            @block.vector
            def _(v):
                ph.wait(v)
                fn(v)
        def SC(fn):
            @block.scalar
            def _(s):
                ph.wait(s)
                fn(s)
        def SY(fn):
            @block.sync
            def _(sy):
                ph.wait(sy)
                fn(sy)
        def GQ(fn):
            @block.gpsimd
            def _(g):
                ph.wait(g)
                fn(g)

        sgs = sbuf("sgs", [128, NIC], F32)
        sus = sbuf("sus", [128, NIC], F32)
        sds = sbuf("sds", [128, NFC], F32)
        bos = sbuf("bos", [128, NFC], F32)
        f32t = sbuf("f32t", [128, TT], F32)
        f32u = sbuf("f32u", [128, TT], F32)
        gt = sbuf("gt", [128, TT], BF16)
        at2b = sbuf("at2b", [128, TT], BF16)
        ut = sbuf("ut", [128, TT], BF16)
        pM = st.enter_context(nc.psum_tensor("n_pM", [128, TT], F32))
        pG = st.enter_context(nc.psum_tensor("n_pG", [128, TT], F32))
        pU = st.enter_context(nc.psum_tensor("n_pU", [128, TT], F32))

        def lds(sy):
            for src, dst in ((sg, sgs), (su, sus), (sd, sds), (bo, bos)):
                ph.inc(sy.dma_start(out=dst[:, :], in_=src), 16)
        SY(lds)

        with nc.sbuf_tensor("n_xn1sb", [128, NFC * TPC], BF16) as xn1sb, \
             nc.sbuf_tensor("n_hst", [128, NIC * TT], BF16) as hst, \
             nc.sbuf_tensor("n_wgc", [128, NFC * 128], BF16) as wgc, \
             nc.sbuf_tensor("n_wuc", [128, NFC * 128], BF16) as wuc, \
             nc.sbuf_tensor("n_wdc", [128, NIC * 128], BF16) as wdc, \
             nc.sbuf_tensor("n_x2b", [128, TT], F32) as x2b:
            def ldxn(sy):
                for j in range(NFC):
                    ph.inc(sy.dma_start(out=xn1sb[:, j * TPC:(j + 1) * TPC],
                                        in_=xn1s[j * 128:(j + 1) * 128, :]), 16)
            SY(ldxn)
            for tt in range(NT):
                for ic in range(NIC):
                    def ldw(g, ic=ic):
                        ph.inc(g.dma_start(out=wgc[:, :].rearrange("p (j n) -> p j n", n=128),
                                           in_=wg8[:, ic * 128:(ic + 1) * 128]
                                           .rearrange("(j p) n -> p j n", p=128)), 16)
                        ph.inc(g.dma_start(out=wuc[:, :].rearrange("p (j n) -> p j n", n=128),
                                           in_=wu8[:, ic * 128:(ic + 1) * 128]
                                           .rearrange("(j p) n -> p j n", p=128)), 16)
                    GQ(ldw)
                    def mmgu(t, tt=tt):
                        for j in range(NFC):
                            t.matmul(out=pG[:, :], lhsT=wgc[:, j * 128:(j + 1) * 128],
                                     rhs=xn1sb[:, j * TPC + tt * TT:j * TPC + (tt + 1) * TT],
                                     start=(j == 0), stop=(j == NFC - 1))
                        for j in range(NFC):
                            ins = t.matmul(out=pU[:, :], lhsT=wuc[:, j * 128:(j + 1) * 128],
                                           rhs=xn1sb[:, j * TPC + tt * TT:j * TPC + (tt + 1) * TT],
                                           start=(j == 0), stop=(j == NFC - 1))
                        ph.inc(ins)
                    T(mmgu)
                    SC(lambda s, ic=ic: ph.inc(s.activation(out=gt[:, :], in_=pG[:, :],
                                                            func=AF.Silu,
                                                            scale=sgs[:, ic:ic + 1])))
                    V(lambda v, ic=ic: ph.inc(v.tensor_scalar_mul(out=ut[:, :], in0=pU[:, :],
                                                                  scalar1=sus[:, ic:ic + 1])))
                    V(lambda v, ic=ic: ph.inc(v.tensor_mul(
                        out=hst[:, ic * TT:(ic + 1) * TT], in0=gt[:, :], in1=ut[:, :])))
                for jo in range(NFC):
                    def lddn(g, jo=jo):
                        ph.inc(g.dma_start(out=wdc[:, :].rearrange("p (j n) -> p j n", n=128),
                                           in_=wd8[:, jo * 128:(jo + 1) * 128]
                                           .rearrange("(j p) n -> p j n", p=128)), 16)
                    GQ(lddn)
                    def mmdn(t):
                        for ic in range(NIC):
                            ins = t.matmul(out=pM[:, :], lhsT=wdc[:, ic * 128:(ic + 1) * 128],
                                           rhs=hst[:, ic * TT:(ic + 1) * TT],
                                           start=(ic == 0), stop=(ic == NIC - 1))
                        ph.inc(ins)
                    T(mmdn)
                    def ldx1(sy, tt=tt, jo=jo):
                        ph.inc(sy.dma_start(out=x2b[:, :],
                                            in_=x1s[jo * 128:(jo + 1) * 128,
                                                    tt * TT:(tt + 1) * TT]), 16)
                        ph.inc(sy.dma_start(out=at2b[:, :],
                                            in_=attn2_sl[jo * 128:(jo + 1) * 128,
                                                         tt * TT:(tt + 1) * TT]), 16)
                    SY(ldx1)
                    V(lambda v, jo=jo: ph.inc(v.tensor_scalar_add(
                        out=f32u[:, :], in0=at2b[:, :], scalar1=bos[:, jo:jo + 1])))
                    def fin(v, jo=jo):
                        v.tensor_scalar_mul(out=f32t[:, :], in0=pM[:, :],
                                            scalar1=sds[:, jo:jo + 1])
                        v.engine_nop()
                        v.tensor_add(out=x2b[:, :], in0=x2b[:, :], in1=f32u[:, :])
                        v.engine_nop()
                        ph.inc(v.tensor_add(out=x2b[:, :], in0=x2b[:, :], in1=f32t[:, :]))
                    V(fin)
                    def stx2(g, tt=tt, jo=jo):
                        ph.inc(g.dma_start(out=x2o[jo * 128:(jo + 1) * 128,
                                                   tt * TT:(tt + 1) * TT],
                                           in_=x2b[:, :]), 16)
                    GQ(stx2)

        @block.gpsimd
        def _(g):
            ph.wait(g)



def fused_inmap(core, ins):
    c = core
    x = ins['L0_attn_hidden_states'].reshape(TOK, D)
    x_fm = np.ascontiguousarray(np.asarray(x, np.float32).T)
    m = {"x_sl": np.ascontiguousarray(x_fm[:, c * TPC:(c + 1) * TPC])}
    m0 = l0attn_inmap(c, None, ins)
    m0.pop("xn", None)
    m.update(m0)
    m["owp"] = np.ascontiguousarray(ins["L0_attn_o_w"][256 * c:256 * (c + 1), :])
    m["onw"] = np.ascontiguousarray(ins["L0_attn_o_norm_w"][:, None])
    m1 = mid_inmap(c, None, None, None, ins)
    for k in ("ofm", "xns", "xs", "ow", "onw"):
        m1.pop(k, None)
    m1["identf_m"] = m1.pop("identf")
    m.update(m1)
    m2 = l1attn_inmap(c, None, ins)
    m2.pop("xn1", None)
    m2["wq1"] = m2.pop("wq"); m2["wk1"] = m2.pop("wk"); m2["wv1"] = m2.pop("wv")
    m.update(m2)
    m["wop"] = np.ascontiguousarray(ins["L1_attn_Wo"][256 * c:256 * (c + 1), :])
    m3 = fin_inmap(c, None, None, None, ins)
    for k in ("ctx", "xn1s", "x1s", "wo"):
        m3.pop(k, None)
    m.update(m3)
    return m


def kernel_fused(upto=99, **inputs):
    nc = build_fused(upto=upto)
    maps = [fused_inmap(c, inputs) for c in range(NCORES)]
    res = run_bass_kernel_spmd(nc, maps, core_ids=list(range(NCORES)))
    if upto < 8:
        return [np.asarray(r["dbgo"]) for r in res.results]
    x2_fm = np.concatenate([np.asarray(r["x2_sl"]) for r in res.results], axis=1)
    return np.ascontiguousarray(x2_fm.T).reshape(4, 4096, D)



def kernel(**inputs):
    nc = build_fused(upto=8)
    maps = [fused_inmap(c, inputs) for c in range(NCORES)]
    res = run_bass_kernel_spmd(nc, maps, core_ids=list(range(NCORES)))
    x2_fm = np.concatenate([np.asarray(r["x2_sl"]) for r in res.results], axis=1)
    return np.ascontiguousarray(x2_fm.T).reshape(4, 4096, D)
